# version 01
# baseline (speedup 1.0000x reference)
import math
import numpy as np
from contextlib import ExitStack

import concourse.bass as bass
import concourse.mybir as mybir
from concourse.bass_utils import run_bass_kernel_spmd

F32 = mybir.dt.float32
BF16 = mybir.dt.bfloat16
I32 = mybir.dt.int32
AF = mybir.ActivationFunctionType
ALU = mybir.AluOpType

B, N, DM = 4, 2048, 2048
H, HKV, DH = 16, 8, 128
NE, HE, HS = 8, 1024, 2048
SCALE = 1.0 / math.sqrt(DH)
FLA_W = 0.01
T = 1024
KI = DM // 128
NP, PW = 2, 512
CH = 128
NCH = T // CH
EPS = 1e-5
NHE = HE // 128
NHS = HS // 128
NLEV = 7


class Sem:
    def __init__(self, es, nc, name):
        self.h = es.enter_context(nc.semaphore(name))
        self.val = 0

    def inc(self, n=1):
        self.val += n
        return self.val


class SelfSync:
    _WRAP = ("tensor_tensor", "tensor_scalar", "scalar_tensor_tensor", "tensor_copy",
             "tensor_add", "tensor_sub", "tensor_mul", "tensor_max", "tensor_relu",
             "reduce_max", "reduce_sum", "tensor_reduce", "reciprocal", "memset",
             "activation", "copy", "mul", "add", "square", "sqrt", "select",
             "tensor_single_scalar", "iota", "bn_stats", "bn_aggr", "memzero")

    def __init__(self, eng, sem):
        self._e = eng
        self._s = sem

    def __getattr__(self, name):
        f = getattr(self._e, name)
        if name not in SelfSync._WRAP:
            return f
        def call(*a, **k):
            self._e.wait_ge(self._s.h, self._s.val)
            ins = f(*a, **k)
            ins.then_inc(self._s.h, 1)
            self._s.inc()
            return ins
        return call


def stream_layout(W):
    d_in, d_out = W.shape
    v = W.reshape(d_in // 128, 128, d_out // 128, 128)
    return np.ascontiguousarray(v.transpose(2, 1, 0, 3), dtype=np.float32)


def host_prep(inputs):
    g = {}
    g["wq"] = stream_layout(np.asarray(inputs["L0_attn_Wq"], np.float32))
    g["wk"] = stream_layout(np.asarray(inputs["L0_attn_Wk"], np.float32))
    g["wv"] = stream_layout(np.asarray(inputs["L0_attn_Wv"], np.float32))
    g["wo"] = stream_layout(np.asarray(inputs["L0_attn_Wo"], np.float32))
    g["wg"] = np.ascontiguousarray(np.asarray(inputs["L0_attn_Wg"], np.float32).reshape(128, 1))
    g["bg16"] = np.full((16, 1), float(np.asarray(inputs["L0_attn_bg"]).reshape(-1)[0]), np.float32)
    gw = np.concatenate([np.asarray(inputs["L0_ffn_gate_w"], np.float32),
                         np.asarray(inputs["L0_ffn_se_gate_w"], np.float32)], axis=0).T
    gw = np.concatenate([gw, np.zeros((DM, 7), np.float32)], axis=1)
    g["gate"] = np.ascontiguousarray(gw.reshape(KI, 128, 16))
    wgate = np.asarray(inputs["L0_ffn_w_gate"], np.float32)
    wup = np.asarray(inputs["L0_ffn_w_up"], np.float32)
    wdown = np.asarray(inputs["L0_ffn_w_down"], np.float32)
    g["wgate"] = np.stack([stream_layout(wgate[e].T) for e in range(NE)])
    g["wup"] = np.stack([stream_layout(wup[e].T) for e in range(NE)])
    g["wdown"] = np.stack([stream_layout(wdown[e].T) for e in range(NE)])
    g["shgate"] = stream_layout(np.ascontiguousarray(np.asarray(inputs["L0_ffn_sh_gate"], np.float32).T))
    g["shup"] = stream_layout(np.ascontiguousarray(np.asarray(inputs["L0_ffn_sh_up"], np.float32).T))
    g["shdown"] = stream_layout(np.ascontiguousarray(np.asarray(inputs["L0_ffn_sh_down"], np.float32).T))
    g["c_ident"] = np.eye(128, dtype=np.float32)
    g["c_ones"] = np.ones((128, 1), np.float32)
    g["c_onesrow"] = np.ones((1, 128), np.float32)
    fm = np.zeros((4, 128, PW), np.float32)
    for o in range(4):
        for i in range(128):
            fm[o, i, o * 128 + i:] = 1.0
    g["c_fmask"] = fm
    sl = np.tril(np.ones((128, 128), np.float32), -1)
    g["c_msl"] = sl
    g["c_msu_neg"] = np.ascontiguousarray(-sl.T)
    g["c_mti_neg"] = -np.triu(np.ones((128, 128), np.float32), 0)
    sel = np.zeros((16, 16, 128), np.float32)
    for j in range(16):
        sel[j, j, :] = 1.0
    g["c_sel16"] = sel
    lc = np.zeros((4, 128, 4), np.float32)
    for j in range(4):
        lc[j, :, j] = 1.0
    g["c_lcol"] = lc
    cs = np.zeros((16, 128, 16), np.float32)
    for j in range(16):
        cs[j, :, j] = 1.0
    g["c_colsel"] = cs
    wgp = np.zeros((8, 128, 16), np.float32)
    for j in range(8):
        wgp[j, :, j] = np.asarray(inputs["L0_attn_Wg"], np.float32).reshape(-1)
    g["wgplace"] = wgp
    return g


def build_nc(stage):
    nc = bass.Bass("TRN2", target_bir_lowering=False, debug=False, num_devices=8)
    es = ExitStack()
    dr = {}

    def din(name, shape, dtype=F32):
        dr[name] = nc.dram_tensor(name, shape, dtype, kind="ExternalInput").ap()

    din("x_in", [KI, 128, T])
    din("rolei", [8], I32)
    dr["y"] = nc.dram_tensor("y", [KI, 128, T], F32, kind="ExternalOutput").ap()
    din("wq", [KI, 128, KI, 128]); din("wk", [HKV, 128, KI, 128]); din("wv", [HKV, 128, KI, 128])
    din("wo", [KI, 128, KI, 128]); din("wg", [128, 1]); din("bg16", [16, 1])
    din("gate", [KI, 128, 16])
    din("wgate", [NE, NHE, 128, KI, 128]); din("wup", [NE, NHE, 128, KI, 128])
    din("wdown", [NE, KI, 128, NHE, 128])
    din("shgate", [NHS, 128, KI, 128]); din("shup", [NHS, 128, KI, 128])
    din("shdown", [KI, 128, NHS, 128])
    din("c_ident", [128, 128]); din("c_ones", [128, 1]); din("c_onesrow", [1, 128])
    din("c_fmask", [4, 128, PW]); din("c_msl", [128, 128]); din("c_msu_neg", [128, 128])
    din("c_mti_neg", [128, 128]); din("c_sel16", [16, 16, 128]); din("c_lcol", [4, 128, 4])
    din("c_colsel", [16, 128, 16]); din("wgplace", [8, 128, 16])
    dr["ksh"] = nc.dram_tensor("ksh", [2, HKV, NCH, 128, 128], BF16, kind="Internal",
                               addr_space="Shared").ap()
    dr["vsh"] = nc.dram_tensor("vsh", [2, NCH, 128, HKV * DH], BF16, kind="Internal",
                               addr_space="Shared").ap()
    dr["rsh"] = nc.dram_tensor("rsh", [2, 16], F32, kind="Internal",
                               addr_space="Shared").ap()
    dr["ksl"] = nc.dram_tensor("ksl", [HKV, NCH, 128, 128], BF16, kind="Internal").ap()
    dr["vsl"] = nc.dram_tensor("vsl", [NCH, 128, HKV * DH], BF16, kind="Internal").ap()
    sb = {}

    def sbuf(name, shape, dtype, stack=es, side=None):
        t = stack.enter_context(nc.sbuf_tensor(name, shape, dtype, side=side))
        sb[name] = t
        return t

    ctx = dict(nc=nc, dr=dr, sb=sb, es=es, stage=stage, sbuf=sbuf)

    sbuf("x", [128, KI * T], F32)
    sbuf("wbufA", [128, KI * 128], BF16); sbuf("wbufB", [128, KI * 128], BF16)
    sbuf("wbufC", [128, KI * 128], BF16); sbuf("wbufD", [128, KI * 128], BF16)
    sbuf("tmpf", [128, T], F32); sbuf("tmpf2", [128, T], F32)
    sbuf("sc1", [128, T], BF16); sbuf("sc2", [128, T], BF16)
    sbuf("sc3", [128, T], BF16); sbuf("sc4", [128, T], BF16)
    sbuf("bcC", [128, T], BF16)
    sbuf("rows", [128, 4096], F32)
    sbuf("combf", [16, T], BF16)
    sbuf("scT", [16, T], BF16)
    sbuf("ctm", [128, 128], BF16)
    sbuf("gatew", [128, KI * 16], BF16)
    sbuf("ident", [128, 128], BF16); sbuf("negident", [128, 128], BF16)
    sbuf("ones", [128, 1], BF16); sbuf("onesf", [128, 1], F32)
    sbuf("onesrow", [1, 128], BF16); sbuf("onesrowf", [1, 128], F32)
    sbuf("fmask", [128, 4 * PW], BF16)
    sbuf("msl", [128, 128], BF16); sbuf("msu_neg", [128, 128], BF16); sbuf("mti_neg", [128, 128], BF16)
    sbuf("sel16", [16, 16 * 128], BF16); sbuf("lcol", [128, 4 * 4], BF16)
    sbuf("colsel", [128, 16 * 16], BF16); sbuf("wgp", [128, 8 * 16], BF16)
    sbuf("sel16f", [16, 16 * 128], F32)
    sbuf("rolet", [1, 8], I32)
    sbuf("wg_sb", [128, 1], BF16); sbuf("bg_sb", [16, 1], F32)

    s_w = Sem(es, nc, "s_w"); s_a = Sem(es, nc, "s_a")
    s_mm = Sem(es, nc, "s_mm"); s_v = Sem(es, nc, "s_v"); s_s = Sem(es, nc, "s_s")
    s_sl = [Sem(es, nc, f"s_sl{i}") for i in range(12)]
    s_bar = Sem(es, nc, "s_bar")
    ps = [es.enter_context(nc.psum_tensor(f"psb{i}", [128, 512], F32)) for i in range(8)]
    ctx.update(ps=ps, s_w=s_w, s_a=s_a, s_mm=s_mm, s_v=s_v, s_s=s_s, s_sl=s_sl, s_bar=s_bar)

    with nc.Block() as b:
        @b.gpsimd
        def _(g):
            g.dma_start(out=sb["x"][:].rearrange("p (k t) -> p k t", k=KI),
                        in_=dr["x_in"].rearrange("k p t -> p k t")).then_inc(s_w.h, 16)
            s_w.inc(16)
            for cn, dst in (("c_ident", "ident"), ("c_ones", "ones"), ("c_onesrow", "onesrow"),
                            ("c_msl", "msl"), ("c_msu_neg", "msu_neg"), ("c_mti_neg", "mti_neg"),
                            ("wg", "wg_sb")):
                g.dma_start(out=sb[dst][:], in_=dr[cn][:, :]).then_inc(s_w.h, 16)
                s_w.inc(16)
            g.dma_start(out=sb["fmask"][:].rearrange("p (o w) -> p o w", o=4),
                        in_=dr["c_fmask"].rearrange("o p w -> p o w")).then_inc(s_w.h, 16)
            s_w.inc(16)
            g.dma_start(out=sb["sel16"][:].rearrange("p (j m) -> p j m", j=16),
                        in_=dr["c_sel16"].rearrange("j p m -> p j m")).then_inc(s_w.h, 16)
            s_w.inc(16)
            g.dma_start(out=sb["lcol"][:].rearrange("p (j m) -> p j m", j=4),
                        in_=dr["c_lcol"].rearrange("j p m -> p j m")).then_inc(s_w.h, 16)
            s_w.inc(16)
            g.dma_start(out=sb["colsel"][:].rearrange("p (j m) -> p j m", j=16),
                        in_=dr["c_colsel"].rearrange("j p m -> p j m")).then_inc(s_w.h, 16)
            s_w.inc(16)
            g.dma_start(out=sb["wgp"][:].rearrange("p (j m) -> p j m", j=8),
                        in_=dr["wgplace"].rearrange("j p m -> p j m")).then_inc(s_w.h, 16)
            s_w.inc(16)
            g.dma_start(out=sb["gatew"][:].rearrange("p (k m) -> p k m", k=KI),
                        in_=dr["gate"].rearrange("k p m -> p k m")).then_inc(s_w.h, 16)
            s_w.inc(16)
        @b.sync
        def _(sy):
            sy.dma_start(out=sb["onesf"][:], in_=dr["c_ones"][:, :]).then_inc(s_a.h, 16)
            sy.dma_start(out=sb["onesrowf"][:], in_=dr["c_onesrow"][:, :]).then_inc(s_a.h, 16)
            sy.dma_start(out=sb["rolet"][0:1, :], in_=dr["rolei"][:]).then_inc(s_a.h, 16)
            sy.dma_start(out=sb["bg_sb"][:], in_=dr["bg16"][:, :]).then_inc(s_a.h, 16)
            sy.dma_start(out=sb["sel16f"][:].rearrange("p (j m) -> p j m", j=16),
                         in_=dr["c_sel16"].rearrange("j p m -> p j m")).then_inc(s_a.h, 16)
            s_a.inc(80)
    with nc.Block() as b:
        for eng in ("tensor", "vector", "scalar", "gpsimd", "sync"):
            def mk():
                def f(h):
                    h.wait_ge(s_w.h, s_w.val)
                    h.wait_ge(s_a.h, s_a.val)
                return f
            getattr(b, eng)(mk())
        @b.vector
        def _(v):
            V = SelfSync(v, s_v)
            V.tensor_scalar(out=sb["negident"][:], in0=sb["ident"][:], scalar1=-1.0,
                            scalar2=None, op0=ALU.mult)

    fstack_xn = ExitStack()
    fstack_h = ExitStack()
    sbuf("xn", [128, KI * T], BF16, stack=fstack_xn)
    sbuf("hbuf", [128, NHS * T], BF16, stack=fstack_h)

    emit_ln(ctx)
    emit_router(ctx)
    emit_ffn(ctx)

    if stage >= 2:
        emit_attention(ctx, fstack_xn, fstack_h)
        if stage >= 3:
            fstack2 = ExitStack()
            sb["xn"] = fstack2.enter_context(nc.sbuf_tensor("xn2", [128, KI * T], BF16))
            sb["hbuf"] = fstack2.enter_context(nc.sbuf_tensor("hbuf2", [128, NHS * T], BF16))
            emit_ln(ctx)
            if stage >= 3.7:
                emit_router(ctx)
            if stage >= 3.9:
                emit_ffn(ctx)
            fstack2.close()
    else:
        fstack_h.close()
        fstack_xn.close()

    with nc.Block() as b:
        @b.sync
        def _(sy):
            sy.wait_ge(s_v.h, s_v.val)
            sy.wait_ge(s_s.h, s_s.val)
            sy.wait_ge(s_mm.h, s_mm.val)
            sy.dma_start(out=dr["y"].rearrange("k p t -> p k t"),
                         in_=sb["x"][:].rearrange("p (k t) -> p k t", k=KI)).then_inc(s_a.h, 16)
            s_a.inc(16)
            sy.wait_ge(s_a.h, s_a.val)
    es.close()
    return nc


def emit_ln(ctx):
    nc = ctx["nc"]; sb = ctx["sb"]; ps = ctx["ps"]
    s_mm, s_v, s_s = ctx["s_mm"], ctx["s_v"], ctx["s_s"]
    x, xn = sb["x"], sb["xn"]
    rows, tmpf = sb["rows"], sb["tmpf"]
    cast = [sb["sc1"], sb["sc2"]]
    sq = [sb["sc3"], sb["sc4"]]
    vb0 = s_v.val
    m_entry = s_mm.val
    sval = {}
    mval = lambda ki: m_entry + ki + 1
    with nc.Block() as b:
        @b.scalar
        def _(s):
            S = SelfSync(s, s_s)
            s.wait_ge(s_v.h, vb0)
            s.wait_ge(s_mm.h, m_entry)
            for ki in range(KI):
                if ki >= 2:
                    s.wait_ge(s_mm.h, mval(ki - 2))
                S.activation(out=cast[ki % 2][:], in_=x[:, ki * T:(ki + 1) * T], func=AF.Copy)
                S.activation(out=sq[ki % 2][:], in_=x[:, ki * T:(ki + 1) * T], func=AF.Square)
                sval[ki] = s_s.val
        @b.tensor
        def _(t):
            for ki in range(KI):
                t.wait_ge(s_s.h, sval[ki])
                ins = None
                for p in range(NP):
                    t.matmul(ps[4 + p][0:1, :], sb["ones"][:, :],
                             cast[ki % 2][:, p * PW:(p + 1) * PW],
                             start=(ki == 0), stop=(ki == KI - 1))
                    ins = t.matmul(ps[6 + p][0:1, :], sb["ones"][:, :],
                                   sq[ki % 2][:, p * PW:(p + 1) * PW],
                                   start=(ki == 0), stop=(ki == KI - 1))
                ins.then_inc(s_mm.h, 1)
                s_mm.inc()
                assert s_mm.val == mval(ki)
    mstat = s_mm.val
    vrec = {}
    srec = {}
    with nc.Block() as b:
        @b.vector
        def _(v):
            V = SelfSync(v, s_v)
            v.wait_ge(s_mm.h, mstat)
            for p in range(NP):
                V.tensor_scalar(out=rows[0:1, p * PW:(p + 1) * PW], in0=ps[4 + p][0:1, :],
                                scalar1=1.0 / DM, scalar2=None, op0=ALU.mult)
                V.tensor_scalar(out=rows[0:1, 2 * T + p * PW:2 * T + (p + 1) * PW],
                                in0=ps[6 + p][0:1, :], scalar1=1.0 / DM, scalar2=None,
                                op0=ALU.mult)
            V.tensor_tensor(out=rows[0:1, 3 * T:4 * T], in0=rows[0:1, 0:T], in1=rows[0:1, 0:T],
                            op=ALU.mult)
            V.tensor_tensor(out=rows[0:1, 2 * T:3 * T], in0=rows[0:1, 2 * T:3 * T],
                            in1=rows[0:1, 3 * T:4 * T], op=ALU.subtract)
            V.tensor_scalar(out=rows[0:1, 2 * T:3 * T], in0=rows[0:1, 2 * T:3 * T],
                            scalar1=EPS, scalar2=None, op0=ALU.add)
            vrec[0] = s_v.val
        @b.scalar
        def _(s):
            S = SelfSync(s, s_s)
            s.wait_ge(s_v.h, vrec[0])
            S.activation(out=rows[0:1, 3 * T:4 * T], in_=rows[0:1, 2 * T:3 * T], func=AF.Sqrt)
            srec[0] = s_s.val
        @b.vector
        def _(v):
            V = SelfSync(v, s_v)
            v.wait_ge(s_s.h, srec[0])
            V.reciprocal(out=rows[0:1, T:2 * T], in_=rows[0:1, 3 * T:4 * T])
            vrec[1] = s_v.val
    with nc.Block() as b:
        mrec = {}
        @b.tensor
        def _(t):
            t.wait_ge(s_v.h, vrec[1])
            ins = None
            for p in range(NP):
                t.matmul(ps[4 + p][:, :], sb["onesrowf"][:, :], rows[0:1, p * PW:(p + 1) * PW],
                         start=True, stop=True)
                ins = t.matmul(ps[6 + p][:, :], sb["onesrowf"][:, :],
                               rows[0:1, T + p * PW:T + (p + 1) * PW],
                               start=True, stop=True)
            ins.then_inc(s_mm.h, 1)
            s_mm.inc()
            mrec[0] = s_mm.val
        @b.vector
        def _(v):
            V = SelfSync(v, s_v)
            v.wait_ge(s_mm.h, mrec[0])
            v.wait_ge(s_mm.h, m_entry)
            for ki in range(KI):
                for p in range(NP):
                    V.tensor_tensor(out=tmpf[:, p * PW:(p + 1) * PW],
                                    in0=x[:, ki * T + p * PW: ki * T + (p + 1) * PW],
                                    in1=ps[4 + p][:, :], op=ALU.subtract)
                    V.tensor_tensor(out=xn[:, ki * T + p * PW: ki * T + (p + 1) * PW],
                                    in0=tmpf[:, p * PW:(p + 1) * PW], in1=ps[6 + p][:, :],
                                    op=ALU.mult)
    ctx["xn_ready_v"] = s_v.val


def emit_router(ctx):
    nc = ctx["nc"]; sb = ctx["sb"]; ps = ctx["ps"]
    s_mm, s_v, s_s = ctx["s_mm"], ctx["s_v"], ctx["s_s"]
    xn = sb["xn"]
    scT, ctm, combf = sb["scT"], sb["ctm"], sb["combf"]
    r = {}
    with nc.Block() as b:
        @b.tensor
        def _(t):
            t.wait_ge(s_v.h, ctx["xn_ready_v"])
            ins = None
            for p in range(NP):
                for ki in range(KI):
                    ins = t.matmul(ps[p][0:16, :], sb["gatew"][:, ki * 16:(ki + 1) * 16],
                                   xn[:, ki * T + p * PW: ki * T + (p + 1) * PW],
                                   start=(ki == 0), stop=(ki == KI - 1))
            ins.then_inc(s_mm.h, 1)
            s_mm.inc()
            r["scmm"] = s_mm.val
        @b.scalar
        def _(s):
            S = SelfSync(s, s_s)
            s.wait_ge(s_mm.h, r["scmm"])
            for p in range(NP):
                S.copy(out=scT[0:16, p * PW:(p + 1) * PW], in_=ps[p][0:16, :])
            r["sct"] = s_s.val
        @b.vector
        def _(v):
            V = SelfSync(v, s_v)
            V.memset(ctm[:], 0.0)
            r["ctmz"] = s_v.val
        @b.tensor
        def _(t):
            t.wait_ge(s_s.h, r["sct"])
            ins = None
            for tt in range(8):
                ins = t.matmul(ps[2][:, tt * 16:(tt + 1) * 16],
                               scT[0:16, tt * 128:(tt + 1) * 128],
                               sb["ident"][0:16, 0:16], start=True, stop=True)
            ins.then_inc(s_mm.h, 1)
            s_mm.inc()
            r["tmm"] = s_mm.val
    st = sb["tmpf"]
    sc = sb["tmpf2"]
    with nc.Block() as b:
        @b.vector
        def _(v):
            V = SelfSync(v, s_v)
            v.wait_ge(s_mm.h, r["tmm"])
            V.tensor_copy(st[:, 0:128], ps[2][:, 0:128])
            for tt in range(8):
                V.reduce_max(st[:, 128 + tt:129 + tt], st[:, tt * 16:tt * 16 + 8],
                             axis=mybir.AxisListType.X)
            V.tensor_scalar(out=st[:, 136:144], in0=st[:, 128:136], scalar1=-1.0, scalar2=None,
                            op0=ALU.mult)
            for tt in range(8):
                V.tensor_scalar(out=sc[:, tt * 16:tt * 16 + 8], in0=st[:, tt * 16:tt * 16 + 8],
                                scalar1=st[:, 128 + tt:129 + tt], scalar2=None, op0=ALU.is_equal)
            for tt in range(8):
                V.scalar_tensor_tensor(out=sc[:, tt * 16:tt * 16 + 8],
                                       in0=sc[:, tt * 16:tt * 16 + 8], scalar=-1e30,
                                       in1=st[:, tt * 16:tt * 16 + 8], op0=ALU.mult, op1=ALU.add)
            for tt in range(8):
                V.reduce_max(st[:, 144 + tt:145 + tt], sc[:, tt * 16:tt * 16 + 8],
                             axis=mybir.AxisListType.X)
            r["v1"] = s_v.val
        @b.scalar
        def _(s):
            S = SelfSync(s, s_s)
            s.wait_ge(s_v.h, r["v1"])
            for tt in range(8):
                S.activation(out=sc[:, 128 + tt * 16:128 + tt * 16 + 8],
                             in_=st[:, tt * 16:tt * 16 + 8], func=AF.Exp,
                             bias=st[:, 136 + tt:137 + tt], scale=1.0)
            for tt in range(8):
                S.activation(out=st[:, 152 + tt:153 + tt], in_=st[:, 144 + tt:145 + tt],
                             func=AF.Exp, bias=st[:, 136 + tt:137 + tt], scale=1.0)
            s.wait_ge(s_v.h, r["ctmz"])
            for tt in range(8):
                S.activation(out=ctm[:, tt * 16 + 8:tt * 16 + 9],
                             in_=st[:, tt * 16 + 8:tt * 16 + 9], func=AF.Sigmoid)
            r["s1"] = s_s.val
        @b.vector
        def _(v):
            V = SelfSync(v, s_v)
            v.wait_ge(s_s.h, r["s1"])
            V.tensor_scalar(out=st[:, 160:168], in0=st[:, 152:160], scalar1=1.0, scalar2=None,
                            op0=ALU.add)
            V.reciprocal(out=st[:, 168:176], in_=st[:, 160:168])
            for tt in range(8):
                V.tensor_scalar(out=sc[:, 256 + tt * 16:256 + tt * 16 + 8],
                                in0=st[:, tt * 16:tt * 16 + 8],
                                scalar1=st[:, 144 + tt:145 + tt], scalar2=None, op0=ALU.is_ge)
            for tt in range(8):
                V.scalar_tensor_tensor(out=ctm[:, tt * 16:tt * 16 + 8],
                                       in0=sc[:, 128 + tt * 16:128 + tt * 16 + 8],
                                       scalar=st[:, 168 + tt:169 + tt],
                                       in1=sc[:, 256 + tt * 16:256 + tt * 16 + 8],
                                       op0=ALU.mult, op1=ALU.mult)
            r["v2"] = s_v.val
    with nc.Block() as b:
        @b.tensor
        def _(t):
            t.wait_ge(s_v.h, r["v2"])
            ins = None
            for tt in range(8):
                bank = ps[3] if tt < 4 else ps[2]
                ins = t.matmul(bank[0:16, (tt % 4) * 128:(tt % 4 + 1) * 128],
                               ctm[:, tt * 16:(tt + 1) * 16], sb["ident"][:, :],
                               start=True, stop=True)
            ins.then_inc(s_mm.h, 1)
            s_mm.inc()
            r["cmm"] = s_mm.val
        @b.scalar
        def _(s):
            S = SelfSync(s, s_s)
            s.wait_ge(s_mm.h, r["cmm"])
            S.copy(out=combf[0:16, 0:512], in_=ps[3][0:16, :])
            S.copy(out=combf[0:16, 512:1024], in_=ps[2][0:16, :])
    ctx["combf_ready_s"] = s_s.val


def emit_ffn(ctx):
    nc = ctx["nc"]; sb = ctx["sb"]; ps = ctx["ps"]; dr = ctx["dr"]
    s_w, s_mm, s_v, s_s = ctx["s_w"], ctx["s_mm"], ctx["s_v"], ctx["s_s"]
    xn, hbuf, bcC, x = sb["xn"], sb["hbuf"], sb["bcC"], sb["x"]
    tmpf, tmpf2, combf = sb["tmpf"], sb["tmpf2"], sb["combf"]
    gslot = [sb["wbufA"], sb["wbufB"]]
    uslot = [sb["wbufC"], sb["wbufD"]]

    down_done_mm = s_mm.val
    for e in range(NE + 1):
        shared = (e == NE)
        NH = NHS if shared else NHE
        wgd = dr["shgate"] if shared else dr["wgate"][e]
        wud = dr["shup"] if shared else dr["wup"][e]
        wdd = dr["shdown"] if shared else dr["wdown"][e]
        r = {"ventry": s_v.val}
        with nc.Block() as b:
            @b.tensor
            def _(t):
                t.wait_ge(s_s.h, ctx["combf_ready_s"])
                if e > 0:
                    t.wait_ge(s_v.h, ctx["dnv_last"])
                ins = None
                for p in range(NP):
                    ins = t.matmul(ps[p][:, :],
                                   sb["sel16"][:, min(e, 8) * 128:(min(e, 8) + 1) * 128],
                                   combf[0:16, p * PW:(p + 1) * PW], start=True, stop=True)
                ins.then_inc(s_mm.h, 1)
                s_mm.inc()
                r["bc"] = s_mm.val
            @b.scalar
            def _(s):
                S = SelfSync(s, s_s)
                s.wait_ge(s_mm.h, r["bc"])
                s.wait_ge(s_v.h, r["ventry"])
                for p in range(NP):
                    S.copy(out=bcC[:, p * PW:(p + 1) * PW], in_=ps[p][:, :])
                r["bcc"] = s_s.val

        s_sl = ctx["s_sl"]
        m0 = s_mm.val
        v0 = s_v.val
        s0 = s_s.val
        gmm = lambda si, p: m0 + si * 4 + p + 1
        umm = lambda si, p: m0 + si * 4 + 2 + p + 1
        sil = lambda si, p: s0 + si * 2 + p + 1
        hev = lambda si, p: v0 + (si * 2 + p + 1) * 2
        gtgt = {}
        utgt = {}
        for si in range(NH):
            gtgt[si] = s_sl[si % 2].inc(16)
            utgt[si] = s_sl[2 + si % 2].inc(16)
        with nc.Block() as b:
            @b.tensor
            def _(t):
                for si in range(NH):
                    t.wait_ge(s_sl[si % 2].h, gtgt[si])
                    t.wait_ge(s_sl[2 + si % 2].h, utgt[si])
                    if si >= 2:
                        t.wait_ge(s_v.h, hev(si - 2, NP - 1))
                    for p in range(NP):
                        ins = None
                        for ki in range(KI):
                            ins = t.matmul(ps[(si % 2) * 2 + p][:, :],
                                           gslot[si % 2][:, ki * 128:(ki + 1) * 128],
                                           xn[:, ki * T + p * PW: ki * T + (p + 1) * PW],
                                           start=(ki == 0), stop=(ki == KI - 1))
                        ins.then_inc(s_mm.h, 1)
                        s_mm.inc()
                        assert s_mm.val == gmm(si, p)
                    for p in range(NP):
                        ins = None
                        for ki in range(KI):
                            ins = t.matmul(ps[4 + (si % 2) * 2 + p][:, :],
                                           uslot[si % 2][:, ki * 128:(ki + 1) * 128],
                                           xn[:, ki * T + p * PW: ki * T + (p + 1) * PW],
                                           start=(ki == 0), stop=(ki == KI - 1))
                        ins.then_inc(s_mm.h, 1)
                        s_mm.inc()
                        assert s_mm.val == umm(si, p)
            @b.gpsimd
            def _(g):
                for si in range(NH):
                    if si < 2:
                        g.wait_ge(s_mm.h, down_done_mm)
                    else:
                        g.wait_ge(s_mm.h, umm(si - 2, NP - 1))
                    g.dma_start(out=gslot[si % 2][:],
                                in_=wgd[si].rearrange("p k m -> p (k m)"))\
                        .then_inc(s_sl[si % 2].h, 16)
                    g.dma_start(out=uslot[si % 2][:],
                                in_=wud[si].rearrange("p k m -> p (k m)"))\
                        .then_inc(s_sl[2 + si % 2].h, 16)
            @b.scalar
            def _(s):
                S = SelfSync(s, s_s)
                for si in range(NH):
                    for p in range(NP):
                        s.wait_ge(s_mm.h, gmm(si, p))
                        if si >= 1:
                            s.wait_ge(s_v.h, hev(si - 1, p))
                        S.activation(out=tmpf[:, p * PW:(p + 1) * PW],
                                     in_=ps[(si % 2) * 2 + p][:, :], func=AF.Silu)
                        assert s_s.val == sil(si, p)
            @b.vector
            def _(v):
                V = SelfSync(v, s_v)
                for si in range(NH):
                    for p in range(NP):
                        v.wait_ge(s_s.h, sil(si, p))
                        v.wait_ge(s_mm.h, umm(si, p))
                        if e > 0 and si == 0 and p == 0:
                            v.wait_ge(s_mm.h, down_done_mm)
                        V.tensor_tensor(out=tmpf2[:, p * PW:(p + 1) * PW],
                                        in0=tmpf[:, p * PW:(p + 1) * PW],
                                        in1=ps[4 + (si % 2) * 2 + p][:, :], op=ALU.mult)
                        V.tensor_tensor(out=hbuf[:, si * T + p * PW: si * T + (p + 1) * PW],
                                        in0=tmpf2[:, p * PW:(p + 1) * PW],
                                        in1=bcC[:, p * PW:(p + 1) * PW], op=ALU.mult)
                        assert s_v.val == hev(si, p)
        hev_last = s_v.val
        gu_done = s_mm.val

        m1 = s_mm.val
        v1 = s_v.val
        dmm = lambda c, p: m1 + c * 2 + p + 1
        dvv = lambda c, p: v1 + c * 2 + p + 1
        dtgt = {}
        for c in range(KI):
            dtgt[c] = s_sl[c % 2].inc(16)
        with nc.Block() as b:
            @b.tensor
            def _(t):
                for c in range(KI):
                    t.wait_ge(s_sl[c % 2].h, dtgt[c])
                    if c == 0:
                        t.wait_ge(s_v.h, hev_last)
                    if c >= 2:
                        t.wait_ge(s_v.h, dvv(c - 2, NP - 1))
                    for p in range(NP):
                        ins = None
                        for jj in range(NH):
                            ins = t.matmul(ps[(c % 2) * 2 + p][:, :],
                                           gslot[c % 2][:, jj * 128:(jj + 1) * 128],
                                           hbuf[:, jj * T + p * PW: jj * T + (p + 1) * PW],
                                           start=(jj == 0), stop=(jj == NH - 1))
                        ins.then_inc(s_mm.h, 1)
                        s_mm.inc()
                        assert s_mm.val == dmm(c, p)
            @b.gpsimd
            def _(g):
                for c in range(KI):
                    if c < 2:
                        g.wait_ge(s_mm.h, gu_done)
                    else:
                        g.wait_ge(s_mm.h, dmm(c - 2, NP - 1))
                    g.dma_start(out=gslot[c % 2][:, 0:NH * 128],
                                in_=wdd[c].rearrange("p k m -> p (k m)"))\
                        .then_inc(s_sl[c % 2].h, 16)
            @b.vector
            def _(v):
                V = SelfSync(v, s_v)
                for c in range(KI):
                    for p in range(NP):
                        v.wait_ge(s_mm.h, dmm(c, p))
                        V.scalar_tensor_tensor(
                            out=x[:, c * T + p * PW: c * T + (p + 1) * PW],
                            in0=ps[(c % 2) * 2 + p][:, :], scalar=0.5,
                            in1=x[:, c * T + p * PW: c * T + (p + 1) * PW],
                            op0=ALU.mult, op1=ALU.add)
                        assert s_v.val == dvv(c, p)
        down_done_mm = s_mm.val
        ctx["dnv_last"] = s_v.val






def emit_attention(ctx, fstack_xn, fstack_h):
    nc = ctx["nc"]; sb = ctx["sb"]
    sbuf = ctx["sbuf"]
    s_sl = ctx["s_sl"]

    emit_ln(ctx)

    fstack_h.close()
    astack = ExitStack()
    q = sbuf("qb", [128, H * T], BF16, stack=astack, side="right")

    emit_proj(ctx)

    fstack_xn.close()
    attnv = sbuf("attnv", [128, H * PW], BF16, stack=astack, side="right")
    kst = sbuf("kst", [128, 8 * 128], BF16, stack=astack, side="right")
    vst = sbuf("vst", [128, 8 * 128], BF16, stack=astack, side="right")
    ptb = sbuf("ptb", [128, 4 * PW], BF16, stack=astack, side="right")
    negC = sbuf("negC", [128, 16], F32, stack=astack, side="right")
    dS = sbuf("dS", [128, HKV * DH], BF16, stack=astack, side="right")
    e1a = sbuf("e1a", [128, HKV * 128], BF16, stack=astack, side="right")
    e2a = sbuf("e2a", [128, HKV * 128], BF16, stack=astack, side="right")
    dsc = sbuf("dsc", [128, 12 * 128], BF16, stack=astack, side="right")
    rbc = sbuf("rbc", [128, PW], BF16, stack=astack, side="right")

    emit_barrier_and_bounds(ctx, negC)

    s_mm, s_v, s_s, s_a, s_w = ctx["s_mm"], ctx["s_v"], ctx["s_s"], ctx["s_a"], ctx["s_w"]
    sems_all = [s_mm, s_v, s_s, s_a, s_w] + s_sl
    engs = [nc.tensor, nc.vector, nc.scalar, nc.gpsimd, nc.sync]
    enames = ("tensor", "vector", "scalar", "gpsimd", "sync")
    regs = {}
    with nc.Block() as b:
        for name in enames:
            def mk(nm):
                def f(h):
                    r = h.alloc_register(f"role_{nm}")
                    h.reg_load(r, sb["rolet"][0:1, 0:1])
                    regs[nm] = r
                return f
            getattr(b, name)(mk(name))
    idxs = [eng.compute_val(regs[nm]) for nm, eng in zip(enames, engs)]

    snap = [sm.val for sm in sems_all]
    snap_keys = set(ctx.keys())
    ends = []
    for role in nc.Switch(engines=engs, index=idxs, n=2):
        if role == 1:
            for sm, vv in zip(sems_all, snap):
                sm.val = vv
            for k in list(ctx.keys()):
                if k not in snap_keys:
                    del ctx[k]
        emit_flash_delta_wo(ctx, role, q, attnv, kst, vst, ptb, negC, dS, e1a, e2a, dsc, rbc)
        ends.append([sm.val for sm in sems_all])
    ceil = [max(a, c) for a, c in zip(ends[0], ends[1])]
    own = [nc.tensor, nc.vector, nc.scalar, nc.sync, nc.gpsimd,
           nc.gpsimd, nc.gpsimd, nc.gpsimd, nc.gpsimd,
           nc.sync, nc.sync, nc.sync, nc.sync, nc.sync, nc.sync, nc.sync, nc.sync]
    for role in nc.Switch(engines=engs, index=idxs, n=2):
        for i, sm in enumerate(sems_all):
            d = ceil[i] - ends[role][i]
            if d > 0:
                own[i].wait_ge(sm.h, ends[role][i])
                own[i].sem_inc(sm.h, d)
    for sm, vv in zip(sems_all, ceil):
        sm.val = vv
    astack.close()
    ctx["dnv_last"] = s_v.val


_HALF_N = [0]


def _half_ix(nc, sb, tag):
    rr = {}
    col = _HALF_N[0] % 8
    _HALF_N[0] += 1
    with nc.Block() as b:
        @b.sync
        def _(sy):
            r = sy.alloc_register(f"half_{tag}{_HALF_N[0]}")
            sy.reg_load(r, sb["rolet"][0:1, col:col + 1])
            rr[0] = r
    return nc.sync.compute_val(rr[0])


def _sync_switch(ctx, col, emit_body):
    nc = ctx["nc"]; sb = ctx["sb"]
    rr = {}
    with nc.Block() as b:
        @b.sync
        def _(sy):
            r = sy.alloc_register(f"swr{col}_{_HALF_N[0]}")
            _HALF_N[0] += 1
            sy.reg_load(r, sb["rolet"][0:1, col:col + 1])
            rr[0] = r
    idx = nc.sync.compute_val(rr[0])
    sems = [ctx["s_a"]] + ctx["s_sl"]
    snap = [sm.val for sm in sems]
    ends = []
    for half in nc.Switch(engines=[nc.sync], index=[idx], n=2):
        if half == 1:
            for sm, vv in zip(sems, snap):
                sm.val = vv
        emit_body(half, nc.sync)
        ends.append([sm.val for sm in sems])
    assert ends[0] == ends[1], (ends[0], ends[1])


def emit_proj(ctx):
    nc = ctx["nc"]; sb = ctx["sb"]; ps = ctx["ps"]; dr = ctx["dr"]
    s_a, s_mm, s_v, s_s = ctx["s_a"], ctx["s_mm"], ctx["s_v"], ctx["s_s"]
    s_sl = ctx["s_sl"]
    xn, q, rows = sb["xn"], sb["qb"], sb["rows"]
    wslot = [sb["wbufA"], sb["wbufB"]]
    kstage = [sb["sc1"], sb["sc2"]]
    vstage = sb["bcC"]
    sqs = sb["sc3"]
    sq4 = sb["sc4"]

    with nc.Block() as b:
        @b.vector
        def _(v):
            V = SelfSync(v, s_v)
            V.memset(rows[0:1, T:T + 40], 0.0)

    dt = {}
    for si in range(KI):
        dt[si] = s_sl[si % 2].inc(16)
    m0 = s_mm.val
    pm = lambda si, p: m0 + si * 2 + p + 1
    s0 = s_s.val
    ev = lambda si, p: s0 + si * 2 + p + 1
    with nc.Block() as b:
        @b.gpsimd
        def _(g):
            for si in range(KI):
                if si >= 2:
                    g.wait_ge(s_mm.h, pm(si - 2, NP - 1))
                g.dma_start(out=wslot[si % 2][:],
                            in_=dr["wq"][si].rearrange("p k m -> p (k m)"))\
                    .then_inc(s_sl[si % 2].h, 16)
        @b.tensor
        def _(t):
            t.wait_ge(s_v.h, ctx["xn_ready_v"])
            for si in range(KI):
                t.wait_ge(s_sl[si % 2].h, dt[si])
                if si >= 2:
                    t.wait_ge(s_s.h, ev(si - 2, NP - 1))
                for p in range(NP):
                    ins = None
                    for ki in range(KI):
                        ins = t.matmul(ps[(si % 2) * 2 + p][:, :],
                                       wslot[si % 2][:, ki * 128:(ki + 1) * 128],
                                       xn[:, ki * T + p * PW: ki * T + (p + 1) * PW],
                                       start=(ki == 0), stop=(ki == KI - 1))
                    ins.then_inc(s_mm.h, 1)
                    s_mm.inc()
        @b.scalar
        def _(s):
            S = SelfSync(s, s_s)
            for si in range(KI):
                for p in range(NP):
                    s.wait_ge(s_mm.h, pm(si, p))
                    S.copy(out=q[:, si * T + p * PW: si * T + (p + 1) * PW],
                           in_=ps[(si % 2) * 2 + p][:, :])
                    assert s_s.val == ev(si, p)

    m1 = s_mm.val
    s1 = s_s.val
    v1 = s_v.val
    qm = lambda si, p: m1 + si * 2 + p + 1
    qe = lambda si, p: s1 + si * 4 + p * 2 + 2
    qv = lambda si, p: v1 + si * 4 + p * 2 + 2
    with nc.Block() as b:
        @b.scalar
        def _(s):
            S = SelfSync(s, s_s)
            for si in range(KI):
                for p in range(NP):
                    if (si, p) != (0, 0):
                        s.wait_ge(s_mm.h, qm(si, p - 1) if p > 0 else qm(si - 1, NP - 1))
                    S.activation(out=sqs[:, p * PW:(p + 1) * PW],
                                 in_=q[:, si * T + p * PW: si * T + (p + 1) * PW], func=AF.Exp)
                    S.activation(out=sq4[:, p * PW:(p + 1) * PW],
                                 in_=q[:, si * T + p * PW: si * T + (p + 1) * PW],
                                 func=AF.Square)
                    assert s_s.val == qe(si, p)
        @b.tensor
        def _(t):
            for si in range(KI):
                for p in range(NP):
                    t.wait_ge(s_s.h, qe(si, p))
                    if (si, p) != (0, 0):
                        t.wait_ge(s_v.h, qv(si, p - 1) if p > 0 else qv(si - 1, NP - 1))
                    t.matmul((ps[6] if p == 0 else ps[5])[0:16, 0:PW],
                             sb["colsel"][:, si * 16:(si + 1) * 16],
                             sqs[:, p * PW:(p + 1) * PW], start=True, stop=True)
                    ins = t.matmul(ps[7][0:1, 0:PW], sb["ones"][:, :],
                                   sq4[:, p * PW:(p + 1) * PW], start=True, stop=True)
                    ins.then_inc(s_mm.h, 1)
                    s_mm.inc()
                    assert s_mm.val == qm(si, p)
        @b.vector
        def _(v):
            V = SelfSync(v, s_v)
            for si in range(KI):
                for p in range(NP):
                    v.wait_ge(s_mm.h, qm(si, p))
                    V.reduce_max(rows[0:1, 2 * T:2 * T + 1], ps[7][0:1, 0:PW],
                                 axis=mybir.AxisListType.X)
                    V.tensor_tensor(out=rows[0:1, T + si:T + si + 1],
                                    in0=rows[0:1, T + si:T + si + 1],
                                    in1=rows[0:1, 2 * T:2 * T + 1], op=ALU.max)
                    assert s_v.val == qv(si, p)
            for p in range(NP):
                V.tensor_copy(rows[0:16, p * PW:(p + 1) * PW],
                              (ps[6] if p == 0 else ps[5])[0:16, 0:PW])
    ctx["qcs_done_v"] = s_v.val

    dt = {}
    for si in range(HKV):
        dt[si] = s_sl[si % 2].inc(16)
    m2 = s_mm.val
    s2 = s_s.val
    v2 = s_v.val
    km = lambda si, p: m2 + si * 4 + p + 1
    knm = lambda si, p: m2 + si * 4 + 2 + p + 1
    kev = lambda si, p: s2 + si * 3 + p + 1
    ksq = lambda si: s2 + si * 3 + 3
    knv = lambda si, p: v2 + si * 4 + (p + 1) * 2
    w4base, w5base = s_sl[4].val, s_sl[5].val
    wrt = lambda si, p: (w4base if si % 2 == 0 else w5base) + 32 * (si // 2 + 1)
    with nc.Block() as b:
        @b.gpsimd
        def _(g):
            for si in range(HKV):
                if si >= 2:
                    g.wait_ge(s_mm.h, km(si - 2, NP - 1))
                g.dma_start(out=wslot[si % 2][:],
                            in_=dr["wk"][si].rearrange("p k m -> p (k m)"))\
                    .then_inc(s_sl[si % 2].h, 16)
        @b.tensor
        def _(t):
            for si in range(HKV):
                t.wait_ge(s_sl[si % 2].h, dt[si])
                if si >= 2:
                    t.wait_ge(s_s.h, kev(si - 2, NP - 1))
                for p in range(NP):
                    ins = None
                    for ki in range(KI):
                        ins = t.matmul(ps[(si % 2) * 2 + p][:, :],
                                       wslot[si % 2][:, ki * 128:(ki + 1) * 128],
                                       xn[:, ki * T + p * PW: ki * T + (p + 1) * PW],
                                       start=(ki == 0), stop=(ki == KI - 1))
                    ins.then_inc(s_mm.h, 1)
                    s_mm.inc()
                    assert s_mm.val == km(si, p)
                t.wait_ge(s_s.h, ksq(si))
                for p in range(NP):
                    if si > 0 or p > 0:
                        t.wait_ge(s_v.h, knv(si, p - 1) if p > 0 else knv(si - 1, NP - 1))
                    t.matmul(ps[7][0:1, 0:PW], sb["ones"][:, :],
                             sq4[:, p * PW:(p + 1) * PW],
                             start=True, stop=True).then_inc(s_mm.h, 1)
                    s_mm.inc()
                    assert s_mm.val == knm(si, p)
        @b.scalar
        def _(s):
            S = SelfSync(s, s_s)
            for si in range(HKV):
                for p in range(NP):
                    s.wait_ge(s_mm.h, km(si, p))
                    if si >= 2:
                        s.wait_ge(s_sl[4 + si % 2].h, wrt(si - 2, p))
                    S.copy(out=kstage[si % 2][:, p * PW:(p + 1) * PW],
                           in_=ps[(si % 2) * 2 + p][:, :])
                    assert s_s.val == kev(si, p)
                S.activation(out=sq4[:, 0:T], in_=kstage[si % 2][:], func=AF.Square)
                assert s_s.val == ksq(si)
        @b.vector
        def _(v):
            V = SelfSync(v, s_v)
            for si in range(HKV):
                for p in range(NP):
                    v.wait_ge(s_mm.h, knm(si, p))
                    V.reduce_max(rows[0:1, 2 * T:2 * T + 1], ps[7][0:1, 0:PW],
                                 axis=mybir.AxisListType.X)
                    V.tensor_tensor(out=rows[0:1, T + 16 + si:T + 17 + si],
                                    in0=rows[0:1, T + 16 + si:T + 17 + si],
                                    in1=rows[0:1, 2 * T:2 * T + 1], op=ALU.max)
                    assert s_v.val == knv(si, p)
        @b.sync
        def _(sy):
            for si in range(HKV):
                for p in range(NP):
                    sy.wait_ge(s_s.h, kev(si, p))
                    sy.dma_start(
                        out=dr["ksl"][si, p * 4:(p + 1) * 4, :, :]
                            .rearrange("c d m -> d c m"),
                        in_=kstage[si % 2][:, p * PW:(p + 1) * PW]
                            .rearrange("d (c m) -> d c m", c=4))\
                        .then_inc(s_sl[4 + si % 2].h, 16)
                    s_sl[4 + si % 2].inc(16)
    kmax_done_v = s_v.val

    dt = {}
    for si in range(HKV):
        dt[si] = s_sl[si % 2].inc(16)
    m3 = s_mm.val
    s3 = s_s.val
    vm = lambda si, p: m3 + si * 4 + p * 2 + 1
    vtm = lambda si, p: m3 + si * 4 + p * 2 + 2
    vev = lambda si, p: s3 + si * 4 + p * 2 + 1
    vte = lambda si, p: s3 + si * 4 + p * 2 + 2
    wrt = {}
    for si in range(HKV):
        for p in range(NP):
            wrt[(si, p)] = s_sl[6].inc(64)
    with nc.Block() as b:
        @b.gpsimd
        def _(g):
            for si in range(HKV):
                if si >= 2:
                    g.wait_ge(s_mm.h, vm(si - 2, NP - 1))
                g.dma_start(out=wslot[si % 2][:],
                            in_=dr["wv"][si].rearrange("p k m -> p (k m)"))\
                    .then_inc(s_sl[si % 2].h, 16)
        @b.tensor
        def _(t):
            for si in range(HKV):
                t.wait_ge(s_sl[si % 2].h, dt[si])
                for p in range(NP):
                    if si >= 1:
                        t.wait_ge(s_s.h, vev(si - 1, p))
                    ins = None
                    for ki in range(KI):
                        ins = t.matmul(ps[4 + p][:, :],
                                       wslot[si % 2][:, ki * 128:(ki + 1) * 128],
                                       xn[:, ki * T + p * PW: ki * T + (p + 1) * PW],
                                       start=(ki == 0), stop=(ki == KI - 1))
                    ins.then_inc(s_mm.h, 1)
                    s_mm.inc()
                    assert s_mm.val == vm(si, p)
                    t.wait_ge(s_s.h, vev(si, p))
                    ins = None
                    for c in range(4):
                        ins = t.matmul(ps[6][:, c * 128:(c + 1) * 128],
                                       vstage[:, p * PW + c * 128: p * PW + (c + 1) * 128],
                                       sb["ident"][:, :], start=True, stop=True)
                    ins.then_inc(s_mm.h, 1)
                    s_mm.inc()
                    assert s_mm.val == vtm(si, p)
        @b.scalar
        def _(s):
            S = SelfSync(s, s_s)
            for si in range(HKV):
                for p in range(NP):
                    s.wait_ge(s_mm.h, vm(si, p))
                    S.copy(out=vstage[:, p * PW:(p + 1) * PW], in_=ps[4 + p][:, :])
                    assert s_s.val == vev(si, p)
                    s.wait_ge(s_mm.h, vtm(si, p))
                    if (si, p) != (0, 0):
                        s.wait_ge(s_sl[6].h, wrt[(si, p - 1) if p > 0 else (si - 1, NP - 1)])
                    S.copy(out=sqs[:, 0:PW], in_=ps[6][:, 0:PW])
                    assert s_s.val == vte(si, p)
        @b.sync
        def _(sy):
            for si in range(HKV):
                for p in range(NP):
                    sy.wait_ge(s_s.h, vte(si, p))
                    for cc in range(4):
                        sy.dma_start(
                            out=dr["vsl"][p * 4 + cc, :, si * 128:(si + 1) * 128],
                            in_=sqs[:, cc * 128:(cc + 1) * 128])\
                            .then_inc(s_sl[6].h, 16)
    kslv, vslv = s_sl[4].val, s_sl[6].val

    def _shwrite(half, sy):
        sy.wait_ge(s_sl[4].h, kslv)
        sy.wait_ge(s_sl[5].h, s_sl[5].val)
        sy.wait_ge(s_sl[6].h, vslv)
        sy.wait_ge(s_v.h, kmax_done_v)
        sy.dma_start(out=dr["ksh"][half], in_=dr["ksl"][:]).then_inc(s_a.h, 16)
        sy.dma_start(out=dr["vsh"][half], in_=dr["vsl"][:]).then_inc(s_a.h, 16)
        sy.dma_start(out=dr["rsh"][half, 0:8], in_=rows[0:1, T + 16:T + 24])\
            .then_inc(s_a.h, 16)
        s_a.inc(48)
    _sync_switch(ctx, 3, _shwrite)
    ctx["proj_write_targets"] = (s_sl[4].val, s_sl[5].val, s_sl[6].val, s_a.val)


def emit_barrier_and_bounds(ctx, negC):
    nc = ctx["nc"]; sb = ctx["sb"]; ps = ctx["ps"]; dr = ctx["dr"]
    s_a, s_mm, s_v, s_s, s_w = ctx["s_a"], ctx["s_mm"], ctx["s_v"], ctx["s_s"], ctx["s_w"]
    rows = sb["rows"]
    t4, t5, t6, ta = ctx["proj_write_targets"]
    with nc.Block() as b:
        @b.gpsimd
        def _(g):
            g.wait_ge(ctx["s_sl"][4].h, t4)
            g.wait_ge(ctx["s_sl"][5].h, t5)
            g.wait_ge(ctx["s_sl"][6].h, t6)
            g.wait_ge(s_a.h, ta)
    nc.all_core_barrier()
    s_bar = ctx["s_bar"]
    with nc.Block() as b:
        @b.gpsimd
        def _(g):
            g.sem_inc(s_bar.h, 1)
            s_bar.inc(1)
    bar = s_bar.val
    r = {}
    with nc.Block() as b:
        @b.sync
        def _(sy):
            sy.wait_ge(s_bar.h, bar)
            sy.dma_start(out=rows[0:1, T + 24:T + 32], in_=dr["rsh"][0, 0:8])\
                .then_inc(s_a.h, 16)
            sy.dma_start(out=rows[0:1, T + 32:T + 40], in_=dr["rsh"][1, 0:8])\
                .then_inc(s_a.h, 16)
            s_a.inc(32)
            r["a"] = s_a.val
        @b.vector
        def _(v):
            V = SelfSync(v, s_v)
            v.wait_ge(s_a.h, r["a"])
            V.tensor_tensor(out=rows[0:1, T + 24:T + 32], in0=rows[0:1, T + 24:T + 32],
                            in1=rows[0:1, T + 32:T + 40], op=ALU.max)
            r["v"] = s_v.val
        @b.scalar
        def _(s):
            S = SelfSync(s, s_s)
            s.wait_ge(s_v.h, r["v"])
            S.activation(out=rows[0:1, T + 64:T + 72], in_=rows[0:1, T + 24:T + 32],
                         func=AF.Sqrt)
            S.activation(out=rows[0:1, T + 80:T + 96], in_=rows[0:1, T:T + 16], func=AF.Sqrt)
            r["s"] = s_s.val
        @b.vector
        def _(v):
            V = SelfSync(v, s_v)
            v.wait_ge(s_s.h, r["s"])
            for par in range(2):
                V.tensor_tensor(out=rows[0:1, T + 48 + par:T + 64:2],
                                in0=rows[0:1, T + 80 + par:T + 96:2],
                                in1=rows[0:1, T + 64:T + 72], op=ALU.mult)
            V.tensor_scalar(out=rows[0:1, T + 48:T + 64], in0=rows[0:1, T + 48:T + 64],
                            scalar1=-SCALE, scalar2=None, op0=ALU.mult)
            r["v2"] = s_v.val
    r2 = {}
    with nc.Block() as b:
        @b.tensor
        def _(t):
            t.wait_ge(s_v.h, r["v2"])
            t.matmul(ps[5][:, 0:16], sb["onesrowf"][:, :], rows[0:1, T + 48:T + 64],
                     start=True, stop=True).then_inc(s_mm.h, 1)
            s_mm.inc()
            r2["m"] = s_mm.val
        @b.scalar
        def _(s):
            S = SelfSync(s, s_s)
            s.wait_ge(s_mm.h, r2["m"])
            S.copy(out=negC[:], in_=ps[5][:, 0:16])
    ctx["negc_done_s"] = s_s.val


def emit_flash_delta_wo(ctx, role, q, attnv, kst, vst, ptb, negC, dS, e1a, e2a, dsc, rbc):
    nc = ctx["nc"]; sb = ctx["sb"]; ps = ctx["ps"]; dr = ctx["dr"]
    s_mm, s_v, s_s = ctx["s_mm"], ctx["s_v"], ctx["s_s"]
    s_sl = ctx["s_sl"]
    x, rows = sb["x"], sb["rows"]
    t = nc.tensor; v = nc.vector; sc = nc.scalar; sy = nc.sync; gp = nc.gpsimd
    V = SelfSync(v, s_v)
    S = SelfSync(sc, s_s)

    t.wait_ge(s_s.h, ctx["negc_done_s"])

    prev_l_evac_s = 0
    prev_av_evac = {}
    wo_done_mm = 0
    wo_done_v = 0
    for p in range(NP):
        qlo = role * T + p * PW
        kts = list(range(0, (qlo + PW - 1) // 128 + 1))
        n_kt = len(kts)
        for hp in range(8):
            qh0, qh1 = 2 * hp, 2 * hp + 1
            m_f = s_mm.val
            S_mm = lambda i, qhl: m_f + 6 * i + 3 * qhl + 1
            PV_mm = lambda i, qhl: m_f + 6 * i + 3 * qhl + 2
            L_mm = lambda i, qhl: m_f + 6 * i + 3 * qhl + 3
            KSEM = (4, 5, 8, 9)
            VSEM = (6, 7, 10, 11)
            kd = {}
            vd = {}
            for i, kt in enumerate(kts):
                kd[i] = s_sl[KSEM[i % 4]].inc(16)
                vd[i] = s_sl[VSEM[i % 4]].inc(16)
            for i, kt in enumerate(kts):
                slot = i % 4
                if i >= 4:
                    sy.wait_ge(s_mm.h, S_mm(i - 4, 1))
                    sy.wait_ge(s_mm.h, PV_mm(i - 4, 1))
                else:
                    sy.wait_ge(s_mm.h, m_f)
                sy.dma_start(out=kst[:, slot * 256:slot * 256 + 128],
                             in_=dr["ksh"][kt // NCH, hp, kt % NCH, :, :])\
                    .then_inc(s_sl[KSEM[i % 4]].h, 16)
                sy.dma_start(out=vst[:, slot * 256:slot * 256 + 128],
                             in_=dr["vsh"][kt // NCH, kt % NCH, :, hp * 128:(hp + 1) * 128])\
                    .then_inc(s_sl[VSEM[i % 4]].h, 16)
            exp_done = {}
            for i, kt in enumerate(kts):
                slot = i % 4
                diag = kt * 128 >= qlo
                o = kt - qlo // 128 if diag else 0
                for qhl, qh in ((0, qh0), (1, qh1)):
                    t.wait_ge(s_sl[KSEM[i % 4]].h, kd[i])
                    if i >= 1:
                        t.wait_ge(s_s.h, exp_done.get((i - 1, qhl, "s"), 0))
                        if (i - 1, qhl, "v") in exp_done:
                            t.wait_ge(s_v.h, exp_done[(i - 1, qhl, "v")])
                    if i == 0 and hp == 0 and p == 1:
                        t.wait_ge(s_v.h, wo_done_v)
                    t.matmul(ps[qhl][:, :], kst[:, slot * 256:slot * 256 + 128],
                             q[:, qh * T + p * PW: qh * T + (p + 1) * PW],
                             start=True, stop=True).then_inc(s_mm.h, 1)
                    s_mm.inc()
                    assert s_mm.val == S_mm(i, qhl)
                    ptslot = (i % 2) * 2 + qhl
                    sc.wait_ge(s_mm.h, S_mm(i, qhl))
                    if i >= 2:
                        sc.wait_ge(s_mm.h, L_mm(i - 2, qhl))
                    S.activation(out=ptb[:, ptslot * PW:(ptslot + 1) * PW],
                                 in_=ps[qhl][:, :], func=AF.Exp,
                                 bias=negC[:, qh:qh + 1], scale=SCALE)
                    exp_done[(i, qhl, "s")] = s_s.val
                    if diag:
                        v.wait_ge(s_s.h, s_s.val)
                        V.tensor_tensor(out=ptb[:, ptslot * PW:(ptslot + 1) * PW],
                                        in0=ptb[:, ptslot * PW:(ptslot + 1) * PW],
                                        in1=sb["fmask"][:, o * PW:(o + 1) * PW], op=ALU.mult)
                        exp_done[(i, qhl, "v")] = s_v.val
                        t.wait_ge(s_v.h, s_v.val)
                    else:
                        t.wait_ge(s_s.h, s_s.val)
                    t.wait_ge(s_sl[VSEM[i % 4]].h, vd[i])
                    if i == 0:
                        if (qhl,) in prev_av_evac:
                            t.wait_ge(s_v.h, prev_av_evac[(qhl,)])
                        if qhl == 0 and prev_l_evac_s:
                            t.wait_ge(s_s.h, prev_l_evac_s)
                    t.matmul(ps[2 + qhl][:, :], vst[:, slot * 256:slot * 256 + 128],
                             ptb[:, ptslot * PW:(ptslot + 1) * PW],
                             start=(i == 0), stop=(i == n_kt - 1)).then_inc(s_mm.h, 1)
                    s_mm.inc()
                    assert s_mm.val == PV_mm(i, qhl)
                    t.matmul(ps[6][0:4, 0:PW], sb["lcol"][:, qhl * 4:(qhl + 1) * 4],
                             ptb[:, ptslot * PW:(ptslot + 1) * PW],
                             start=(i == 0 and qhl == 0),
                             stop=(i == n_kt - 1 and qhl == 1)).then_inc(s_mm.h, 1)
                    s_mm.inc()
                    assert s_mm.val == L_mm(i, qhl)
            lmm = s_mm.val
            sc.wait_ge(s_mm.h, lmm)
            S.copy(out=rows[0:2, 2 * T:2 * T + PW], in_=ps[6][0:2, 0:PW])
            prev_l_evac_s = s_s.val
            v.wait_ge(s_s.h, s_s.val)
            V.reciprocal(out=rows[0:2, 2 * T + PW:2 * T + 2 * PW],
                         in_=rows[0:2, 2 * T:2 * T + PW])
            sc.wait_ge(s_v.h, s_v.val)
            S.copy(out=sb["sc4"][0:2, 0:PW], in_=rows[0:2, 2 * T + PW:2 * T + 2 * PW])
            lrec_s = s_s.val
            for qhl, qh in ((0, qh0), (1, qh1)):
                t.wait_ge(s_s.h, lrec_s)
                t.matmul(ps[4 + qhl][:, :], sb["sel16"][0:2, qhl * 128:(qhl + 1) * 128],
                         sb["sc4"][0:2, 0:PW], start=True, stop=True).then_inc(s_mm.h, 1)
                s_mm.inc()
                sc.wait_ge(s_mm.h, s_mm.val)
                sc.wait_ge(s_v.h, s_v.val)
                S.copy(out=rbc[:, :], in_=ps[4 + qhl][:, :])
                v.wait_ge(s_s.h, s_s.val)
                v.wait_ge(s_mm.h, PV_mm(n_kt - 1, qhl))
                if p > 0:
                    v.wait_ge(s_mm.h, wo_done_mm)
                V.tensor_tensor(out=attnv[:, qh * PW:(qh + 1) * PW],
                                in0=ps[2 + qhl][:, :], in1=rbc[:, :], op=ALU.mult)
                prev_av_evac[(qhl,)] = s_v.val
        av_last = s_v.val
        wdt = {}
        for c in range(KI):
            wdt[c] = s_sl[c % 2].inc(16)
        m0 = s_mm.val
        wm = lambda c: m0 + c + 1
        v0w = s_v.val
        for c in range(KI):
            if c < 2:
                gp.wait_ge(s_mm.h, m0)
            else:
                gp.wait_ge(s_mm.h, wm(c - 2))
            gp.dma_start(out=sb["wbufA" if c % 2 == 0 else "wbufB"][:],
                         in_=dr["wo"][c].rearrange("p k m -> p (k m)"))\
                .then_inc(s_sl[c % 2].h, 16)
        for c in range(KI):
            t.wait_ge(s_sl[c % 2].h, wdt[c])
            if c == 0:
                t.wait_ge(s_v.h, av_last)
            if c >= 2:
                t.wait_ge(s_v.h, v0w + c - 1)
            wbuf = sb["wbufA" if c % 2 == 0 else "wbufB"]
            ins = None
            for ki in range(KI):
                ins = t.matmul(ps[(c % 2)][:, :], wbuf[:, ki * 128:(ki + 1) * 128],
                               attnv[:, ki * PW:(ki + 1) * PW],
                               start=(ki == 0), stop=(ki == KI - 1))
            ins.then_inc(s_mm.h, 1)
            s_mm.inc()
            assert s_mm.val == wm(c)
            v.wait_ge(s_mm.h, wm(c))
            V.scalar_tensor_tensor(out=x[:, c * T + p * PW: c * T + (p + 1) * PW],
                                   in0=ps[(c % 2)][:, :], scalar=1.0 - FLA_W,
                                   in1=x[:, c * T + p * PW: c * T + (p + 1) * PW],
                                   op0=ALU.mult, op1=ALU.add)
        wo_done_mm = s_mm.val
        wo_done_v = s_v.val

    emit_delta(ctx, role, q, kst, vst, dS, e1a, e2a, dsc)


def emit_delta(ctx, role, q, kst, vst, dS, e1a, e2a, dsc):
    nc = ctx["nc"]; sb = ctx["sb"]; ps = ctx["ps"]; dr = ctx["dr"]
    s_mm, s_v, s_s = ctx["s_mm"], ctx["s_v"], ctx["s_s"]
    s_sl = ctx["s_sl"]
    x, rows = sb["x"], sb["rows"]
    t = nc.tensor; v = nc.vector; sc = nc.scalar; sy = nc.sync
    V = SelfSync(v, s_v)
    S = SelfSync(sc, s_s)
    ident, negident = sb["ident"], sb["negident"]
    msl, msu_neg, mti_neg = sb["msl"], sb["msu_neg"], sb["mti_neg"]
    E = sb["sc3"]
    RD = 2 * T
    SLOT = {n: i * 128 for i, n in enumerate(
        ("L", "LT", "P", "PT", "P2", "PT2", "TT", "Rb", "Ub", "Q1", "Mn", "Ktn"))}
    R_G, R_CA, R_CB, R_BETA, R_IBETA, R_KHAT, R_KBAR, R_NEGB, R_KTIL = \
        [RD + 128 * j for j in range(1, 10)]
    R_RQC, R_QSC = 3 * T + 512, 3 * T + 640

    chunks = list(range(16)) if role == 1 else list(range(8))
    own0 = 8 if role == 1 else 0

    V.memset(dS[:], 0.0)
    for c in chunks:
        own = (c >= own0)
        lc = c - own0
        kt_t = s_sl[4].inc(16)
        vt_t = s_sl[6].inc(16)
        sy.wait_ge(s_mm.h, s_mm.val)
        sy.wait_ge(s_s.h, s_s.val)
        sy.dma_start(out=kst[:].rearrange("d (s m) -> d s m", s=8),
                     in_=dr["ksh"][c // NCH, :, c % NCH, :, :].rearrange("s d m -> d s m"))\
            .then_inc(s_sl[4].h, 16)
        sy.dma_start(out=vst[:], in_=dr["vsh"][c // NCH, c % NCH, :, :])\
            .then_inc(s_sl[6].h, 16)
        t.wait_ge(s_sl[4].h, kt_t)
        t.wait_ge(s_v.h, s_v.val)
        ins = None
        for s8 in range(8):
            ins = t.matmul(ps[6][0:16, 0:128], sb["wgp"][:, s8 * 16:(s8 + 1) * 16],
                           kst[:, s8 * 128:(s8 + 1) * 128], start=(s8 == 0), stop=(s8 == 7))
        ins.then_inc(s_mm.h, 1)
        s_mm.inc()
        zmm = s_mm.val
        sc.wait_ge(s_mm.h, zmm)
        S.activation(out=rows[0:8, R_G:R_G + 128], in_=ps[6][0:8, 0:128],
                     func=AF.Sigmoid, bias=sb["bg_sb"][0:8, :], scale=1.0)
        S.activation(out=rows[0:8, R_G:R_G + 128], in_=rows[0:8, R_G:R_G + 128], func=AF.Ln)
        g_s = s_s.val
        v.wait_ge(s_s.h, g_s)
        V.tensor_scalar(out=rows[0:8, R_CA:R_CA + 128], in0=rows[0:8, R_G:R_G + 128],
                        scalar1=1.0 / 16.0, scalar2=None, op0=ALU.mult)
        src, dst = R_CA, R_CB
        for sh in (1, 2, 4, 8, 16, 32, 64):
            V.tensor_copy(rows[0:8, dst:dst + sh], rows[0:8, src:src + sh])
            V.tensor_tensor(out=rows[0:8, dst + sh:dst + 128],
                            in0=rows[0:8, src + sh:src + 128],
                            in1=rows[0:8, src:src + 128 - sh], op=ALU.add)
            src, dst = dst, src
        cums_v = s_v.val
        sc.wait_ge(s_v.h, cums_v)
        S.activation(out=rows[0:8, R_BETA:R_BETA + 128], in_=rows[0:8, src:src + 128],
                     func=AF.Exp)
        beta_s = s_s.val
        v.wait_ge(s_s.h, beta_s)
        V.reciprocal(out=rows[0:8, R_IBETA:R_IBETA + 128], in_=rows[0:8, R_BETA:R_BETA + 128])
        sc.wait_ge(s_sl[4].h, kt_t)
        S.activation(out=E[:, 0:1024], in_=kst[:], func=AF.Exp)
        E_s = s_s.val
        t.wait_ge(s_s.h, E_s)
        ins = None
        for s8 in range(8):
            ins = t.matmul(ps[7][0:16, 0:128], sb["colsel"][:, s8 * 16:(s8 + 1) * 16],
                           E[:, s8 * 128:(s8 + 1) * 128], start=(s8 == 0), stop=(s8 == 7))
        ins.then_inc(s_mm.h, 1)
        s_mm.inc()
        cse_mm = s_mm.val
        v.wait_ge(s_mm.h, cse_mm)
        V.reciprocal(out=rows[0:8, R_KHAT:R_KHAT + 128], in_=ps[7][0:8, 0:128])
        V.tensor_tensor(out=rows[0:8, R_KBAR:R_KBAR + 128],
                        in0=rows[0:8, R_BETA:R_BETA + 128],
                        in1=rows[0:8, R_KHAT:R_KHAT + 128], op=ALU.mult)
        V.tensor_tensor(out=rows[0:8, R_KHAT:R_KHAT + 128],
                        in0=rows[0:8, R_IBETA:R_IBETA + 128],
                        in1=rows[0:8, R_KHAT:R_KHAT + 128], op=ALU.mult)
        V.tensor_scalar(out=rows[0:8, R_NEGB:R_NEGB + 128],
                        in0=rows[0:8, R_BETA:R_BETA + 128],
                        scalar1=-1.0, scalar2=None, op0=ALU.mult)
        V.tensor_scalar(out=rows[0:8, R_KTIL:R_KTIL + 128],
                        in0=rows[0:8, R_KHAT:R_KHAT + 128],
                        scalar1=rows[0:8, R_NEGB + 127:R_NEGB + 128], scalar2=None,
                        op0=ALU.mult)
        scales_v = s_v.val
        t.wait_ge(s_v.h, scales_v)
        ins = None
        for s8 in range(8):
            bank = ps[4] if s8 < 4 else ps[5]
            ins = t.matmul(bank[:, (s8 % 4) * 128:(s8 % 4 + 1) * 128],
                           sb["sel16f"][0:8, s8 * 128:(s8 + 1) * 128],
                           rows[0:8, R_KBAR:R_KBAR + 128], start=True, stop=True)
        ins.then_inc(s_mm.h, 1)
        s_mm.inc()
        b1mm = s_mm.val
        v.wait_ge(s_mm.h, b1mm)
        V.tensor_tensor(out=e1a[:, 0:512], in0=E[:, 0:512], in1=ps[4][:, :], op=ALU.mult)
        V.tensor_tensor(out=e1a[:, 512:1024], in0=E[:, 512:1024], in1=ps[5][:, :],
                        op=ALU.mult)
        e1_v = s_v.val
        t.wait_ge(s_v.h, e1_v)
        ins = None
        for s8 in range(8):
            bank = ps[4] if s8 < 4 else ps[5]
            ins = t.matmul(bank[:, (s8 % 4) * 128:(s8 % 4 + 1) * 128],
                           sb["sel16f"][0:8, s8 * 128:(s8 + 1) * 128],
                           rows[0:8, R_KHAT:R_KHAT + 128], start=True, stop=True)
        ins.then_inc(s_mm.h, 1)
        s_mm.inc()
        b2mm = s_mm.val
        v.wait_ge(s_mm.h, b2mm)
        V.tensor_tensor(out=e2a[:, 0:512], in0=E[:, 0:512], in1=ps[4][:, :], op=ALU.mult)
        V.tensor_tensor(out=e2a[:, 512:1024], in0=E[:, 512:1024], in1=ps[5][:, :],
                        op=ALU.mult)
        e2_v = s_v.val

        for s8 in range(HKV):
            E1 = e1a[:, s8 * 128:(s8 + 1) * 128]
            E2 = e2a[:, s8 * 128:(s8 + 1) * 128]
            Esl = E[:, s8 * 128:(s8 + 1) * 128]
            Ssl = dS[:, s8 * 128:(s8 + 1) * 128]
            Vsl = vst[:, s8 * 128:(s8 + 1) * 128]
            t.wait_ge(s_v.h, e2_v if s8 == 0 else s_v.val)
            t.matmul(ps[4][:, 0:128], E1, E2, start=True, stop=True).then_inc(s_mm.h, 1)
            s_mm.inc()
            t.matmul(ps[4][:, 128:256], E2, E1, start=True, stop=True).then_inc(s_mm.h, 1)
            s_mm.inc()
            lmm = s_mm.val
            v.wait_ge(s_mm.h, lmm)
            V.tensor_tensor(out=dsc[:, SLOT["L"]:SLOT["L"] + 128], in0=ps[4][:, 0:128],
                            in1=msl[:], op=ALU.mult)
            V.tensor_tensor(out=dsc[:, SLOT["TT"]:SLOT["TT"] + 128], in0=ps[4][:, 128:256],
                            in1=msu_neg[:], op=ALU.mult)
            V.tensor_tensor(out=dsc[:, SLOT["TT"]:SLOT["TT"] + 128],
                            in0=dsc[:, SLOT["TT"]:SLOT["TT"] + 128], in1=ident[:], op=ALU.add)
            V.tensor_tensor(out=dsc[:, SLOT["LT"]:SLOT["LT"] + 128], in0=ps[4][:, 128:256],
                            in1=msu_neg[:], op=ALU.mult)
            V.tensor_scalar(out=dsc[:, SLOT["LT"]:SLOT["LT"] + 128],
                            in0=dsc[:, SLOT["LT"]:SLOT["LT"] + 128], scalar1=-1.0,
                            scalar2=None, op0=ALU.mult)
            Pn, PTn = "L", "LT"
            for lev in range(1, NLEV):
                dst_p = "P" if Pn in ("L", "P2") else "P2"
                dst_pt = "PT" if PTn in ("LT", "PT2") else "PT2"
                t.wait_ge(s_v.h, s_v.val)
                t.matmul(ps[5][:, 0:128], dsc[:, SLOT[PTn]:SLOT[PTn] + 128],
                         dsc[:, SLOT[Pn]:SLOT[Pn] + 128], start=True, stop=True)\
                    .then_inc(s_mm.h, 1)
                s_mm.inc()
                if lev < NLEV - 1:
                    t.matmul(ps[5][:, 128:256], dsc[:, SLOT[Pn]:SLOT[Pn] + 128],
                             dsc[:, SLOT[PTn]:SLOT[PTn] + 128], start=True, stop=True)\
                        .then_inc(s_mm.h, 1)
                    s_mm.inc()
                pmm = s_mm.val
                v.wait_ge(s_mm.h, pmm)
                V.tensor_copy(dsc[:, SLOT[dst_p]:SLOT[dst_p] + 128], ps[5][:, 0:128])
                if lev < NLEV - 1:
                    V.tensor_copy(dsc[:, SLOT[dst_pt]:SLOT[dst_pt] + 128], ps[5][:, 128:256])
                t.wait_ge(s_v.h, s_v.val)
                t.matmul(ps[6][:, 0:128], dsc[:, SLOT[dst_p]:SLOT[dst_p] + 128],
                         dsc[:, SLOT["TT"]:SLOT["TT"] + 128], start=True, stop=False)
                t.matmul(ps[6][:, 0:128], ident[:], dsc[:, SLOT["TT"]:SLOT["TT"] + 128],
                         start=False, stop=True).then_inc(s_mm.h, 1)
                s_mm.inc()
                tmm = s_mm.val
                v.wait_ge(s_mm.h, tmm)
                V.tensor_copy(dsc[:, SLOT["TT"]:SLOT["TT"] + 128], ps[6][:, 0:128])
                Pn, PTn = dst_p, dst_pt
            t.wait_ge(s_v.h, s_v.val)
            t.wait_ge(s_sl[6].h, vt_t)
            t.matmul(ps[4][:, 0:128], E1, Ssl, start=True, stop=False)
            t.matmul(ps[4][:, 0:128], negident[:], Vsl, start=False, stop=True)\
                .then_inc(s_mm.h, 1)
            s_mm.inc()
            rmm = s_mm.val
            v.wait_ge(s_mm.h, rmm)
            V.tensor_copy(dsc[:, SLOT["Rb"]:SLOT["Rb"] + 128], ps[4][:, 0:128])
            rbv = s_v.val
            t.wait_ge(s_v.h, rbv)
            t.matmul(ps[5][:, 0:128], dsc[:, SLOT["TT"]:SLOT["TT"] + 128],
                     dsc[:, SLOT["Rb"]:SLOT["Rb"] + 128], start=True, stop=True)\
                .then_inc(s_mm.h, 1)
            s_mm.inc()
            umm = s_mm.val
            v.wait_ge(s_mm.h, umm)
            V.tensor_copy(dsc[:, SLOT["Ub"]:SLOT["Ub"] + 128], ps[5][:, 0:128])
            ubv = s_v.val
            if own:
                for qh in (2 * s8, 2 * s8 + 1):
                    sc.wait_ge(s_mm.h, s_mm.val)
                    S.activation(out=dsc[:, SLOT["Q1"]:SLOT["Q1"] + 128],
                                 in_=q[:, qh * T + lc * 128: qh * T + (lc + 1) * 128],
                                 func=AF.Exp)
                    eq_s = s_s.val
                    t.wait_ge(s_s.h, eq_s)
                    t.wait_ge(s_v.h, s_v.val)
                    t.matmul(ps[6][0:1, 0:128], sb["ones"][:, :],
                             dsc[:, SLOT["Q1"]:SLOT["Q1"] + 128], start=True, stop=True)\
                        .then_inc(s_mm.h, 1)
                    s_mm.inc()
                    t.matmul(ps[6][:, 128:256],
                             sb["sel16f"][0:8, s8 * 128:(s8 + 1) * 128],
                             rows[0:8, R_BETA:R_BETA + 128], start=True, stop=True)\
                        .then_inc(s_mm.h, 1)
                    s_mm.inc()
                    qcm = s_mm.val
                    v.wait_ge(s_mm.h, qcm)
                    V.reciprocal(out=rows[0:1, R_RQC:R_RQC + 128], in_=ps[6][0:1, 0:128])
                    V.tensor_tensor(out=rows[0:1, R_QSC:R_QSC + 128],
                                    in0=ps[6][0:1, 128:256],
                                    in1=rows[0:1, R_RQC:R_RQC + 128], op=ALU.mult)
                    qsc_v = s_v.val
                    t.wait_ge(s_v.h, qsc_v)
                    t.matmul(ps[6][:, 128:256], sb["onesrowf"][:, :],
                             rows[0:1, R_QSC:R_QSC + 128], start=True, stop=True)\
                        .then_inc(s_mm.h, 1)
                    s_mm.inc()
                    qbm = s_mm.val
                    v.wait_ge(s_mm.h, qbm)
                    V.tensor_tensor(out=dsc[:, SLOT["Q1"]:SLOT["Q1"] + 128],
                                    in0=dsc[:, SLOT["Q1"]:SLOT["Q1"] + 128],
                                    in1=ps[6][:, 128:256], op=ALU.mult)
                    q1v = s_v.val
                    t.wait_ge(s_v.h, q1v)
                    t.matmul(ps[6][:, 0:128], E2, dsc[:, SLOT["Q1"]:SLOT["Q1"] + 128],
                             start=True, stop=True).then_inc(s_mm.h, 1)
                    s_mm.inc()
                    mm_ = s_mm.val
                    v.wait_ge(s_mm.h, mm_)
                    V.tensor_tensor(out=dsc[:, SLOT["Mn"]:SLOT["Mn"] + 128],
                                    in0=ps[6][:, 0:128], in1=mti_neg[:], op=ALU.mult)
                    mnv = s_v.val
                    t.wait_ge(s_v.h, mnv)
                    t.matmul(ps[7][:, 0:128], Ssl, dsc[:, SLOT["Q1"]:SLOT["Q1"] + 128],
                             start=True, stop=False)
                    t.matmul(ps[7][:, 0:128], dsc[:, SLOT["Ub"]:SLOT["Ub"] + 128],
                             dsc[:, SLOT["Mn"]:SLOT["Mn"] + 128], start=False, stop=True)\
                        .then_inc(s_mm.h, 1)
                    s_mm.inc()
                    omm = s_mm.val
                    v.wait_ge(s_mm.h, omm)
                    V.scalar_tensor_tensor(
                        out=x[:, qh * T + lc * 128: qh * T + (lc + 1) * 128],
                        in0=ps[7][:, 0:128], scalar=FLA_W * SCALE,
                        in1=x[:, qh * T + lc * 128: qh * T + (lc + 1) * 128],
                        op0=ALU.mult, op1=ALU.add)
            t.wait_ge(s_v.h, s_v.val)
            t.matmul(ps[6][:, 128:256], sb["sel16f"][0:8, s8 * 128:(s8 + 1) * 128],
                     rows[0:8, R_KTIL:R_KTIL + 128], start=True, stop=True)\
                .then_inc(s_mm.h, 1)
            s_mm.inc()
            ktrow_mm = s_mm.val
            sc.wait_ge(s_mm.h, ktrow_mm)
            S.copy(out=sb["sc4"][0:1, 0:128], in_=ps[6][0:1, 128:256])
            ktrow_s = s_s.val
            t.wait_ge(s_s.h, ktrow_s)
            t.matmul(ps[6][:, 256:257], sb["sc4"][0:1, 0:128], sb["ones"][0:1, :],
                     start=True, stop=True).then_inc(s_mm.h, 1)
            s_mm.inc()
            t.matmul(ps[5][:, 128:256], Esl, ident[:], start=True, stop=True)\
                .then_inc(s_mm.h, 1)
            s_mm.inc()
            etcol_mm = s_mm.val
            v.wait_ge(s_mm.h, etcol_mm)
            V.tensor_scalar(out=dsc[:, SLOT["Ktn"]:SLOT["Ktn"] + 128],
                            in0=ps[5][:, 128:256], scalar1=ps[6][:, 256:257], scalar2=None,
                            op0=ALU.mult)
            ktn_v = s_v.val
            t.wait_ge(s_v.h, ktn_v)
            t.matmul(ps[6][:, 128:256], sb["sel16f"][0:8, s8 * 128:(s8 + 1) * 128],
                     rows[0:8, R_BETA:R_BETA + 128], start=True, stop=True)\
                .then_inc(s_mm.h, 1)
            s_mm.inc()
            bcol_mm = s_mm.val
            v.wait_ge(s_mm.h, bcol_mm)
            V.tensor_scalar(out=sb["ctm"][:], in0=ident[:], scalar1=ps[6][:, 255:256],
                            scalar2=None, op0=ALU.mult)
            bI_v = s_v.val
            t.wait_ge(s_v.h, bI_v)
            t.matmul(ps[7][:, 0:128], sb["ctm"][:], Ssl, start=True, stop=False)
            t.matmul(ps[7][:, 0:128], dsc[:, SLOT["Ktn"]:SLOT["Ktn"] + 128],
                     dsc[:, SLOT["Ub"]:SLOT["Ub"] + 128], start=False, stop=True)\
                .then_inc(s_mm.h, 1)
            s_mm.inc()
            s1mm = s_mm.val
            sc.wait_ge(s_mm.h, s1mm)
            S.copy(out=Ssl, in_=ps[7][:, 0:128])


_NC_CACHE = {}


def kernel(**inputs):
    stage = 4
    g = host_prep(inputs)
    x = np.asarray(inputs["L0_attn_hidden_states"], np.float32)
    if stage not in _NC_CACHE:
        _NC_CACHE[stage] = build_nc(stage)
    nc = _NC_CACHE[stage]
    shared_names = ["wq", "wk", "wv", "wo", "wg", "bg16", "gate", "wgate", "wup", "wdown",
                    "shgate", "shup", "shdown", "c_ident", "c_ones", "c_onesrow", "c_fmask",
                    "c_msl", "c_msu_neg", "c_mti_neg", "c_sel16", "c_lcol", "c_colsel", "wgplace"]
    in_maps = []
    for c in range(8):
        seq, half = c // 2, c % 2
        xc = np.ascontiguousarray(
            x[seq, half * T:(half + 1) * T, :].T.reshape(KI, 128, T))
        m = {"x_in": xc, "rolei": np.full(8, half, np.int32)}
        for n in shared_names:
            m[n] = g[n]
        in_maps.append(m)
    res = run_bass_kernel_spmd(nc, in_maps, core_ids=list(range(8)))
    out = np.empty((B, N, DM), np.float32)
    for c in range(8):
        seq, half = c // 2, c % 2
        yc = res.results[c]["y"]
        out[seq, half * T:(half + 1) * T, :] = yc.reshape(DM, T).T
    return out


# version 02
# speedup vs baseline: 1.1067x; 1.1067x over previous
import math
import numpy as np
from contextlib import ExitStack

import concourse.bass as bass
import concourse.mybir as mybir
from concourse.bass_utils import run_bass_kernel_spmd

F32 = mybir.dt.float32
BF16 = mybir.dt.bfloat16
I32 = mybir.dt.int32
AF = mybir.ActivationFunctionType
ALU = mybir.AluOpType

B, N, DM = 4, 2048, 2048
H, HKV, DH = 16, 8, 128
NE, HE, HS = 8, 1024, 2048
SCALE = 1.0 / math.sqrt(DH)
FLA_W = 0.01
T = 1024
KI = DM // 128
NP, PW = 2, 512
CH = 128
NCH = T // CH
EPS = 1e-5
NHE = HE // 128
NHS = HS // 128
NLEV = 7


class Sem:
    def __init__(self, es, nc, name):
        self.h = es.enter_context(nc.semaphore(name))
        self.val = 0

    def inc(self, n=1):
        self.val += n
        return self.val


class SelfSync:
    _WRAP = ("tensor_tensor", "tensor_scalar", "scalar_tensor_tensor", "tensor_copy",
             "tensor_add", "tensor_sub", "tensor_mul", "tensor_max", "tensor_relu",
             "reduce_max", "reduce_sum", "tensor_reduce", "reciprocal", "memset",
             "activation", "copy", "mul", "add", "square", "sqrt", "select",
             "tensor_single_scalar", "iota", "bn_stats", "bn_aggr", "memzero")

    def __init__(self, eng, sem):
        self._e = eng
        self._s = sem

    def __getattr__(self, name):
        f = getattr(self._e, name)
        if name not in SelfSync._WRAP:
            return f
        def call(*a, **k):
            self._e.wait_ge(self._s.h, self._s.val)
            ins = f(*a, **k)
            ins.then_inc(self._s.h, 1)
            self._s.inc()
            return ins
        return call


def stream_layout(W):
    d_in, d_out = W.shape
    v = W.reshape(d_in // 128, 128, d_out // 128, 128)
    return np.ascontiguousarray(v.transpose(2, 1, 0, 3), dtype=np.float32)


def host_prep(inputs):
    g = {}
    g["wq"] = stream_layout(np.asarray(inputs["L0_attn_Wq"], np.float32))
    g["wk"] = stream_layout(np.asarray(inputs["L0_attn_Wk"], np.float32))
    g["wv"] = stream_layout(np.asarray(inputs["L0_attn_Wv"], np.float32))
    g["wo"] = stream_layout(np.asarray(inputs["L0_attn_Wo"], np.float32))
    g["wg"] = np.ascontiguousarray(np.asarray(inputs["L0_attn_Wg"], np.float32).reshape(128, 1))
    g["bg16"] = np.full((16, 1), float(np.asarray(inputs["L0_attn_bg"]).reshape(-1)[0]), np.float32)
    gw = np.concatenate([np.asarray(inputs["L0_ffn_gate_w"], np.float32),
                         np.asarray(inputs["L0_ffn_se_gate_w"], np.float32)], axis=0).T
    gw = np.concatenate([gw, np.zeros((DM, 7), np.float32)], axis=1)
    g["gate"] = np.ascontiguousarray(gw.reshape(KI, 128, 16))
    wgate = np.asarray(inputs["L0_ffn_w_gate"], np.float32)
    wup = np.asarray(inputs["L0_ffn_w_up"], np.float32)
    wdown = np.asarray(inputs["L0_ffn_w_down"], np.float32)
    g["wgate"] = np.stack([stream_layout(wgate[e].T) for e in range(NE)])
    g["wup"] = np.stack([stream_layout(wup[e].T) for e in range(NE)])
    g["wdown"] = np.stack([stream_layout(wdown[e].T) for e in range(NE)])
    g["shgate"] = stream_layout(np.ascontiguousarray(np.asarray(inputs["L0_ffn_sh_gate"], np.float32).T))
    g["shup"] = stream_layout(np.ascontiguousarray(np.asarray(inputs["L0_ffn_sh_up"], np.float32).T))
    g["shdown"] = stream_layout(np.ascontiguousarray(np.asarray(inputs["L0_ffn_sh_down"], np.float32).T))
    g["c_ident"] = np.eye(128, dtype=np.float32)
    g["c_ones"] = np.ones((128, 1), np.float32)
    g["c_onesrow"] = np.ones((1, 128), np.float32)
    fm = np.zeros((4, 128, PW), np.float32)
    for o in range(4):
        for i in range(128):
            fm[o, i, o * 128 + i:] = 1.0
    g["c_fmask"] = fm
    sl = np.tril(np.ones((128, 128), np.float32), -1)
    g["c_msl"] = sl
    g["c_msu_neg"] = np.ascontiguousarray(-sl.T)
    g["c_mti_neg"] = -np.triu(np.ones((128, 128), np.float32), 0)
    sel = np.zeros((16, 16, 128), np.float32)
    for j in range(16):
        sel[j, j, :] = 1.0
    g["c_sel16"] = sel
    lc = np.zeros((4, 128, 4), np.float32)
    for j in range(4):
        lc[j, :, j] = 1.0
    g["c_lcol"] = lc
    cs = np.zeros((16, 128, 16), np.float32)
    for j in range(16):
        cs[j, :, j] = 1.0
    g["c_colsel"] = cs
    wgp = np.zeros((8, 128, 16), np.float32)
    for j in range(8):
        wgp[j, :, j] = np.asarray(inputs["L0_attn_Wg"], np.float32).reshape(-1)
    g["wgplace"] = wgp
    return g


def build_nc(stage):
    nc = bass.Bass("TRN2", target_bir_lowering=False, debug=False, num_devices=8)
    es = ExitStack()
    dr = {}

    def din(name, shape, dtype=F32):
        dr[name] = nc.dram_tensor(name, shape, dtype, kind="ExternalInput").ap()

    din("x_in", [KI, 128, T])
    din("rolei", [8], I32)
    dr["y"] = nc.dram_tensor("y", [KI, 128, T], F32, kind="ExternalOutput").ap()
    din("wq", [KI, 128, KI, 128]); din("wk", [HKV, 128, KI, 128]); din("wv", [HKV, 128, KI, 128])
    din("wo", [KI, 128, KI, 128]); din("wg", [128, 1]); din("bg16", [16, 1])
    din("gate", [KI, 128, 16])
    din("wgate", [NE, NHE, 128, KI, 128]); din("wup", [NE, NHE, 128, KI, 128])
    din("wdown", [NE, KI, 128, NHE, 128])
    din("shgate", [NHS, 128, KI, 128]); din("shup", [NHS, 128, KI, 128])
    din("shdown", [KI, 128, NHS, 128])
    din("c_ident", [128, 128]); din("c_ones", [128, 1]); din("c_onesrow", [1, 128])
    din("c_fmask", [4, 128, PW]); din("c_msl", [128, 128]); din("c_msu_neg", [128, 128])
    din("c_mti_neg", [128, 128]); din("c_sel16", [16, 16, 128]); din("c_lcol", [4, 128, 4])
    din("c_colsel", [16, 128, 16]); din("wgplace", [8, 128, 16])
    dr["ksh"] = nc.dram_tensor("ksh", [2, HKV, NCH, 128, 128], BF16, kind="Internal",
                               addr_space="Shared").ap()
    dr["vsh"] = nc.dram_tensor("vsh", [2, NCH, 128, HKV * DH], BF16, kind="Internal",
                               addr_space="Shared").ap()
    dr["rsh"] = nc.dram_tensor("rsh", [2, 16], F32, kind="Internal",
                               addr_space="Shared").ap()
    dr["ksl"] = nc.dram_tensor("ksl", [HKV, NCH, 128, 128], BF16, kind="Internal").ap()
    dr["vsl"] = nc.dram_tensor("vsl", [NCH, 128, HKV * DH], BF16, kind="Internal").ap()
    sb = {}

    def sbuf(name, shape, dtype, stack=es, side=None):
        t = stack.enter_context(nc.sbuf_tensor(name, shape, dtype, side=side))
        sb[name] = t
        return t

    ctx = dict(nc=nc, dr=dr, sb=sb, es=es, stage=stage, sbuf=sbuf)

    sbuf("x", [128, KI * T], F32)
    sbuf("wbufA", [128, KI * 128], BF16); sbuf("wbufB", [128, KI * 128], BF16)
    sbuf("wbufC", [128, KI * 128], BF16); sbuf("wbufD", [128, KI * 128], BF16)
    sbuf("tmpf", [128, T], F32); sbuf("tmpf2", [128, T], F32)
    sbuf("sc1", [128, T], BF16); sbuf("sc2", [128, T], BF16)
    sbuf("sc3", [128, T], BF16); sbuf("sc4", [128, T], BF16)
    sbuf("bcC", [128, T], BF16)
    sbuf("rows", [128, 4096], F32)
    sbuf("combf", [16, T], BF16)
    sbuf("scT", [16, T], BF16)
    sbuf("ctm", [128, 128], BF16)
    sbuf("gatew", [128, KI * 16], BF16)
    sbuf("ident", [128, 128], BF16); sbuf("negident", [128, 128], BF16)
    sbuf("ones", [128, 1], BF16); sbuf("onesf", [128, 1], F32)
    sbuf("onesrow", [1, 128], BF16); sbuf("onesrowf", [1, 128], F32)
    sbuf("fmask", [128, 4 * PW], BF16)
    sbuf("msl", [128, 128], BF16); sbuf("msu_neg", [128, 128], BF16); sbuf("mti_neg", [128, 128], BF16)
    sbuf("sel16", [16, 16 * 128], BF16); sbuf("lcol", [128, 4 * 4], BF16)
    sbuf("colsel", [128, 16 * 16], BF16); sbuf("wgp", [128, 8 * 16], BF16)
    sbuf("sel16f", [16, 16 * 128], F32)
    sbuf("rolet", [1, 8], I32)
    sbuf("wg_sb", [128, 1], BF16); sbuf("bg_sb", [16, 1], F32)

    s_w = Sem(es, nc, "s_w"); s_a = Sem(es, nc, "s_a")
    s_mm = Sem(es, nc, "s_mm"); s_v = Sem(es, nc, "s_v"); s_s = Sem(es, nc, "s_s")
    s_sl = [Sem(es, nc, f"s_sl{i}") for i in range(12)]
    s_bar = Sem(es, nc, "s_bar")
    ps = [es.enter_context(nc.psum_tensor(f"psb{i}", [128, 512], F32)) for i in range(8)]
    ctx.update(ps=ps, s_w=s_w, s_a=s_a, s_mm=s_mm, s_v=s_v, s_s=s_s, s_sl=s_sl, s_bar=s_bar)

    with nc.Block() as b:
        @b.gpsimd
        def _(g):
            g.dma_start(out=sb["x"][:].rearrange("p (k t) -> p k t", k=KI),
                        in_=dr["x_in"].rearrange("k p t -> p k t")).then_inc(s_w.h, 16)
            s_w.inc(16)
            for cn, dst in (("c_ident", "ident"), ("c_ones", "ones"), ("c_onesrow", "onesrow"),
                            ("c_msl", "msl"), ("c_msu_neg", "msu_neg"), ("c_mti_neg", "mti_neg"),
                            ("wg", "wg_sb")):
                g.dma_start(out=sb[dst][:], in_=dr[cn][:, :]).then_inc(s_w.h, 16)
                s_w.inc(16)
            g.dma_start(out=sb["fmask"][:].rearrange("p (o w) -> p o w", o=4),
                        in_=dr["c_fmask"].rearrange("o p w -> p o w")).then_inc(s_w.h, 16)
            s_w.inc(16)
            g.dma_start(out=sb["sel16"][:].rearrange("p (j m) -> p j m", j=16),
                        in_=dr["c_sel16"].rearrange("j p m -> p j m")).then_inc(s_w.h, 16)
            s_w.inc(16)
            g.dma_start(out=sb["lcol"][:].rearrange("p (j m) -> p j m", j=4),
                        in_=dr["c_lcol"].rearrange("j p m -> p j m")).then_inc(s_w.h, 16)
            s_w.inc(16)
            g.dma_start(out=sb["colsel"][:].rearrange("p (j m) -> p j m", j=16),
                        in_=dr["c_colsel"].rearrange("j p m -> p j m")).then_inc(s_w.h, 16)
            s_w.inc(16)
            g.dma_start(out=sb["wgp"][:].rearrange("p (j m) -> p j m", j=8),
                        in_=dr["wgplace"].rearrange("j p m -> p j m")).then_inc(s_w.h, 16)
            s_w.inc(16)
            g.dma_start(out=sb["gatew"][:].rearrange("p (k m) -> p k m", k=KI),
                        in_=dr["gate"].rearrange("k p m -> p k m")).then_inc(s_w.h, 16)
            s_w.inc(16)
        @b.sync
        def _(sy):
            sy.dma_start(out=sb["onesf"][:], in_=dr["c_ones"][:, :]).then_inc(s_a.h, 16)
            sy.dma_start(out=sb["onesrowf"][:], in_=dr["c_onesrow"][:, :]).then_inc(s_a.h, 16)
            sy.dma_start(out=sb["rolet"][0:1, :], in_=dr["rolei"][:]).then_inc(s_a.h, 16)
            sy.dma_start(out=sb["bg_sb"][:], in_=dr["bg16"][:, :]).then_inc(s_a.h, 16)
            sy.dma_start(out=sb["sel16f"][:].rearrange("p (j m) -> p j m", j=16),
                         in_=dr["c_sel16"].rearrange("j p m -> p j m")).then_inc(s_a.h, 16)
            s_a.inc(80)
    with nc.Block() as b:
        for eng in ("tensor", "vector", "scalar", "gpsimd", "sync"):
            def mk():
                def f(h):
                    h.wait_ge(s_w.h, s_w.val)
                    h.wait_ge(s_a.h, s_a.val)
                return f
            getattr(b, eng)(mk())
        @b.vector
        def _(v):
            V = SelfSync(v, s_v)
            V.tensor_scalar(out=sb["negident"][:], in0=sb["ident"][:], scalar1=-1.0,
                            scalar2=None, op0=ALU.mult)

    fstack_xn = ExitStack()
    fstack_h = ExitStack()
    sbuf("xn", [128, KI * T], BF16, stack=fstack_xn)
    sbuf("hbuf", [128, NHS * T], BF16, stack=fstack_h)

    emit_ln(ctx)
    emit_router(ctx)
    emit_ffn(ctx)

    if stage >= 2:
        emit_attention(ctx, fstack_xn, fstack_h)
        if stage >= 3:
            fstack2 = ExitStack()
            sb["xn"] = fstack2.enter_context(nc.sbuf_tensor("xn2", [128, KI * T], BF16))
            sb["hbuf"] = fstack2.enter_context(nc.sbuf_tensor("hbuf2", [128, NHS * T], BF16))
            emit_ln(ctx)
            if stage >= 3.7:
                emit_router(ctx)
            if stage >= 3.9:
                emit_ffn(ctx)
            fstack2.close()
    else:
        fstack_h.close()
        fstack_xn.close()

    with nc.Block() as b:
        @b.sync
        def _(sy):
            sy.wait_ge(s_v.h, s_v.val)
            sy.wait_ge(s_s.h, s_s.val)
            sy.wait_ge(s_mm.h, s_mm.val)
            sy.dma_start(out=dr["y"].rearrange("k p t -> p k t"),
                         in_=sb["x"][:].rearrange("p (k t) -> p k t", k=KI)).then_inc(s_a.h, 16)
            s_a.inc(16)
            sy.wait_ge(s_a.h, s_a.val)
    es.close()
    return nc


def emit_ln(ctx):
    nc = ctx["nc"]; sb = ctx["sb"]; ps = ctx["ps"]
    s_mm, s_v, s_s = ctx["s_mm"], ctx["s_v"], ctx["s_s"]
    x, xn = sb["x"], sb["xn"]
    rows, tmpf = sb["rows"], sb["tmpf"]
    cast = [sb["sc1"], sb["sc2"]]
    sq = [sb["sc3"], sb["sc4"]]
    vb0 = s_v.val
    m_entry = s_mm.val
    sval = {}
    mval = lambda ki: m_entry + ki + 1
    with nc.Block() as b:
        @b.scalar
        def _(s):
            S = SelfSync(s, s_s)
            s.wait_ge(s_v.h, vb0)
            s.wait_ge(s_mm.h, m_entry)
            for ki in range(KI):
                if ki >= 2:
                    s.wait_ge(s_mm.h, mval(ki - 2))
                S.activation(out=cast[ki % 2][:], in_=x[:, ki * T:(ki + 1) * T], func=AF.Copy)
                S.activation(out=sq[ki % 2][:], in_=x[:, ki * T:(ki + 1) * T], func=AF.Square)
                sval[ki] = s_s.val
        @b.tensor
        def _(t):
            for ki in range(KI):
                t.wait_ge(s_s.h, sval[ki])
                ins = None
                for p in range(NP):
                    t.matmul(ps[4 + p][0:1, :], sb["ones"][:, :],
                             cast[ki % 2][:, p * PW:(p + 1) * PW],
                             start=(ki == 0), stop=(ki == KI - 1))
                    ins = t.matmul(ps[6 + p][0:1, :], sb["ones"][:, :],
                                   sq[ki % 2][:, p * PW:(p + 1) * PW],
                                   start=(ki == 0), stop=(ki == KI - 1))
                ins.then_inc(s_mm.h, 1)
                s_mm.inc()
                assert s_mm.val == mval(ki)
    mstat = s_mm.val
    vrec = {}
    srec = {}
    with nc.Block() as b:
        @b.vector
        def _(v):
            V = SelfSync(v, s_v)
            v.wait_ge(s_mm.h, mstat)
            for p in range(NP):
                V.tensor_scalar(out=rows[0:1, p * PW:(p + 1) * PW], in0=ps[4 + p][0:1, :],
                                scalar1=1.0 / DM, scalar2=None, op0=ALU.mult)
                V.tensor_scalar(out=rows[0:1, 2 * T + p * PW:2 * T + (p + 1) * PW],
                                in0=ps[6 + p][0:1, :], scalar1=1.0 / DM, scalar2=None,
                                op0=ALU.mult)
            V.tensor_tensor(out=rows[0:1, 3 * T:4 * T], in0=rows[0:1, 0:T], in1=rows[0:1, 0:T],
                            op=ALU.mult)
            V.tensor_tensor(out=rows[0:1, 2 * T:3 * T], in0=rows[0:1, 2 * T:3 * T],
                            in1=rows[0:1, 3 * T:4 * T], op=ALU.subtract)
            V.tensor_scalar(out=rows[0:1, 2 * T:3 * T], in0=rows[0:1, 2 * T:3 * T],
                            scalar1=EPS, scalar2=None, op0=ALU.add)
            vrec[0] = s_v.val
        @b.scalar
        def _(s):
            S = SelfSync(s, s_s)
            s.wait_ge(s_v.h, vrec[0])
            S.activation(out=rows[0:1, 3 * T:4 * T], in_=rows[0:1, 2 * T:3 * T], func=AF.Sqrt)
            srec[0] = s_s.val
        @b.vector
        def _(v):
            V = SelfSync(v, s_v)
            v.wait_ge(s_s.h, srec[0])
            V.reciprocal(out=rows[0:1, T:2 * T], in_=rows[0:1, 3 * T:4 * T])
            vrec[1] = s_v.val
    with nc.Block() as b:
        mrec = {}
        @b.tensor
        def _(t):
            t.wait_ge(s_v.h, vrec[1])
            ins = None
            for p in range(NP):
                t.matmul(ps[4 + p][:, :], sb["onesrowf"][:, :], rows[0:1, p * PW:(p + 1) * PW],
                         start=True, stop=True)
                ins = t.matmul(ps[6 + p][:, :], sb["onesrowf"][:, :],
                               rows[0:1, T + p * PW:T + (p + 1) * PW],
                               start=True, stop=True)
            ins.then_inc(s_mm.h, 1)
            s_mm.inc()
            mrec[0] = s_mm.val
        @b.vector
        def _(v):
            V = SelfSync(v, s_v)
            v.wait_ge(s_mm.h, mrec[0])
            v.wait_ge(s_mm.h, m_entry)
            for ki in range(KI):
                for p in range(NP):
                    V.tensor_tensor(out=tmpf[:, p * PW:(p + 1) * PW],
                                    in0=x[:, ki * T + p * PW: ki * T + (p + 1) * PW],
                                    in1=ps[4 + p][:, :], op=ALU.subtract)
                    V.tensor_tensor(out=xn[:, ki * T + p * PW: ki * T + (p + 1) * PW],
                                    in0=tmpf[:, p * PW:(p + 1) * PW], in1=ps[6 + p][:, :],
                                    op=ALU.mult)
    ctx["xn_ready_v"] = s_v.val


def emit_router(ctx):
    nc = ctx["nc"]; sb = ctx["sb"]; ps = ctx["ps"]
    s_mm, s_v, s_s = ctx["s_mm"], ctx["s_v"], ctx["s_s"]
    xn = sb["xn"]
    scT, ctm, combf = sb["scT"], sb["ctm"], sb["combf"]
    r = {}
    with nc.Block() as b:
        @b.tensor
        def _(t):
            t.wait_ge(s_v.h, ctx["xn_ready_v"])
            ins = None
            for p in range(NP):
                for ki in range(KI):
                    ins = t.matmul(ps[p][0:16, :], sb["gatew"][:, ki * 16:(ki + 1) * 16],
                                   xn[:, ki * T + p * PW: ki * T + (p + 1) * PW],
                                   start=(ki == 0), stop=(ki == KI - 1))
            ins.then_inc(s_mm.h, 1)
            s_mm.inc()
            r["scmm"] = s_mm.val
        @b.scalar
        def _(s):
            S = SelfSync(s, s_s)
            s.wait_ge(s_mm.h, r["scmm"])
            for p in range(NP):
                S.copy(out=scT[0:16, p * PW:(p + 1) * PW], in_=ps[p][0:16, :])
            r["sct"] = s_s.val
        @b.vector
        def _(v):
            V = SelfSync(v, s_v)
            V.memset(ctm[:], 0.0)
            r["ctmz"] = s_v.val
        @b.tensor
        def _(t):
            t.wait_ge(s_s.h, r["sct"])
            ins = None
            for tt in range(8):
                ins = t.matmul(ps[2][:, tt * 16:(tt + 1) * 16],
                               scT[0:16, tt * 128:(tt + 1) * 128],
                               sb["ident"][0:16, 0:16], start=True, stop=True)
            ins.then_inc(s_mm.h, 1)
            s_mm.inc()
            r["tmm"] = s_mm.val
    st = sb["tmpf"]
    sc = sb["tmpf2"]
    with nc.Block() as b:
        @b.vector
        def _(v):
            V = SelfSync(v, s_v)
            v.wait_ge(s_mm.h, r["tmm"])
            V.tensor_copy(st[:, 0:128], ps[2][:, 0:128])
            for tt in range(8):
                V.reduce_max(st[:, 128 + tt:129 + tt], st[:, tt * 16:tt * 16 + 8],
                             axis=mybir.AxisListType.X)
            V.tensor_scalar(out=st[:, 136:144], in0=st[:, 128:136], scalar1=-1.0, scalar2=None,
                            op0=ALU.mult)
            for tt in range(8):
                V.tensor_scalar(out=sc[:, tt * 16:tt * 16 + 8], in0=st[:, tt * 16:tt * 16 + 8],
                                scalar1=st[:, 128 + tt:129 + tt], scalar2=None, op0=ALU.is_equal)
            for tt in range(8):
                V.scalar_tensor_tensor(out=sc[:, tt * 16:tt * 16 + 8],
                                       in0=sc[:, tt * 16:tt * 16 + 8], scalar=-1e30,
                                       in1=st[:, tt * 16:tt * 16 + 8], op0=ALU.mult, op1=ALU.add)
            for tt in range(8):
                V.reduce_max(st[:, 144 + tt:145 + tt], sc[:, tt * 16:tt * 16 + 8],
                             axis=mybir.AxisListType.X)
            r["v1"] = s_v.val
        @b.scalar
        def _(s):
            S = SelfSync(s, s_s)
            s.wait_ge(s_v.h, r["v1"])
            for tt in range(8):
                S.activation(out=sc[:, 128 + tt * 16:128 + tt * 16 + 8],
                             in_=st[:, tt * 16:tt * 16 + 8], func=AF.Exp,
                             bias=st[:, 136 + tt:137 + tt], scale=1.0)
            for tt in range(8):
                S.activation(out=st[:, 152 + tt:153 + tt], in_=st[:, 144 + tt:145 + tt],
                             func=AF.Exp, bias=st[:, 136 + tt:137 + tt], scale=1.0)
            s.wait_ge(s_v.h, r["ctmz"])
            for tt in range(8):
                S.activation(out=ctm[:, tt * 16 + 8:tt * 16 + 9],
                             in_=st[:, tt * 16 + 8:tt * 16 + 9], func=AF.Sigmoid)
            r["s1"] = s_s.val
        @b.vector
        def _(v):
            V = SelfSync(v, s_v)
            v.wait_ge(s_s.h, r["s1"])
            V.tensor_scalar(out=st[:, 160:168], in0=st[:, 152:160], scalar1=1.0, scalar2=None,
                            op0=ALU.add)
            V.reciprocal(out=st[:, 168:176], in_=st[:, 160:168])
            for tt in range(8):
                V.tensor_scalar(out=sc[:, 256 + tt * 16:256 + tt * 16 + 8],
                                in0=st[:, tt * 16:tt * 16 + 8],
                                scalar1=st[:, 144 + tt:145 + tt], scalar2=None, op0=ALU.is_ge)
            for tt in range(8):
                V.scalar_tensor_tensor(out=ctm[:, tt * 16:tt * 16 + 8],
                                       in0=sc[:, 128 + tt * 16:128 + tt * 16 + 8],
                                       scalar=st[:, 168 + tt:169 + tt],
                                       in1=sc[:, 256 + tt * 16:256 + tt * 16 + 8],
                                       op0=ALU.mult, op1=ALU.mult)
            r["v2"] = s_v.val
    with nc.Block() as b:
        @b.tensor
        def _(t):
            t.wait_ge(s_v.h, r["v2"])
            ins = None
            for tt in range(8):
                bank = ps[3] if tt < 4 else ps[2]
                ins = t.matmul(bank[0:16, (tt % 4) * 128:(tt % 4 + 1) * 128],
                               ctm[:, tt * 16:(tt + 1) * 16], sb["ident"][:, :],
                               start=True, stop=True)
            ins.then_inc(s_mm.h, 1)
            s_mm.inc()
            r["cmm"] = s_mm.val
        @b.scalar
        def _(s):
            S = SelfSync(s, s_s)
            s.wait_ge(s_mm.h, r["cmm"])
            S.copy(out=combf[0:16, 0:512], in_=ps[3][0:16, :])
            S.copy(out=combf[0:16, 512:1024], in_=ps[2][0:16, :])
    ctx["combf_ready_s"] = s_s.val


def emit_ffn(ctx):
    nc = ctx["nc"]; sb = ctx["sb"]; ps = ctx["ps"]; dr = ctx["dr"]
    s_w, s_mm, s_v, s_s = ctx["s_w"], ctx["s_mm"], ctx["s_v"], ctx["s_s"]
    xn, hbuf, bcC, x = sb["xn"], sb["hbuf"], sb["bcC"], sb["x"]
    tmpf, tmpf2, combf = sb["tmpf"], sb["tmpf2"], sb["combf"]
    gslot = [sb["wbufA"], sb["wbufB"]]
    uslot = [sb["wbufC"], sb["wbufD"]]

    down_done_mm = s_mm.val
    for e in range(NE + 1):
        shared = (e == NE)
        NH = NHS if shared else NHE
        wgd = dr["shgate"] if shared else dr["wgate"][e]
        wud = dr["shup"] if shared else dr["wup"][e]
        wdd = dr["shdown"] if shared else dr["wdown"][e]
        r = {"ventry": s_v.val}
        with nc.Block() as b:
            @b.tensor
            def _(t):
                t.wait_ge(s_s.h, ctx["combf_ready_s"])
                if e > 0:
                    t.wait_ge(s_v.h, ctx["dnv_last"])
                ins = None
                for p in range(NP):
                    ins = t.matmul(ps[p][:, :],
                                   sb["sel16"][:, min(e, 8) * 128:(min(e, 8) + 1) * 128],
                                   combf[0:16, p * PW:(p + 1) * PW], start=True, stop=True)
                ins.then_inc(s_mm.h, 1)
                s_mm.inc()
                r["bc"] = s_mm.val
            @b.scalar
            def _(s):
                S = SelfSync(s, s_s)
                s.wait_ge(s_mm.h, r["bc"])
                s.wait_ge(s_v.h, r["ventry"])
                for p in range(NP):
                    S.copy(out=bcC[:, p * PW:(p + 1) * PW], in_=ps[p][:, :])
                r["bcc"] = s_s.val

        s_sl = ctx["s_sl"]
        m0 = s_mm.val
        v0 = s_v.val
        s0 = s_s.val
        gmm = lambda si, p: m0 + si * 4 + p + 1
        umm = lambda si, p: m0 + si * 4 + 2 + p + 1
        sil = lambda si, p: s0 + si * 2 + p + 1
        hev = lambda si, p: v0 + (si * 2 + p + 1) * 2
        gtgt = {}
        utgt = {}
        for si in range(NH):
            gtgt[si] = s_sl[si % 2].inc(16)
            utgt[si] = s_sl[2 + si % 2].inc(16)
        with nc.Block() as b:
            @b.tensor
            def _(t):
                for si in range(NH):
                    t.wait_ge(s_sl[si % 2].h, gtgt[si])
                    t.wait_ge(s_sl[2 + si % 2].h, utgt[si])
                    if si >= 2:
                        t.wait_ge(s_v.h, hev(si - 2, NP - 1))
                    for p in range(NP):
                        ins = None
                        for ki in range(KI):
                            ins = t.matmul(ps[(si % 2) * 2 + p][:, :],
                                           gslot[si % 2][:, ki * 128:(ki + 1) * 128],
                                           xn[:, ki * T + p * PW: ki * T + (p + 1) * PW],
                                           start=(ki == 0), stop=(ki == KI - 1))
                        ins.then_inc(s_mm.h, 1)
                        s_mm.inc()
                        assert s_mm.val == gmm(si, p)
                    for p in range(NP):
                        ins = None
                        for ki in range(KI):
                            ins = t.matmul(ps[4 + (si % 2) * 2 + p][:, :],
                                           uslot[si % 2][:, ki * 128:(ki + 1) * 128],
                                           xn[:, ki * T + p * PW: ki * T + (p + 1) * PW],
                                           start=(ki == 0), stop=(ki == KI - 1))
                        ins.then_inc(s_mm.h, 1)
                        s_mm.inc()
                        assert s_mm.val == umm(si, p)
            @b.gpsimd
            def _(g):
                for si in range(NH):
                    if si < 2:
                        g.wait_ge(s_mm.h, down_done_mm)
                    else:
                        g.wait_ge(s_mm.h, umm(si - 2, NP - 1))
                    g.dma_start(out=gslot[si % 2][:],
                                in_=wgd[si].rearrange("p k m -> p (k m)"))\
                        .then_inc(s_sl[si % 2].h, 16)
                    g.dma_start(out=uslot[si % 2][:],
                                in_=wud[si].rearrange("p k m -> p (k m)"))\
                        .then_inc(s_sl[2 + si % 2].h, 16)
            @b.scalar
            def _(s):
                S = SelfSync(s, s_s)
                for si in range(NH):
                    for p in range(NP):
                        s.wait_ge(s_mm.h, gmm(si, p))
                        if si >= 1:
                            s.wait_ge(s_v.h, hev(si - 1, p))
                        S.activation(out=tmpf[:, p * PW:(p + 1) * PW],
                                     in_=ps[(si % 2) * 2 + p][:, :], func=AF.Silu)
                        assert s_s.val == sil(si, p)
            @b.vector
            def _(v):
                V = SelfSync(v, s_v)
                for si in range(NH):
                    for p in range(NP):
                        v.wait_ge(s_s.h, sil(si, p))
                        v.wait_ge(s_mm.h, umm(si, p))
                        if e > 0 and si == 0 and p == 0:
                            v.wait_ge(s_mm.h, down_done_mm)
                        V.tensor_tensor(out=tmpf2[:, p * PW:(p + 1) * PW],
                                        in0=tmpf[:, p * PW:(p + 1) * PW],
                                        in1=ps[4 + (si % 2) * 2 + p][:, :], op=ALU.mult)
                        V.tensor_tensor(out=hbuf[:, si * T + p * PW: si * T + (p + 1) * PW],
                                        in0=tmpf2[:, p * PW:(p + 1) * PW],
                                        in1=bcC[:, p * PW:(p + 1) * PW], op=ALU.mult)
                        assert s_v.val == hev(si, p)
        hev_last = s_v.val
        gu_done = s_mm.val

        m1 = s_mm.val
        v1 = s_v.val
        dmm = lambda c, p: m1 + c * 2 + p + 1
        dvv = lambda c, p: v1 + c * 2 + p + 1
        dtgt = {}
        for c in range(KI):
            dtgt[c] = s_sl[c % 2].inc(16)
        with nc.Block() as b:
            @b.tensor
            def _(t):
                for c in range(KI):
                    t.wait_ge(s_sl[c % 2].h, dtgt[c])
                    if c == 0:
                        t.wait_ge(s_v.h, hev_last)
                    if c >= 2:
                        t.wait_ge(s_v.h, dvv(c - 2, NP - 1))
                    for p in range(NP):
                        ins = None
                        for jj in range(NH):
                            ins = t.matmul(ps[(c % 2) * 2 + p][:, :],
                                           gslot[c % 2][:, jj * 128:(jj + 1) * 128],
                                           hbuf[:, jj * T + p * PW: jj * T + (p + 1) * PW],
                                           start=(jj == 0), stop=(jj == NH - 1))
                        ins.then_inc(s_mm.h, 1)
                        s_mm.inc()
                        assert s_mm.val == dmm(c, p)
            @b.gpsimd
            def _(g):
                for c in range(KI):
                    if c < 2:
                        g.wait_ge(s_mm.h, gu_done)
                    else:
                        g.wait_ge(s_mm.h, dmm(c - 2, NP - 1))
                    g.dma_start(out=gslot[c % 2][:, 0:NH * 128],
                                in_=wdd[c].rearrange("p k m -> p (k m)"))\
                        .then_inc(s_sl[c % 2].h, 16)
            @b.vector
            def _(v):
                V = SelfSync(v, s_v)
                for c in range(KI):
                    for p in range(NP):
                        v.wait_ge(s_mm.h, dmm(c, p))
                        V.scalar_tensor_tensor(
                            out=x[:, c * T + p * PW: c * T + (p + 1) * PW],
                            in0=ps[(c % 2) * 2 + p][:, :], scalar=0.5,
                            in1=x[:, c * T + p * PW: c * T + (p + 1) * PW],
                            op0=ALU.mult, op1=ALU.add)
                        assert s_v.val == dvv(c, p)
        down_done_mm = s_mm.val
        ctx["dnv_last"] = s_v.val






def emit_attention(ctx, fstack_xn, fstack_h):
    nc = ctx["nc"]; sb = ctx["sb"]
    sbuf = ctx["sbuf"]
    s_sl = ctx["s_sl"]

    emit_ln(ctx)

    fstack_h.close()
    astack = ExitStack()
    q = sbuf("qb", [128, H * T], BF16, stack=astack, side="right")

    emit_proj(ctx)

    fstack_xn.close()
    attnv = sbuf("attnv", [128, H * PW], BF16, stack=astack, side="right")
    kst = sbuf("kst", [128, 8 * 128], BF16, stack=astack, side="right")
    vst = sbuf("vst", [128, 8 * 128], BF16, stack=astack, side="right")
    ptb = sbuf("ptb", [128, 4 * PW], BF16, stack=astack, side="right")
    negC = sbuf("negC", [128, 16], F32, stack=astack, side="right")
    dS = sbuf("dS", [128, HKV * DH], BF16, stack=astack, side="right")
    e1a = sbuf("e1a", [128, HKV * 128], BF16, stack=astack, side="right")
    e2a = sbuf("e2a", [128, HKV * 128], BF16, stack=astack, side="right")
    dsc = sbuf("dsc", [128, 26 * 128], BF16, stack=astack, side="right")
    rbc = sbuf("rbc", [128, PW], BF16, stack=astack, side="right")

    emit_barrier_and_bounds(ctx, negC)

    s_mm, s_v, s_s, s_a, s_w = ctx["s_mm"], ctx["s_v"], ctx["s_s"], ctx["s_a"], ctx["s_w"]
    sems_all = [s_mm, s_v, s_s, s_a, s_w] + s_sl
    engs = [nc.tensor, nc.vector, nc.scalar, nc.gpsimd, nc.sync]
    enames = ("tensor", "vector", "scalar", "gpsimd", "sync")
    regs = {}
    with nc.Block() as b:
        for name in enames:
            def mk(nm):
                def f(h):
                    r = h.alloc_register(f"role_{nm}")
                    h.reg_load(r, sb["rolet"][0:1, 0:1])
                    regs[nm] = r
                return f
            getattr(b, name)(mk(name))
    idxs = [eng.compute_val(regs[nm]) for nm, eng in zip(enames, engs)]

    snap = [sm.val for sm in sems_all]
    snap_keys = set(ctx.keys())
    ends = []
    for role in nc.Switch(engines=engs, index=idxs, n=2):
        if role == 1:
            for sm, vv in zip(sems_all, snap):
                sm.val = vv
            for k in list(ctx.keys()):
                if k not in snap_keys:
                    del ctx[k]
        emit_flash_delta_wo(ctx, role, q, attnv, kst, vst, ptb, negC, dS, e1a, e2a, dsc, rbc)
        ends.append([sm.val for sm in sems_all])
    ceil = [max(a, c) for a, c in zip(ends[0], ends[1])]
    own = [nc.tensor, nc.vector, nc.scalar, nc.sync, nc.gpsimd,
           nc.gpsimd, nc.gpsimd, nc.gpsimd, nc.gpsimd,
           nc.sync, nc.sync, nc.sync, nc.sync, nc.sync, nc.sync, nc.sync, nc.sync]
    for role in nc.Switch(engines=engs, index=idxs, n=2):
        for i, sm in enumerate(sems_all):
            d = ceil[i] - ends[role][i]
            if d > 0:
                own[i].wait_ge(sm.h, ends[role][i])
                own[i].sem_inc(sm.h, d)
    for sm, vv in zip(sems_all, ceil):
        sm.val = vv
    astack.close()
    ctx["dnv_last"] = s_v.val


_HALF_N = [0]


def _half_ix(nc, sb, tag):
    rr = {}
    col = _HALF_N[0] % 8
    _HALF_N[0] += 1
    with nc.Block() as b:
        @b.sync
        def _(sy):
            r = sy.alloc_register(f"half_{tag}{_HALF_N[0]}")
            sy.reg_load(r, sb["rolet"][0:1, col:col + 1])
            rr[0] = r
    return nc.sync.compute_val(rr[0])


def _sync_switch(ctx, col, emit_body):
    nc = ctx["nc"]; sb = ctx["sb"]
    rr = {}
    with nc.Block() as b:
        @b.sync
        def _(sy):
            r = sy.alloc_register(f"swr{col}_{_HALF_N[0]}")
            _HALF_N[0] += 1
            sy.reg_load(r, sb["rolet"][0:1, col:col + 1])
            rr[0] = r
    idx = nc.sync.compute_val(rr[0])
    sems = [ctx["s_a"]] + ctx["s_sl"]
    snap = [sm.val for sm in sems]
    ends = []
    for half in nc.Switch(engines=[nc.sync], index=[idx], n=2):
        if half == 1:
            for sm, vv in zip(sems, snap):
                sm.val = vv
        emit_body(half, nc.sync)
        ends.append([sm.val for sm in sems])
    assert ends[0] == ends[1], (ends[0], ends[1])


def emit_proj(ctx):
    nc = ctx["nc"]; sb = ctx["sb"]; ps = ctx["ps"]; dr = ctx["dr"]
    s_a, s_mm, s_v, s_s = ctx["s_a"], ctx["s_mm"], ctx["s_v"], ctx["s_s"]
    s_sl = ctx["s_sl"]
    xn, q, rows = sb["xn"], sb["qb"], sb["rows"]
    wslot = [sb["wbufA"], sb["wbufB"]]
    kstage = [sb["sc1"], sb["sc2"]]
    vstage = sb["bcC"]
    sqs = sb["sc3"]
    sq4 = sb["sc4"]

    with nc.Block() as b:
        @b.vector
        def _(v):
            V = SelfSync(v, s_v)
            V.memset(rows[0:1, T:T + 40], 0.0)

    dt = {}
    for si in range(KI):
        dt[si] = s_sl[si % 2].inc(16)
    m0 = s_mm.val
    pm = lambda si, p: m0 + si * 2 + p + 1
    s0 = s_s.val
    ev = lambda si, p: s0 + si * 2 + p + 1
    with nc.Block() as b:
        @b.gpsimd
        def _(g):
            for si in range(KI):
                if si >= 2:
                    g.wait_ge(s_mm.h, pm(si - 2, NP - 1))
                g.dma_start(out=wslot[si % 2][:],
                            in_=dr["wq"][si].rearrange("p k m -> p (k m)"))\
                    .then_inc(s_sl[si % 2].h, 16)
        @b.tensor
        def _(t):
            t.wait_ge(s_v.h, ctx["xn_ready_v"])
            for si in range(KI):
                t.wait_ge(s_sl[si % 2].h, dt[si])
                if si >= 2:
                    t.wait_ge(s_s.h, ev(si - 2, NP - 1))
                for p in range(NP):
                    ins = None
                    for ki in range(KI):
                        ins = t.matmul(ps[(si % 2) * 2 + p][:, :],
                                       wslot[si % 2][:, ki * 128:(ki + 1) * 128],
                                       xn[:, ki * T + p * PW: ki * T + (p + 1) * PW],
                                       start=(ki == 0), stop=(ki == KI - 1))
                    ins.then_inc(s_mm.h, 1)
                    s_mm.inc()
        @b.scalar
        def _(s):
            S = SelfSync(s, s_s)
            for si in range(KI):
                for p in range(NP):
                    s.wait_ge(s_mm.h, pm(si, p))
                    S.copy(out=q[:, si * T + p * PW: si * T + (p + 1) * PW],
                           in_=ps[(si % 2) * 2 + p][:, :])
                    assert s_s.val == ev(si, p)

    m1 = s_mm.val
    s1 = s_s.val
    v1 = s_v.val
    qm = lambda si, p: m1 + si * 2 + p + 1
    qe = lambda si, p: s1 + si * 4 + p * 2 + 2
    qv = lambda si, p: v1 + si * 4 + p * 2 + 2
    with nc.Block() as b:
        @b.scalar
        def _(s):
            S = SelfSync(s, s_s)
            for si in range(KI):
                for p in range(NP):
                    if (si, p) != (0, 0):
                        s.wait_ge(s_mm.h, qm(si, p - 1) if p > 0 else qm(si - 1, NP - 1))
                    S.activation(out=sqs[:, p * PW:(p + 1) * PW],
                                 in_=q[:, si * T + p * PW: si * T + (p + 1) * PW], func=AF.Exp)
                    S.activation(out=sq4[:, p * PW:(p + 1) * PW],
                                 in_=q[:, si * T + p * PW: si * T + (p + 1) * PW],
                                 func=AF.Square)
                    assert s_s.val == qe(si, p)
        @b.tensor
        def _(t):
            for si in range(KI):
                for p in range(NP):
                    t.wait_ge(s_s.h, qe(si, p))
                    if (si, p) != (0, 0):
                        t.wait_ge(s_v.h, qv(si, p - 1) if p > 0 else qv(si - 1, NP - 1))
                    t.matmul((ps[6] if p == 0 else ps[5])[0:16, 0:PW],
                             sb["colsel"][:, si * 16:(si + 1) * 16],
                             sqs[:, p * PW:(p + 1) * PW], start=True, stop=True)
                    ins = t.matmul(ps[7][0:1, 0:PW], sb["ones"][:, :],
                                   sq4[:, p * PW:(p + 1) * PW], start=True, stop=True)
                    ins.then_inc(s_mm.h, 1)
                    s_mm.inc()
                    assert s_mm.val == qm(si, p)
        @b.vector
        def _(v):
            V = SelfSync(v, s_v)
            for si in range(KI):
                for p in range(NP):
                    v.wait_ge(s_mm.h, qm(si, p))
                    V.reduce_max(rows[0:1, 2 * T:2 * T + 1], ps[7][0:1, 0:PW],
                                 axis=mybir.AxisListType.X)
                    V.tensor_tensor(out=rows[0:1, T + si:T + si + 1],
                                    in0=rows[0:1, T + si:T + si + 1],
                                    in1=rows[0:1, 2 * T:2 * T + 1], op=ALU.max)
                    assert s_v.val == qv(si, p)
            for p in range(NP):
                V.tensor_copy(rows[0:16, p * PW:(p + 1) * PW],
                              (ps[6] if p == 0 else ps[5])[0:16, 0:PW])
    ctx["qcs_done_v"] = s_v.val

    dt = {}
    for si in range(HKV):
        dt[si] = s_sl[si % 2].inc(16)
    m2 = s_mm.val
    s2 = s_s.val
    v2 = s_v.val
    km = lambda si, p: m2 + si * 4 + p + 1
    knm = lambda si, p: m2 + si * 4 + 2 + p + 1
    kev = lambda si, p: s2 + si * 3 + p + 1
    ksq = lambda si: s2 + si * 3 + 3
    knv = lambda si, p: v2 + si * 4 + (p + 1) * 2
    w4base, w5base = s_sl[4].val, s_sl[5].val
    wrt = lambda si, p: (w4base if si % 2 == 0 else w5base) + 32 * (si // 2 + 1)
    with nc.Block() as b:
        @b.gpsimd
        def _(g):
            for si in range(HKV):
                if si >= 2:
                    g.wait_ge(s_mm.h, km(si - 2, NP - 1))
                g.dma_start(out=wslot[si % 2][:],
                            in_=dr["wk"][si].rearrange("p k m -> p (k m)"))\
                    .then_inc(s_sl[si % 2].h, 16)
        @b.tensor
        def _(t):
            for si in range(HKV):
                t.wait_ge(s_sl[si % 2].h, dt[si])
                if si >= 2:
                    t.wait_ge(s_s.h, kev(si - 2, NP - 1))
                for p in range(NP):
                    ins = None
                    for ki in range(KI):
                        ins = t.matmul(ps[(si % 2) * 2 + p][:, :],
                                       wslot[si % 2][:, ki * 128:(ki + 1) * 128],
                                       xn[:, ki * T + p * PW: ki * T + (p + 1) * PW],
                                       start=(ki == 0), stop=(ki == KI - 1))
                    ins.then_inc(s_mm.h, 1)
                    s_mm.inc()
                    assert s_mm.val == km(si, p)
                t.wait_ge(s_s.h, ksq(si))
                for p in range(NP):
                    if si > 0 or p > 0:
                        t.wait_ge(s_v.h, knv(si, p - 1) if p > 0 else knv(si - 1, NP - 1))
                    t.matmul(ps[7][0:1, 0:PW], sb["ones"][:, :],
                             sq4[:, p * PW:(p + 1) * PW],
                             start=True, stop=True).then_inc(s_mm.h, 1)
                    s_mm.inc()
                    assert s_mm.val == knm(si, p)
        @b.scalar
        def _(s):
            S = SelfSync(s, s_s)
            for si in range(HKV):
                for p in range(NP):
                    s.wait_ge(s_mm.h, km(si, p))
                    if si >= 2:
                        s.wait_ge(s_sl[4 + si % 2].h, wrt(si - 2, p))
                    S.copy(out=kstage[si % 2][:, p * PW:(p + 1) * PW],
                           in_=ps[(si % 2) * 2 + p][:, :])
                    assert s_s.val == kev(si, p)
                S.activation(out=sq4[:, 0:T], in_=kstage[si % 2][:], func=AF.Square)
                assert s_s.val == ksq(si)
        @b.vector
        def _(v):
            V = SelfSync(v, s_v)
            for si in range(HKV):
                for p in range(NP):
                    v.wait_ge(s_mm.h, knm(si, p))
                    V.reduce_max(rows[0:1, 2 * T:2 * T + 1], ps[7][0:1, 0:PW],
                                 axis=mybir.AxisListType.X)
                    V.tensor_tensor(out=rows[0:1, T + 16 + si:T + 17 + si],
                                    in0=rows[0:1, T + 16 + si:T + 17 + si],
                                    in1=rows[0:1, 2 * T:2 * T + 1], op=ALU.max)
                    assert s_v.val == knv(si, p)
        @b.sync
        def _(sy):
            for si in range(HKV):
                for p in range(NP):
                    sy.wait_ge(s_s.h, kev(si, p))
                    sy.dma_start(
                        out=dr["ksl"][si, p * 4:(p + 1) * 4, :, :]
                            .rearrange("c d m -> d c m"),
                        in_=kstage[si % 2][:, p * PW:(p + 1) * PW]
                            .rearrange("d (c m) -> d c m", c=4))\
                        .then_inc(s_sl[4 + si % 2].h, 16)
                    s_sl[4 + si % 2].inc(16)
    kmax_done_v = s_v.val

    dt = {}
    for si in range(HKV):
        dt[si] = s_sl[si % 2].inc(16)
    m3 = s_mm.val
    s3 = s_s.val
    vm = lambda si, p: m3 + si * 4 + p * 2 + 1
    vtm = lambda si, p: m3 + si * 4 + p * 2 + 2
    vev = lambda si, p: s3 + si * 4 + p * 2 + 1
    vte = lambda si, p: s3 + si * 4 + p * 2 + 2
    wrt = {}
    for si in range(HKV):
        for p in range(NP):
            wrt[(si, p)] = s_sl[6].inc(64)
    with nc.Block() as b:
        @b.gpsimd
        def _(g):
            for si in range(HKV):
                if si >= 2:
                    g.wait_ge(s_mm.h, vm(si - 2, NP - 1))
                g.dma_start(out=wslot[si % 2][:],
                            in_=dr["wv"][si].rearrange("p k m -> p (k m)"))\
                    .then_inc(s_sl[si % 2].h, 16)
        @b.tensor
        def _(t):
            for si in range(HKV):
                t.wait_ge(s_sl[si % 2].h, dt[si])
                for p in range(NP):
                    if si >= 1:
                        t.wait_ge(s_s.h, vev(si - 1, p))
                    ins = None
                    for ki in range(KI):
                        ins = t.matmul(ps[4 + p][:, :],
                                       wslot[si % 2][:, ki * 128:(ki + 1) * 128],
                                       xn[:, ki * T + p * PW: ki * T + (p + 1) * PW],
                                       start=(ki == 0), stop=(ki == KI - 1))
                    ins.then_inc(s_mm.h, 1)
                    s_mm.inc()
                    assert s_mm.val == vm(si, p)
                    t.wait_ge(s_s.h, vev(si, p))
                    ins = None
                    for c in range(4):
                        ins = t.matmul(ps[6][:, c * 128:(c + 1) * 128],
                                       vstage[:, p * PW + c * 128: p * PW + (c + 1) * 128],
                                       sb["ident"][:, :], start=True, stop=True)
                    ins.then_inc(s_mm.h, 1)
                    s_mm.inc()
                    assert s_mm.val == vtm(si, p)
        @b.scalar
        def _(s):
            S = SelfSync(s, s_s)
            for si in range(HKV):
                for p in range(NP):
                    s.wait_ge(s_mm.h, vm(si, p))
                    S.copy(out=vstage[:, p * PW:(p + 1) * PW], in_=ps[4 + p][:, :])
                    assert s_s.val == vev(si, p)
                    s.wait_ge(s_mm.h, vtm(si, p))
                    if (si, p) != (0, 0):
                        s.wait_ge(s_sl[6].h, wrt[(si, p - 1) if p > 0 else (si - 1, NP - 1)])
                    S.copy(out=sqs[:, 0:PW], in_=ps[6][:, 0:PW])
                    assert s_s.val == vte(si, p)
        @b.sync
        def _(sy):
            for si in range(HKV):
                for p in range(NP):
                    sy.wait_ge(s_s.h, vte(si, p))
                    for cc in range(4):
                        sy.dma_start(
                            out=dr["vsl"][p * 4 + cc, :, si * 128:(si + 1) * 128],
                            in_=sqs[:, cc * 128:(cc + 1) * 128])\
                            .then_inc(s_sl[6].h, 16)
    kslv, vslv = s_sl[4].val, s_sl[6].val

    def _shwrite(half, sy):
        sy.wait_ge(s_sl[4].h, kslv)
        sy.wait_ge(s_sl[5].h, s_sl[5].val)
        sy.wait_ge(s_sl[6].h, vslv)
        sy.wait_ge(s_v.h, kmax_done_v)
        sy.dma_start(out=dr["ksh"][half], in_=dr["ksl"][:]).then_inc(s_a.h, 16)
        sy.dma_start(out=dr["vsh"][half], in_=dr["vsl"][:]).then_inc(s_a.h, 16)
        sy.dma_start(out=dr["rsh"][half, 0:8], in_=rows[0:1, T + 16:T + 24])\
            .then_inc(s_a.h, 16)
        s_a.inc(48)
    _sync_switch(ctx, 3, _shwrite)
    ctx["proj_write_targets"] = (s_sl[4].val, s_sl[5].val, s_sl[6].val, s_a.val)


def emit_barrier_and_bounds(ctx, negC):
    nc = ctx["nc"]; sb = ctx["sb"]; ps = ctx["ps"]; dr = ctx["dr"]
    s_a, s_mm, s_v, s_s, s_w = ctx["s_a"], ctx["s_mm"], ctx["s_v"], ctx["s_s"], ctx["s_w"]
    rows = sb["rows"]
    t4, t5, t6, ta = ctx["proj_write_targets"]
    with nc.Block() as b:
        @b.gpsimd
        def _(g):
            g.wait_ge(ctx["s_sl"][4].h, t4)
            g.wait_ge(ctx["s_sl"][5].h, t5)
            g.wait_ge(ctx["s_sl"][6].h, t6)
            g.wait_ge(s_a.h, ta)
    nc.all_core_barrier()
    s_bar = ctx["s_bar"]
    with nc.Block() as b:
        @b.gpsimd
        def _(g):
            g.sem_inc(s_bar.h, 1)
            s_bar.inc(1)
    bar = s_bar.val
    r = {}
    with nc.Block() as b:
        @b.sync
        def _(sy):
            sy.wait_ge(s_bar.h, bar)
            sy.dma_start(out=rows[0:1, T + 24:T + 32], in_=dr["rsh"][0, 0:8])\
                .then_inc(s_a.h, 16)
            sy.dma_start(out=rows[0:1, T + 32:T + 40], in_=dr["rsh"][1, 0:8])\
                .then_inc(s_a.h, 16)
            s_a.inc(32)
            r["a"] = s_a.val
        @b.vector
        def _(v):
            V = SelfSync(v, s_v)
            v.wait_ge(s_a.h, r["a"])
            V.tensor_tensor(out=rows[0:1, T + 24:T + 32], in0=rows[0:1, T + 24:T + 32],
                            in1=rows[0:1, T + 32:T + 40], op=ALU.max)
            r["v"] = s_v.val
        @b.scalar
        def _(s):
            S = SelfSync(s, s_s)
            s.wait_ge(s_v.h, r["v"])
            S.activation(out=rows[0:1, T + 64:T + 72], in_=rows[0:1, T + 24:T + 32],
                         func=AF.Sqrt)
            S.activation(out=rows[0:1, T + 80:T + 96], in_=rows[0:1, T:T + 16], func=AF.Sqrt)
            r["s"] = s_s.val
        @b.vector
        def _(v):
            V = SelfSync(v, s_v)
            v.wait_ge(s_s.h, r["s"])
            for par in range(2):
                V.tensor_tensor(out=rows[0:1, T + 48 + par:T + 64:2],
                                in0=rows[0:1, T + 80 + par:T + 96:2],
                                in1=rows[0:1, T + 64:T + 72], op=ALU.mult)
            V.tensor_scalar(out=rows[0:1, T + 48:T + 64], in0=rows[0:1, T + 48:T + 64],
                            scalar1=-SCALE, scalar2=None, op0=ALU.mult)
            r["v2"] = s_v.val
    r2 = {}
    with nc.Block() as b:
        @b.tensor
        def _(t):
            t.wait_ge(s_v.h, r["v2"])
            t.matmul(ps[5][:, 0:16], sb["onesrowf"][:, :], rows[0:1, T + 48:T + 64],
                     start=True, stop=True).then_inc(s_mm.h, 1)
            s_mm.inc()
            r2["m"] = s_mm.val
        @b.scalar
        def _(s):
            S = SelfSync(s, s_s)
            s.wait_ge(s_mm.h, r2["m"])
            S.copy(out=negC[:], in_=ps[5][:, 0:16])
    ctx["negc_done_s"] = s_s.val


SKIP_DELTA = [False]
SKIP_FLASH = [False]


def emit_flash_delta_wo(ctx, role, q, attnv, kst, vst, ptb, negC, dS, e1a, e2a, dsc, rbc):
    nc = ctx["nc"]; sb = ctx["sb"]; ps = ctx["ps"]; dr = ctx["dr"]
    s_mm, s_v, s_s = ctx["s_mm"], ctx["s_v"], ctx["s_s"]
    s_sl = ctx["s_sl"]
    x, rows = sb["x"], sb["rows"]
    t = nc.tensor; v = nc.vector; sc = nc.scalar; sy = nc.sync; gp = nc.gpsimd
    V = SelfSync(v, s_v)
    S = SelfSync(sc, s_s)

    t.wait_ge(s_s.h, ctx["negc_done_s"])

    prev_l_evac_s = 0
    prev_av_evac = {}
    wo_done_mm = 0
    wo_done_v = 0
    for p in range(NP if not SKIP_FLASH[0] else 0):
        qlo = role * T + p * PW
        kts = list(range(0, (qlo + PW - 1) // 128 + 1))
        n_kt = len(kts)
        for hp in range(8):
            qh0, qh1 = 2 * hp, 2 * hp + 1
            m_f = s_mm.val
            S_mm = lambda i, qhl: m_f + 6 * i + 3 * qhl + 1
            PV_mm = lambda i, qhl: m_f + 6 * i + 3 * qhl + 2
            L_mm = lambda i, qhl: m_f + 6 * i + 3 * qhl + 3
            KSEM = (4, 5, 8, 9)
            VSEM = (6, 7, 10, 11)
            kd = {}
            vd = {}
            for i, kt in enumerate(kts):
                kd[i] = s_sl[KSEM[i % 4]].inc(16)
                vd[i] = s_sl[VSEM[i % 4]].inc(16)
            for i, kt in enumerate(kts):
                slot = i % 4
                if i >= 4:
                    sy.wait_ge(s_mm.h, S_mm(i - 4, 1))
                    sy.wait_ge(s_mm.h, PV_mm(i - 4, 1))
                else:
                    sy.wait_ge(s_mm.h, m_f)
                sy.dma_start(out=kst[:, slot * 256:slot * 256 + 128],
                             in_=dr["ksh"][kt // NCH, hp, kt % NCH, :, :])\
                    .then_inc(s_sl[KSEM[i % 4]].h, 16)
                sy.dma_start(out=vst[:, slot * 256:slot * 256 + 128],
                             in_=dr["vsh"][kt // NCH, kt % NCH, :, hp * 128:(hp + 1) * 128])\
                    .then_inc(s_sl[VSEM[i % 4]].h, 16)
            exp_done = {}
            for i, kt in enumerate(kts):
                slot = i % 4
                diag = kt * 128 >= qlo
                o = kt - qlo // 128 if diag else 0
                for qhl, qh in ((0, qh0), (1, qh1)):
                    t.wait_ge(s_sl[KSEM[i % 4]].h, kd[i])
                    if i >= 1:
                        t.wait_ge(s_s.h, exp_done.get((i - 1, qhl, "s"), 0))
                        if (i - 1, qhl, "v") in exp_done:
                            t.wait_ge(s_v.h, exp_done[(i - 1, qhl, "v")])
                    if i == 0 and hp == 0 and p == 1:
                        t.wait_ge(s_v.h, wo_done_v)
                    t.matmul(ps[qhl][:, :], kst[:, slot * 256:slot * 256 + 128],
                             q[:, qh * T + p * PW: qh * T + (p + 1) * PW],
                             start=True, stop=True).then_inc(s_mm.h, 1)
                    s_mm.inc()
                    assert s_mm.val == S_mm(i, qhl)
                    ptslot = (i % 2) * 2 + qhl
                    sc.wait_ge(s_mm.h, S_mm(i, qhl))
                    if i >= 2:
                        sc.wait_ge(s_mm.h, L_mm(i - 2, qhl))
                    S.activation(out=ptb[:, ptslot * PW:(ptslot + 1) * PW],
                                 in_=ps[qhl][:, :], func=AF.Exp,
                                 bias=negC[:, qh:qh + 1], scale=SCALE)
                    exp_done[(i, qhl, "s")] = s_s.val
                    if diag:
                        v.wait_ge(s_s.h, s_s.val)
                        V.tensor_tensor(out=ptb[:, ptslot * PW:(ptslot + 1) * PW],
                                        in0=ptb[:, ptslot * PW:(ptslot + 1) * PW],
                                        in1=sb["fmask"][:, o * PW:(o + 1) * PW], op=ALU.mult)
                        exp_done[(i, qhl, "v")] = s_v.val
                        t.wait_ge(s_v.h, s_v.val)
                    else:
                        t.wait_ge(s_s.h, s_s.val)
                    t.wait_ge(s_sl[VSEM[i % 4]].h, vd[i])
                    if i == 0:
                        if (qhl,) in prev_av_evac:
                            t.wait_ge(s_v.h, prev_av_evac[(qhl,)])
                        if qhl == 0 and prev_l_evac_s:
                            t.wait_ge(s_s.h, prev_l_evac_s)
                    t.matmul(ps[2 + qhl][:, :], vst[:, slot * 256:slot * 256 + 128],
                             ptb[:, ptslot * PW:(ptslot + 1) * PW],
                             start=(i == 0), stop=(i == n_kt - 1)).then_inc(s_mm.h, 1)
                    s_mm.inc()
                    assert s_mm.val == PV_mm(i, qhl)
                    t.matmul(ps[6][0:4, 0:PW], sb["lcol"][:, qhl * 4:(qhl + 1) * 4],
                             ptb[:, ptslot * PW:(ptslot + 1) * PW],
                             start=(i == 0 and qhl == 0),
                             stop=(i == n_kt - 1 and qhl == 1)).then_inc(s_mm.h, 1)
                    s_mm.inc()
                    assert s_mm.val == L_mm(i, qhl)
            lmm = s_mm.val
            sc.wait_ge(s_mm.h, lmm)
            S.copy(out=rows[0:2, 2 * T:2 * T + PW], in_=ps[6][0:2, 0:PW])
            prev_l_evac_s = s_s.val
            v.wait_ge(s_s.h, s_s.val)
            V.reciprocal(out=rows[0:2, 2 * T + PW:2 * T + 2 * PW],
                         in_=rows[0:2, 2 * T:2 * T + PW])
            sc.wait_ge(s_v.h, s_v.val)
            S.copy(out=sb["sc4"][0:2, 0:PW], in_=rows[0:2, 2 * T + PW:2 * T + 2 * PW])
            lrec_s = s_s.val
            for qhl, qh in ((0, qh0), (1, qh1)):
                t.wait_ge(s_s.h, lrec_s)
                t.matmul(ps[4 + qhl][:, :], sb["sel16"][0:2, qhl * 128:(qhl + 1) * 128],
                         sb["sc4"][0:2, 0:PW], start=True, stop=True).then_inc(s_mm.h, 1)
                s_mm.inc()
                sc.wait_ge(s_mm.h, s_mm.val)
                sc.wait_ge(s_v.h, s_v.val)
                S.copy(out=rbc[:, :], in_=ps[4 + qhl][:, :])
                v.wait_ge(s_s.h, s_s.val)
                v.wait_ge(s_mm.h, PV_mm(n_kt - 1, qhl))
                if p > 0:
                    v.wait_ge(s_mm.h, wo_done_mm)
                V.tensor_tensor(out=attnv[:, qh * PW:(qh + 1) * PW],
                                in0=ps[2 + qhl][:, :], in1=rbc[:, :], op=ALU.mult)
                prev_av_evac[(qhl,)] = s_v.val
        av_last = s_v.val
        wdt = {}
        for c in range(KI):
            wdt[c] = s_sl[c % 2].inc(16)
        m0 = s_mm.val
        wm = lambda c: m0 + c + 1
        v0w = s_v.val
        for c in range(KI):
            if c < 2:
                gp.wait_ge(s_mm.h, m0)
            else:
                gp.wait_ge(s_mm.h, wm(c - 2))
            gp.dma_start(out=sb["wbufA" if c % 2 == 0 else "wbufB"][:],
                         in_=dr["wo"][c].rearrange("p k m -> p (k m)"))\
                .then_inc(s_sl[c % 2].h, 16)
        for c in range(KI):
            t.wait_ge(s_sl[c % 2].h, wdt[c])
            if c == 0:
                t.wait_ge(s_v.h, av_last)
            if c >= 2:
                t.wait_ge(s_v.h, v0w + c - 1)
            wbuf = sb["wbufA" if c % 2 == 0 else "wbufB"]
            ins = None
            for ki in range(KI):
                ins = t.matmul(ps[(c % 2)][:, :], wbuf[:, ki * 128:(ki + 1) * 128],
                               attnv[:, ki * PW:(ki + 1) * PW],
                               start=(ki == 0), stop=(ki == KI - 1))
            ins.then_inc(s_mm.h, 1)
            s_mm.inc()
            assert s_mm.val == wm(c)
            v.wait_ge(s_mm.h, wm(c))
            V.scalar_tensor_tensor(out=x[:, c * T + p * PW: c * T + (p + 1) * PW],
                                   in0=ps[(c % 2)][:, :], scalar=1.0 - FLA_W,
                                   in1=x[:, c * T + p * PW: c * T + (p + 1) * PW],
                                   op0=ALU.mult, op1=ALU.add)
        wo_done_mm = s_mm.val
        wo_done_v = s_v.val

    if not SKIP_DELTA[0]:
        emit_delta(ctx, role, q, kst, vst, dS, e1a, e2a, dsc)


def emit_delta(ctx, role, q, kst, vst, dS, e1a, e2a, dsc):
    nc = ctx["nc"]; sb = ctx["sb"]; ps = ctx["ps"]; dr = ctx["dr"]
    s_mm, s_v, s_s = ctx["s_mm"], ctx["s_v"], ctx["s_s"]
    s_sl = ctx["s_sl"]
    x, rows = sb["x"], sb["rows"]
    t = nc.tensor; v = nc.vector; sc = nc.scalar; sy = nc.sync
    V = SelfSync(v, s_v)
    S = SelfSync(sc, s_s)
    ident, negident = sb["ident"], sb["negident"]
    msl, msu_neg, mti_neg = sb["msl"], sb["msu_neg"], sb["mti_neg"]
    E = sb["sc3"]
    RD = 2 * T
    SLOT = {n: i * 128 for i, n in enumerate(
        ("L", "LT", "P", "PT", "P2", "PT2", "TT", "Rb", "Ub", "Q1", "Mn", "Ktn"))}
    R_G, R_CA, R_CB, R_BETA, R_IBETA, R_KHAT, R_KBAR, R_NEGB, R_KTIL = \
        [RD + 128 * j for j in range(1, 10)]
    R_RQC, R_QSC = 3 * T + 512, 3 * T + 640

    chunks = list(range(16)) if role == 1 else list(range(8))
    own0 = 8 if role == 1 else 0

    V.memset(dS[:], 0.0)
    for c in chunks:
        own = (c >= own0)
        lc = c - own0
        kt_t = s_sl[4].inc(16)
        vt_t = s_sl[6].inc(16)
        sy.wait_ge(s_mm.h, s_mm.val)
        sy.wait_ge(s_s.h, s_s.val)
        sy.dma_start(out=kst[:].rearrange("d (s m) -> d s m", s=8),
                     in_=dr["ksh"][c // NCH, :, c % NCH, :, :].rearrange("s d m -> d s m"))\
            .then_inc(s_sl[4].h, 16)
        sy.dma_start(out=vst[:], in_=dr["vsh"][c // NCH, c % NCH, :, :])\
            .then_inc(s_sl[6].h, 16)
        t.wait_ge(s_sl[4].h, kt_t)
        t.wait_ge(s_v.h, s_v.val)
        ins = None
        for s8 in range(8):
            ins = t.matmul(ps[6][0:16, 0:128], sb["wgp"][:, s8 * 16:(s8 + 1) * 16],
                           kst[:, s8 * 128:(s8 + 1) * 128], start=(s8 == 0), stop=(s8 == 7))
        ins.then_inc(s_mm.h, 1)
        s_mm.inc()
        zmm = s_mm.val
        sc.wait_ge(s_mm.h, zmm)
        S.activation(out=rows[0:8, R_G:R_G + 128], in_=ps[6][0:8, 0:128],
                     func=AF.Sigmoid, bias=sb["bg_sb"][0:8, :], scale=1.0)
        S.activation(out=rows[0:8, R_G:R_G + 128], in_=rows[0:8, R_G:R_G + 128], func=AF.Ln)
        g_s = s_s.val
        v.wait_ge(s_s.h, g_s)
        V.tensor_scalar(out=rows[0:8, R_CA:R_CA + 128], in0=rows[0:8, R_G:R_G + 128],
                        scalar1=1.0 / 16.0, scalar2=None, op0=ALU.mult)
        src, dst = R_CA, R_CB
        for sh in (1, 2, 4, 8, 16, 32, 64):
            V.tensor_copy(rows[0:8, dst:dst + sh], rows[0:8, src:src + sh])
            V.tensor_tensor(out=rows[0:8, dst + sh:dst + 128],
                            in0=rows[0:8, src + sh:src + 128],
                            in1=rows[0:8, src:src + 128 - sh], op=ALU.add)
            src, dst = dst, src
        cums_v = s_v.val
        sc.wait_ge(s_v.h, cums_v)
        S.activation(out=rows[0:8, R_BETA:R_BETA + 128], in_=rows[0:8, src:src + 128],
                     func=AF.Exp)
        beta_s = s_s.val
        v.wait_ge(s_s.h, beta_s)
        V.reciprocal(out=rows[0:8, R_IBETA:R_IBETA + 128], in_=rows[0:8, R_BETA:R_BETA + 128])
        sc.wait_ge(s_sl[4].h, kt_t)
        S.activation(out=E[:, 0:1024], in_=kst[:], func=AF.Exp)
        E_s = s_s.val
        t.wait_ge(s_s.h, E_s)
        ins = None
        for s8 in range(8):
            ins = t.matmul(ps[7][0:16, 0:128], sb["colsel"][:, s8 * 16:(s8 + 1) * 16],
                           E[:, s8 * 128:(s8 + 1) * 128], start=(s8 == 0), stop=(s8 == 7))
        ins.then_inc(s_mm.h, 1)
        s_mm.inc()
        cse_mm = s_mm.val
        v.wait_ge(s_mm.h, cse_mm)
        V.reciprocal(out=rows[0:8, R_KHAT:R_KHAT + 128], in_=ps[7][0:8, 0:128])
        V.tensor_tensor(out=rows[0:8, R_KBAR:R_KBAR + 128],
                        in0=rows[0:8, R_BETA:R_BETA + 128],
                        in1=rows[0:8, R_KHAT:R_KHAT + 128], op=ALU.mult)
        V.tensor_tensor(out=rows[0:8, R_KHAT:R_KHAT + 128],
                        in0=rows[0:8, R_IBETA:R_IBETA + 128],
                        in1=rows[0:8, R_KHAT:R_KHAT + 128], op=ALU.mult)
        V.tensor_scalar(out=rows[0:8, R_NEGB:R_NEGB + 128],
                        in0=rows[0:8, R_BETA:R_BETA + 128],
                        scalar1=-1.0, scalar2=None, op0=ALU.mult)
        V.tensor_scalar(out=rows[0:8, R_KTIL:R_KTIL + 128],
                        in0=rows[0:8, R_KHAT:R_KHAT + 128],
                        scalar1=rows[0:8, R_NEGB + 127:R_NEGB + 128], scalar2=None,
                        op0=ALU.mult)
        scales_v = s_v.val
        t.wait_ge(s_v.h, scales_v)
        ins = None
        for s8 in range(8):
            bank = ps[4] if s8 < 4 else ps[5]
            ins = t.matmul(bank[:, (s8 % 4) * 128:(s8 % 4 + 1) * 128],
                           sb["sel16f"][0:8, s8 * 128:(s8 + 1) * 128],
                           rows[0:8, R_KBAR:R_KBAR + 128], start=True, stop=True)
        ins.then_inc(s_mm.h, 1)
        s_mm.inc()
        b1mm = s_mm.val
        v.wait_ge(s_mm.h, b1mm)
        V.tensor_tensor(out=e1a[:, 0:512], in0=E[:, 0:512], in1=ps[4][:, :], op=ALU.mult)
        V.tensor_tensor(out=e1a[:, 512:1024], in0=E[:, 512:1024], in1=ps[5][:, :],
                        op=ALU.mult)
        e1_v = s_v.val
        t.wait_ge(s_v.h, e1_v)
        ins = None
        for s8 in range(8):
            bank = ps[4] if s8 < 4 else ps[5]
            ins = t.matmul(bank[:, (s8 % 4) * 128:(s8 % 4 + 1) * 128],
                           sb["sel16f"][0:8, s8 * 128:(s8 + 1) * 128],
                           rows[0:8, R_KHAT:R_KHAT + 128], start=True, stop=True)
        ins.then_inc(s_mm.h, 1)
        s_mm.inc()
        b2mm = s_mm.val
        v.wait_ge(s_mm.h, b2mm)
        V.tensor_tensor(out=e2a[:, 0:512], in0=E[:, 0:512], in1=ps[4][:, :], op=ALU.mult)
        V.tensor_tensor(out=e2a[:, 512:1024], in0=E[:, 512:1024], in1=ps[5][:, :],
                        op=ALU.mult)
        e2_v = s_v.val

        NSL = 13

        def lane_chain(lane, s8):
            E1 = e1a[:, s8 * 128:(s8 + 1) * 128]
            E2 = e2a[:, s8 * 128:(s8 + 1) * 128]
            Esl = E[:, s8 * 128:(s8 + 1) * 128]
            Ssl = dS[:, s8 * 128:(s8 + 1) * 128]
            Vsl = vst[:, s8 * 128:(s8 + 1) * 128]
            SL = lambda n: slice((lane * NSL + {"L": 0, "LT": 1, "P": 2, "PT": 3, "P2": 4,
                                                "PT2": 5, "TT": 6, "Rb": 7, "Ub": 8, "Q1": 9,
                                                "Mn": 10, "Ktn": 11, "BI": 12}[n]) * 128,
                                 (lane * NSL + {"L": 0, "LT": 1, "P": 2, "PT": 3, "P2": 4,
                                                "PT2": 5, "TT": 6, "Rb": 7, "Ub": 8, "Q1": 9,
                                                "Mn": 10, "Ktn": 11, "BI": 12}[n] + 1) * 128)
            psA = ps[4 + lane]
            tub = psA[:, 256:384]
            bcr = psA[:, 384:512]
            csq = psA[0:1, 0:128]
            osp = psA[:, 256:384]
            kcol = psA[:, 256:257]
            ml = {}

            def t_L():
                t.wait_ge(s_v.h, e2_v)
                t.wait_ge(s_s.h, s_s.val)
                t.matmul(psA[:, 0:128], E1, E2, start=True, stop=True).then_inc(s_mm.h, 1)
                s_mm.inc()
                t.matmul(psA[:, 128:256], E2, E1, start=True, stop=True).then_inc(s_mm.h, 1)
                s_mm.inc()
                ml["mm"] = s_mm.val
            yield ("t", t_L)

            def v_L():
                v.wait_ge(s_mm.h, ml["mm"])
                V.tensor_tensor(out=dsc[:, SL("L")], in0=psA[:, 0:128], in1=msl[:], op=ALU.mult)
                V.tensor_tensor(out=dsc[:, SL("TT")], in0=psA[:, 128:256], in1=msu_neg[:],
                                op=ALU.mult)
                V.tensor_tensor(out=dsc[:, SL("TT")], in0=dsc[:, SL("TT")], in1=ident[:],
                                op=ALU.add)
                V.tensor_tensor(out=dsc[:, SL("LT")], in0=psA[:, 128:256], in1=msu_neg[:],
                                op=ALU.mult)
                V.tensor_scalar(out=dsc[:, SL("LT")], in0=dsc[:, SL("LT")], scalar1=-1.0,
                                scalar2=None, op0=ALU.mult)
                ml["v"] = s_v.val
            yield ("v", v_L)

            names = [("L", "LT")]
            for lev in range(1, NLEV):
                Pn, PTn = names[-1]
                dst_p = "P" if Pn in ("L", "P2") else "P2"
                dst_pt = "PT" if PTn in ("LT", "PT2") else "PT2"
                names.append((dst_p, dst_pt))

                def t_sq(Pn=Pn, PTn=PTn, lev=lev):
                    t.wait_ge(s_v.h, ml["v"])
                    t.matmul(psA[:, 0:128], dsc[:, SL(PTn)], dsc[:, SL(Pn)],
                             start=True, stop=True).then_inc(s_mm.h, 1)
                    s_mm.inc()
                    if lev < NLEV - 1:
                        t.matmul(psA[:, 128:256], dsc[:, SL(Pn)], dsc[:, SL(PTn)],
                                 start=True, stop=True).then_inc(s_mm.h, 1)
                        s_mm.inc()
                    ml["mm"] = s_mm.val
                yield ("t", t_sq)

                def v_sq(dst_p=dst_p, dst_pt=dst_pt, lev=lev):
                    v.wait_ge(s_mm.h, ml["mm"])
                    V.tensor_copy(dsc[:, SL(dst_p)], psA[:, 0:128])
                    if lev < NLEV - 1:
                        V.tensor_copy(dsc[:, SL(dst_pt)], psA[:, 128:256])
                    ml["v"] = s_v.val
                yield ("v", v_sq)

                def t_tu(dst_p=dst_p):
                    t.wait_ge(s_v.h, ml["v"])
                    t.matmul(tub, dsc[:, SL(dst_p)], dsc[:, SL("TT")], start=True, stop=False)
                    t.matmul(tub, ident[:], dsc[:, SL("TT")], start=False, stop=True)\
                        .then_inc(s_mm.h, 1)
                    s_mm.inc()
                    ml["mm"] = s_mm.val
                yield ("t", t_tu)

                def v_tu():
                    v.wait_ge(s_mm.h, ml["mm"])
                    V.tensor_copy(dsc[:, SL("TT")], tub)
                    ml["v"] = s_v.val
                yield ("v", v_tu)

            def t_rhs():
                t.wait_ge(s_v.h, ml["v"])
                t.wait_ge(s_sl[6].h, vt_t)
                t.matmul(psA[:, 0:128], E1, Ssl, start=True, stop=False)
                t.matmul(psA[:, 0:128], negident[:], Vsl, start=False, stop=True)\
                    .then_inc(s_mm.h, 1)
                s_mm.inc()
                ml["mm"] = s_mm.val
            yield ("t", t_rhs)

            def v_rhs():
                v.wait_ge(s_mm.h, ml["mm"])
                V.tensor_copy(dsc[:, SL("Rb")], psA[:, 0:128])
                ml["v"] = s_v.val
            yield ("v", v_rhs)

            def t_u():
                t.wait_ge(s_v.h, ml["v"])
                t.matmul(psA[:, 0:128], dsc[:, SL("TT")], dsc[:, SL("Rb")],
                         start=True, stop=True).then_inc(s_mm.h, 1)
                s_mm.inc()
                ml["mm"] = s_mm.val
            yield ("t", t_u)

            def v_u():
                v.wait_ge(s_mm.h, ml["mm"])
                V.tensor_copy(dsc[:, SL("Ub")], psA[:, 0:128])
                ml["v"] = s_v.val
            yield ("v", v_u)

            if own:
                for qh in (2 * s8, 2 * s8 + 1):
                    def s_eq(qh=qh):
                        sc.wait_ge(s_mm.h, s_mm.val)
                        sc.wait_ge(s_v.h, ml["v"])
                        S.activation(out=dsc[:, SL("Q1")],
                                     in_=q[:, qh * T + lc * 128: qh * T + (lc + 1) * 128],
                                     func=AF.Exp)
                        ml["s"] = s_s.val
                    yield ("s", s_eq)

                    def t_q1():
                        t.wait_ge(s_s.h, ml["s"])
                        t.wait_ge(s_v.h, ml["v"])
                        t.matmul(csq, sb["ones"][:, :], dsc[:, SL("Q1")],
                                 start=True, stop=True).then_inc(s_mm.h, 1)
                        s_mm.inc()
                        t.matmul(bcr, sb["sel16f"][0:8, s8 * 128:(s8 + 1) * 128],
                                 rows[0:8, R_BETA:R_BETA + 128], start=True, stop=True)\
                            .then_inc(s_mm.h, 1)
                        s_mm.inc()
                        ml["mm"] = s_mm.val
                    yield ("t", t_q1)

                    def v_q1(lane=lane):
                        v.wait_ge(s_mm.h, ml["mm"])
                        rq = rows[0:1, 3 * T + 512 + lane * 128:3 * T + 640 + lane * 128]
                        qs_ = rows[0:1, 3 * T + 768 + lane * 128:3 * T + 896 + lane * 128]
                        V.reciprocal(out=rq, in_=csq)
                        V.tensor_tensor(out=qs_, in0=bcr[0:1, :], in1=rq, op=ALU.mult)
                        ml["v"] = s_v.val
                    yield ("v", v_q1)

                    def t_qb(lane=lane):
                        t.wait_ge(s_v.h, ml["v"])
                        qs_ = rows[0:1, 3 * T + 768 + lane * 128:3 * T + 896 + lane * 128]
                        t.matmul(bcr, sb["onesrowf"][:, :], qs_, start=True, stop=True)\
                            .then_inc(s_mm.h, 1)
                        s_mm.inc()
                        ml["mm"] = s_mm.val
                    yield ("t", t_qb)

                    def v_qb():
                        v.wait_ge(s_mm.h, ml["mm"])
                        V.tensor_tensor(out=dsc[:, SL("Q1")], in0=dsc[:, SL("Q1")],
                                        in1=bcr, op=ALU.mult)
                        ml["v"] = s_v.val
                    yield ("v", v_qb)

                    def t_m():
                        t.wait_ge(s_v.h, ml["v"])
                        t.matmul(psA[:, 0:128], E2, dsc[:, SL("Q1")], start=True, stop=True)\
                            .then_inc(s_mm.h, 1)
                        s_mm.inc()
                        ml["mm"] = s_mm.val
                    yield ("t", t_m)

                    def v_m():
                        v.wait_ge(s_mm.h, ml["mm"])
                        V.tensor_tensor(out=dsc[:, SL("Mn")], in0=psA[:, 0:128],
                                        in1=mti_neg[:], op=ALU.mult)
                        ml["v"] = s_v.val
                    yield ("v", v_m)

                    def t_o():
                        t.wait_ge(s_v.h, ml["v"])
                        t.matmul(osp, Ssl, dsc[:, SL("Q1")], start=True, stop=False)
                        t.matmul(osp, dsc[:, SL("Ub")], dsc[:, SL("Mn")],
                                 start=False, stop=True).then_inc(s_mm.h, 1)
                        s_mm.inc()
                        ml["mm"] = s_mm.val
                    yield ("t", t_o)

                    def v_o(qh=qh):
                        v.wait_ge(s_mm.h, ml["mm"])
                        V.scalar_tensor_tensor(
                            out=x[:, qh * T + lc * 128: qh * T + (lc + 1) * 128],
                            in0=osp, scalar=FLA_W * SCALE,
                            in1=x[:, qh * T + lc * 128: qh * T + (lc + 1) * 128],
                            op0=ALU.mult, op1=ALU.add)
                        ml["v"] = s_v.val
                    yield ("v", v_o)

            def t_kt1():
                t.wait_ge(s_v.h, ml["v"])
                t.matmul(bcr, sb["sel16f"][0:8, s8 * 128:(s8 + 1) * 128],
                         rows[0:8, R_KTIL:R_KTIL + 128], start=True, stop=True)\
                    .then_inc(s_mm.h, 1)
                s_mm.inc()
                ml["mm"] = s_mm.val
            yield ("t", t_kt1)

            def s_kt(lane=lane):
                sc.wait_ge(s_mm.h, ml["mm"])
                S.copy(out=sb["sc4"][0:1, lane * 128:(lane + 1) * 128], in_=bcr[0:1, :])
                ml["s"] = s_s.val
            yield ("s", s_kt)

            def t_kt2(lane=lane):
                t.wait_ge(s_s.h, ml["s"])
                t.matmul(kcol, sb["sc4"][0:1, lane * 128:(lane + 1) * 128],
                         sb["ones"][0:1, :], start=True, stop=True).then_inc(s_mm.h, 1)
                s_mm.inc()
                t.matmul(psA[:, 128:256], Esl, ident[:], start=True, stop=True)\
                    .then_inc(s_mm.h, 1)
                s_mm.inc()
                ml["mm"] = s_mm.val
            yield ("t", t_kt2)

            def v_kt():
                v.wait_ge(s_mm.h, ml["mm"])
                V.tensor_scalar(out=dsc[:, SL("Ktn")], in0=psA[:, 128:256],
                                scalar1=kcol, scalar2=None, op0=ALU.mult)
                ml["v"] = s_v.val
            yield ("v", v_kt)

            def t_bc():
                t.wait_ge(s_v.h, ml["v"])
                t.matmul(bcr, sb["sel16f"][0:8, s8 * 128:(s8 + 1) * 128],
                         rows[0:8, R_BETA:R_BETA + 128], start=True, stop=True)\
                    .then_inc(s_mm.h, 1)
                s_mm.inc()
                ml["mm"] = s_mm.val
            yield ("t", t_bc)

            def v_bi(lane=lane):
                v.wait_ge(s_mm.h, ml["mm"])
                V.tensor_scalar(out=dsc[:, SL("BI")], in0=ident[:],
                                scalar1=psA[:, 511:512],
                                scalar2=None, op0=ALU.mult)
                ml["v"] = s_v.val
            yield ("v", v_bi)

            def t_s1():
                t.wait_ge(s_v.h, ml["v"])
                t.matmul(osp, dsc[:, SL("BI")], Ssl, start=True, stop=False)
                t.matmul(osp, dsc[:, SL("Ktn")], dsc[:, SL("Ub")], start=False, stop=True)\
                    .then_inc(s_mm.h, 1)
                s_mm.inc()
                ml["mm"] = s_mm.val
            yield ("t", t_s1)

            def s_s1():
                sc.wait_ge(s_mm.h, ml["mm"])
                S.copy(out=Ssl, in_=osp)
            yield ("s", s_s1)

        for sa in range(0, HKV, 2):
            chains = [lane_chain(0, sa), lane_chain(1, sa + 1)]
            alive = [True, True]
            while any(alive):
                for li in range(2):
                    if alive[li]:
                        try:
                            _, fn = next(chains[li])
                            fn()
                        except StopIteration:
                            alive[li] = False


_NC_CACHE = {}


def kernel(**inputs):
    stage = 4
    g = host_prep(inputs)
    x = np.asarray(inputs["L0_attn_hidden_states"], np.float32)
    if stage not in _NC_CACHE:
        _NC_CACHE[stage] = build_nc(stage)
    nc = _NC_CACHE[stage]
    shared_names = ["wq", "wk", "wv", "wo", "wg", "bg16", "gate", "wgate", "wup", "wdown",
                    "shgate", "shup", "shdown", "c_ident", "c_ones", "c_onesrow", "c_fmask",
                    "c_msl", "c_msu_neg", "c_mti_neg", "c_sel16", "c_lcol", "c_colsel", "wgplace"]
    in_maps = []
    for c in range(8):
        seq, half = c // 2, c % 2
        xc = np.ascontiguousarray(
            x[seq, half * T:(half + 1) * T, :].T.reshape(KI, 128, T))
        m = {"x_in": xc, "rolei": np.full(8, half, np.int32)}
        for n in shared_names:
            m[n] = g[n]
        in_maps.append(m)
    res = run_bass_kernel_spmd(nc, in_maps, core_ids=list(range(8)))
    out = np.empty((B, N, DM), np.float32)
    for c in range(8):
        seq, half = c // 2, c % 2
        yc = res.results[c]["y"]
        out[seq, half * T:(half + 1) * T, :] = yc.reshape(DM, T).T
    return out


# version 03 (final)
# speedup vs baseline: 1.1694x; 1.0567x over previous
import math
import numpy as np
from contextlib import ExitStack

import concourse.bass as bass
import concourse.mybir as mybir
from concourse.bass_utils import run_bass_kernel_spmd

F32 = mybir.dt.float32
BF16 = mybir.dt.bfloat16
I32 = mybir.dt.int32
AF = mybir.ActivationFunctionType
ALU = mybir.AluOpType

B, N, DM = 4, 2048, 2048
H, HKV, DH = 16, 8, 128
NE, HE, HS = 8, 1024, 2048
SCALE = 1.0 / math.sqrt(DH)
FLA_W = 0.01
T = 1024
KI = DM // 128
NP, PW = 2, 512
CH = 128
NCH = T // CH
EPS = 1e-5
NHE = HE // 128
NHS = HS // 128
NLEV = 7


class Sem:
    def __init__(self, es, nc, name):
        self.h = es.enter_context(nc.semaphore(name))
        self.val = 0

    def inc(self, n=1):
        self.val += n
        return self.val


class SelfSync:
    _WRAP = ("tensor_tensor", "tensor_scalar", "scalar_tensor_tensor", "tensor_copy",
             "tensor_add", "tensor_sub", "tensor_mul", "tensor_max", "tensor_relu",
             "reduce_max", "reduce_sum", "tensor_reduce", "reciprocal", "memset",
             "activation", "copy", "mul", "add", "square", "sqrt", "select",
             "tensor_single_scalar", "iota", "bn_stats", "bn_aggr", "memzero")

    def __init__(self, eng, sem):
        self._e = eng
        self._s = sem

    def __getattr__(self, name):
        f = getattr(self._e, name)
        if name not in SelfSync._WRAP:
            return f
        def call(*a, **k):
            self._e.wait_ge(self._s.h, self._s.val)
            ins = f(*a, **k)
            ins.then_inc(self._s.h, 1)
            self._s.inc()
            return ins
        return call


def stream_layout(W):
    d_in, d_out = W.shape
    v = W.reshape(d_in // 128, 128, d_out // 128, 128)
    return np.ascontiguousarray(v.transpose(2, 1, 0, 3), dtype=np.float32)


def host_prep(inputs):
    g = {}
    g["wq"] = stream_layout(np.asarray(inputs["L0_attn_Wq"], np.float32))
    g["wk"] = stream_layout(np.asarray(inputs["L0_attn_Wk"], np.float32))
    g["wv"] = stream_layout(np.asarray(inputs["L0_attn_Wv"], np.float32))
    g["wo"] = stream_layout(np.asarray(inputs["L0_attn_Wo"], np.float32))
    g["wg"] = np.ascontiguousarray(np.asarray(inputs["L0_attn_Wg"], np.float32).reshape(128, 1))
    g["bg16"] = np.full((16, 1), float(np.asarray(inputs["L0_attn_bg"]).reshape(-1)[0]), np.float32)
    gw = np.concatenate([np.asarray(inputs["L0_ffn_gate_w"], np.float32),
                         np.asarray(inputs["L0_ffn_se_gate_w"], np.float32)], axis=0).T
    gw = np.concatenate([gw, np.zeros((DM, 7), np.float32)], axis=1)
    g["gate"] = np.ascontiguousarray(gw.reshape(KI, 128, 16))
    wgate = np.asarray(inputs["L0_ffn_w_gate"], np.float32)
    wup = np.asarray(inputs["L0_ffn_w_up"], np.float32)
    wdown = np.asarray(inputs["L0_ffn_w_down"], np.float32)
    g["wgate"] = np.stack([stream_layout(wgate[e].T) for e in range(NE)])
    g["wup"] = np.stack([stream_layout(wup[e].T) for e in range(NE)])
    g["wdown"] = np.stack([stream_layout(wdown[e].T) for e in range(NE)])
    g["shgate"] = stream_layout(np.ascontiguousarray(np.asarray(inputs["L0_ffn_sh_gate"], np.float32).T))
    g["shup"] = stream_layout(np.ascontiguousarray(np.asarray(inputs["L0_ffn_sh_up"], np.float32).T))
    g["shdown"] = stream_layout(np.ascontiguousarray(np.asarray(inputs["L0_ffn_sh_down"], np.float32).T))
    g["c_ident"] = np.eye(128, dtype=np.float32)
    g["c_ones"] = np.ones((128, 1), np.float32)
    g["c_onesrow"] = np.ones((1, 128), np.float32)
    fm = np.zeros((4, 128, PW), np.float32)
    for o in range(4):
        for i in range(128):
            fm[o, i, o * 128 + i:] = 1.0
    g["c_fmask"] = fm
    sl = np.tril(np.ones((128, 128), np.float32), -1)
    g["c_msl"] = sl
    g["c_msu_neg"] = np.ascontiguousarray(-sl.T)
    g["c_mti_neg"] = -np.triu(np.ones((128, 128), np.float32), 0)
    sel = np.zeros((16, 16, 128), np.float32)
    for j in range(16):
        sel[j, j, :] = 1.0
    g["c_sel16"] = sel
    lc = np.zeros((4, 128, 4), np.float32)
    for j in range(4):
        lc[j, :, j] = 1.0
    g["c_lcol"] = lc
    cs = np.zeros((16, 128, 16), np.float32)
    for j in range(16):
        cs[j, :, j] = 1.0
    g["c_colsel"] = cs
    wgp = np.zeros((8, 128, 16), np.float32)
    for j in range(8):
        wgp[j, :, j] = np.asarray(inputs["L0_attn_Wg"], np.float32).reshape(-1)
    g["wgplace"] = wgp
    return g


def build_nc(stage):
    nc = bass.Bass("TRN2", target_bir_lowering=False, debug=False, num_devices=8)
    es = ExitStack()
    dr = {}

    def din(name, shape, dtype=F32):
        dr[name] = nc.dram_tensor(name, shape, dtype, kind="ExternalInput").ap()

    din("x_in", [KI, 128, T])
    din("rolei", [8], I32)
    dr["y"] = nc.dram_tensor("y", [KI, 128, T], F32, kind="ExternalOutput").ap()
    din("wq", [KI, 128, KI, 128]); din("wk", [HKV, 128, KI, 128]); din("wv", [HKV, 128, KI, 128])
    din("wo", [KI, 128, KI, 128]); din("wg", [128, 1]); din("bg16", [16, 1])
    din("gate", [KI, 128, 16])
    din("wgate", [NE, NHE, 128, KI, 128]); din("wup", [NE, NHE, 128, KI, 128])
    din("wdown", [NE, KI, 128, NHE, 128])
    din("shgate", [NHS, 128, KI, 128]); din("shup", [NHS, 128, KI, 128])
    din("shdown", [KI, 128, NHS, 128])
    din("c_ident", [128, 128]); din("c_ones", [128, 1]); din("c_onesrow", [1, 128])
    din("c_fmask", [4, 128, PW]); din("c_msl", [128, 128]); din("c_msu_neg", [128, 128])
    din("c_mti_neg", [128, 128]); din("c_sel16", [16, 16, 128]); din("c_lcol", [4, 128, 4])
    din("c_colsel", [16, 128, 16]); din("wgplace", [8, 128, 16])
    dr["ksh"] = nc.dram_tensor("ksh", [2, HKV, NCH, 128, 128], BF16, kind="Internal",
                               addr_space="Shared").ap()
    dr["vsh"] = nc.dram_tensor("vsh", [2, NCH, 128, HKV * DH], BF16, kind="Internal",
                               addr_space="Shared").ap()
    dr["rsh"] = nc.dram_tensor("rsh", [2, 16], F32, kind="Internal",
                               addr_space="Shared").ap()
    dr["ksl"] = nc.dram_tensor("ksl", [HKV, NCH, 128, 128], BF16, kind="Internal").ap()
    dr["vsl"] = nc.dram_tensor("vsl", [NCH, 128, HKV * DH], BF16, kind="Internal").ap()
    sb = {}

    def sbuf(name, shape, dtype, stack=es, side=None):
        t = stack.enter_context(nc.sbuf_tensor(name, shape, dtype, side=side))
        sb[name] = t
        return t

    ctx = dict(nc=nc, dr=dr, sb=sb, es=es, stage=stage, sbuf=sbuf)

    sbuf("x", [128, KI * T], F32)
    sbuf("wbufA", [128, KI * 128], BF16); sbuf("wbufB", [128, KI * 128], BF16)
    sbuf("wbufC", [128, KI * 128], BF16); sbuf("wbufD", [128, KI * 128], BF16)
    sbuf("tmpf", [128, T], F32); sbuf("tmpf2", [128, T], F32)
    sbuf("sc1", [128, T], BF16); sbuf("sc2", [128, T], BF16)
    sbuf("sc3", [128, T], BF16); sbuf("sc4", [128, T], BF16)
    sbuf("bcC", [128, T], BF16)
    sbuf("rows", [128, 4096], F32)
    sbuf("combf", [16, T], BF16)
    sbuf("scT", [16, T], BF16)
    sbuf("ctm", [128, 128], BF16)
    sbuf("gatew", [128, KI * 16], BF16)
    sbuf("ident", [128, 128], BF16); sbuf("negident", [128, 128], BF16)
    sbuf("ones", [128, 1], BF16); sbuf("onesf", [128, 1], F32)
    sbuf("onesrow", [1, 128], BF16); sbuf("onesrowf", [1, 128], F32)
    sbuf("fmask", [128, 4 * PW], BF16)
    sbuf("msl", [128, 128], BF16); sbuf("msu_neg", [128, 128], BF16); sbuf("mti_neg", [128, 128], BF16)
    sbuf("sel16", [16, 16 * 128], BF16); sbuf("lcol", [128, 4 * 4], BF16)
    sbuf("colsel", [128, 16 * 16], BF16); sbuf("wgp", [128, 8 * 16], BF16)
    sbuf("sel16f", [16, 16 * 128], F32)
    sbuf("rolet", [1, 8], I32)
    sbuf("wg_sb", [128, 1], BF16); sbuf("bg_sb", [16, 1], F32)

    s_w = Sem(es, nc, "s_w"); s_a = Sem(es, nc, "s_a")
    s_mm = Sem(es, nc, "s_mm"); s_v = Sem(es, nc, "s_v"); s_s = Sem(es, nc, "s_s")
    s_sl = [Sem(es, nc, f"s_sl{i}") for i in range(12)]
    s_bar = Sem(es, nc, "s_bar")
    ps = [es.enter_context(nc.psum_tensor(f"psb{i}", [128, 512], F32)) for i in range(8)]
    ctx.update(ps=ps, s_w=s_w, s_a=s_a, s_mm=s_mm, s_v=s_v, s_s=s_s, s_sl=s_sl, s_bar=s_bar)

    with nc.Block() as b:
        @b.gpsimd
        def _(g):
            g.dma_start(out=sb["x"][:].rearrange("p (k t) -> p k t", k=KI),
                        in_=dr["x_in"].rearrange("k p t -> p k t")).then_inc(s_w.h, 16)
            s_w.inc(16)
            for cn, dst in (("c_ident", "ident"), ("c_ones", "ones"), ("c_onesrow", "onesrow"),
                            ("c_msl", "msl"), ("c_msu_neg", "msu_neg"), ("c_mti_neg", "mti_neg"),
                            ("wg", "wg_sb")):
                g.dma_start(out=sb[dst][:], in_=dr[cn][:, :]).then_inc(s_w.h, 16)
                s_w.inc(16)
            g.dma_start(out=sb["fmask"][:].rearrange("p (o w) -> p o w", o=4),
                        in_=dr["c_fmask"].rearrange("o p w -> p o w")).then_inc(s_w.h, 16)
            s_w.inc(16)
            g.dma_start(out=sb["sel16"][:].rearrange("p (j m) -> p j m", j=16),
                        in_=dr["c_sel16"].rearrange("j p m -> p j m")).then_inc(s_w.h, 16)
            s_w.inc(16)
            g.dma_start(out=sb["lcol"][:].rearrange("p (j m) -> p j m", j=4),
                        in_=dr["c_lcol"].rearrange("j p m -> p j m")).then_inc(s_w.h, 16)
            s_w.inc(16)
            g.dma_start(out=sb["colsel"][:].rearrange("p (j m) -> p j m", j=16),
                        in_=dr["c_colsel"].rearrange("j p m -> p j m")).then_inc(s_w.h, 16)
            s_w.inc(16)
            g.dma_start(out=sb["wgp"][:].rearrange("p (j m) -> p j m", j=8),
                        in_=dr["wgplace"].rearrange("j p m -> p j m")).then_inc(s_w.h, 16)
            s_w.inc(16)
            g.dma_start(out=sb["gatew"][:].rearrange("p (k m) -> p k m", k=KI),
                        in_=dr["gate"].rearrange("k p m -> p k m")).then_inc(s_w.h, 16)
            s_w.inc(16)
        @b.sync
        def _(sy):
            sy.dma_start(out=sb["onesf"][:], in_=dr["c_ones"][:, :]).then_inc(s_a.h, 16)
            sy.dma_start(out=sb["onesrowf"][:], in_=dr["c_onesrow"][:, :]).then_inc(s_a.h, 16)
            sy.dma_start(out=sb["rolet"][0:1, :], in_=dr["rolei"][:]).then_inc(s_a.h, 16)
            sy.dma_start(out=sb["bg_sb"][:], in_=dr["bg16"][:, :]).then_inc(s_a.h, 16)
            sy.dma_start(out=sb["sel16f"][:].rearrange("p (j m) -> p j m", j=16),
                         in_=dr["c_sel16"].rearrange("j p m -> p j m")).then_inc(s_a.h, 16)
            s_a.inc(80)
    with nc.Block() as b:
        for eng in ("tensor", "vector", "scalar", "gpsimd", "sync"):
            def mk():
                def f(h):
                    h.wait_ge(s_w.h, s_w.val)
                    h.wait_ge(s_a.h, s_a.val)
                return f
            getattr(b, eng)(mk())
        @b.vector
        def _(v):
            V = SelfSync(v, s_v)
            V.tensor_scalar(out=sb["negident"][:], in0=sb["ident"][:], scalar1=-1.0,
                            scalar2=None, op0=ALU.mult)

    fstack_xn = ExitStack()
    fstack_h = ExitStack()
    sbuf("xn", [128, KI * T], BF16, stack=fstack_xn)
    sbuf("hbuf", [128, NHS * T], BF16, stack=fstack_h)

    emit_ln(ctx)
    emit_router(ctx)
    emit_ffn(ctx)

    if stage >= 2:
        emit_attention(ctx, fstack_xn, fstack_h)
        if stage >= 3:
            fstack2 = ExitStack()
            sb["xn"] = fstack2.enter_context(nc.sbuf_tensor("xn2", [128, KI * T], BF16))
            sb["hbuf"] = fstack2.enter_context(nc.sbuf_tensor("hbuf2", [128, NHS * T], BF16))
            emit_ln(ctx)
            if stage >= 3.7:
                emit_router(ctx)
            if stage >= 3.9:
                emit_ffn(ctx)
            fstack2.close()
    else:
        fstack_h.close()
        fstack_xn.close()

    with nc.Block() as b:
        @b.sync
        def _(sy):
            sy.wait_ge(s_v.h, s_v.val)
            sy.wait_ge(s_s.h, s_s.val)
            sy.wait_ge(s_mm.h, s_mm.val)
            sy.dma_start(out=dr["y"].rearrange("k p t -> p k t"),
                         in_=sb["x"][:].rearrange("p (k t) -> p k t", k=KI)).then_inc(s_a.h, 16)
            s_a.inc(16)
            sy.wait_ge(s_a.h, s_a.val)
    es.close()
    return nc


def emit_ln(ctx):
    nc = ctx["nc"]; sb = ctx["sb"]; ps = ctx["ps"]
    s_mm, s_v, s_s = ctx["s_mm"], ctx["s_v"], ctx["s_s"]
    x, xn = sb["x"], sb["xn"]
    rows, tmpf = sb["rows"], sb["tmpf"]
    cast = [sb["sc1"], sb["sc2"]]
    sq = [sb["sc3"], sb["sc4"]]
    vb0 = s_v.val
    m_entry = s_mm.val
    sval = {}
    mval = lambda ki: m_entry + ki + 1
    with nc.Block() as b:
        @b.scalar
        def _(s):
            S = SelfSync(s, s_s)
            s.wait_ge(s_v.h, vb0)
            s.wait_ge(s_mm.h, m_entry)
            for ki in range(KI):
                if ki >= 2:
                    s.wait_ge(s_mm.h, mval(ki - 2))
                S.activation(out=cast[ki % 2][:], in_=x[:, ki * T:(ki + 1) * T], func=AF.Copy)
                S.activation(out=sq[ki % 2][:], in_=x[:, ki * T:(ki + 1) * T], func=AF.Square)
                sval[ki] = s_s.val
        @b.tensor
        def _(t):
            for ki in range(KI):
                t.wait_ge(s_s.h, sval[ki])
                ins = None
                for p in range(NP):
                    t.matmul(ps[4 + p][0:1, :], sb["ones"][:, :],
                             cast[ki % 2][:, p * PW:(p + 1) * PW],
                             start=(ki == 0), stop=(ki == KI - 1))
                    ins = t.matmul(ps[6 + p][0:1, :], sb["ones"][:, :],
                                   sq[ki % 2][:, p * PW:(p + 1) * PW],
                                   start=(ki == 0), stop=(ki == KI - 1))
                ins.then_inc(s_mm.h, 1)
                s_mm.inc()
                assert s_mm.val == mval(ki)
    mstat = s_mm.val
    vrec = {}
    srec = {}
    with nc.Block() as b:
        @b.vector
        def _(v):
            V = SelfSync(v, s_v)
            v.wait_ge(s_mm.h, mstat)
            for p in range(NP):
                V.tensor_scalar(out=rows[0:1, p * PW:(p + 1) * PW], in0=ps[4 + p][0:1, :],
                                scalar1=1.0 / DM, scalar2=None, op0=ALU.mult)
                V.tensor_scalar(out=rows[0:1, 2 * T + p * PW:2 * T + (p + 1) * PW],
                                in0=ps[6 + p][0:1, :], scalar1=1.0 / DM, scalar2=None,
                                op0=ALU.mult)
            V.tensor_tensor(out=rows[0:1, 3 * T:4 * T], in0=rows[0:1, 0:T], in1=rows[0:1, 0:T],
                            op=ALU.mult)
            V.tensor_tensor(out=rows[0:1, 2 * T:3 * T], in0=rows[0:1, 2 * T:3 * T],
                            in1=rows[0:1, 3 * T:4 * T], op=ALU.subtract)
            V.tensor_scalar(out=rows[0:1, 2 * T:3 * T], in0=rows[0:1, 2 * T:3 * T],
                            scalar1=EPS, scalar2=None, op0=ALU.add)
            vrec[0] = s_v.val
        @b.scalar
        def _(s):
            S = SelfSync(s, s_s)
            s.wait_ge(s_v.h, vrec[0])
            S.activation(out=rows[0:1, 3 * T:4 * T], in_=rows[0:1, 2 * T:3 * T], func=AF.Sqrt)
            srec[0] = s_s.val
        @b.vector
        def _(v):
            V = SelfSync(v, s_v)
            v.wait_ge(s_s.h, srec[0])
            V.reciprocal(out=rows[0:1, T:2 * T], in_=rows[0:1, 3 * T:4 * T])
            vrec[1] = s_v.val
    with nc.Block() as b:
        mrec = {}
        @b.tensor
        def _(t):
            t.wait_ge(s_v.h, vrec[1])
            ins = None
            for p in range(NP):
                t.matmul(ps[4 + p][:, :], sb["onesrowf"][:, :], rows[0:1, p * PW:(p + 1) * PW],
                         start=True, stop=True)
                ins = t.matmul(ps[6 + p][:, :], sb["onesrowf"][:, :],
                               rows[0:1, T + p * PW:T + (p + 1) * PW],
                               start=True, stop=True)
            ins.then_inc(s_mm.h, 1)
            s_mm.inc()
            mrec[0] = s_mm.val
        @b.vector
        def _(v):
            V = SelfSync(v, s_v)
            v.wait_ge(s_mm.h, mrec[0])
            v.wait_ge(s_mm.h, m_entry)
            for ki in range(KI):
                for p in range(NP):
                    V.tensor_tensor(out=tmpf[:, p * PW:(p + 1) * PW],
                                    in0=x[:, ki * T + p * PW: ki * T + (p + 1) * PW],
                                    in1=ps[4 + p][:, :], op=ALU.subtract)
                    V.tensor_tensor(out=xn[:, ki * T + p * PW: ki * T + (p + 1) * PW],
                                    in0=tmpf[:, p * PW:(p + 1) * PW], in1=ps[6 + p][:, :],
                                    op=ALU.mult)
    ctx["xn_ready_v"] = s_v.val


def emit_router(ctx):
    nc = ctx["nc"]; sb = ctx["sb"]; ps = ctx["ps"]
    s_mm, s_v, s_s = ctx["s_mm"], ctx["s_v"], ctx["s_s"]
    xn = sb["xn"]
    scT, ctm, combf = sb["scT"], sb["ctm"], sb["combf"]
    r = {}
    with nc.Block() as b:
        @b.tensor
        def _(t):
            t.wait_ge(s_v.h, ctx["xn_ready_v"])
            ins = None
            for p in range(NP):
                for ki in range(KI):
                    ins = t.matmul(ps[p][0:16, :], sb["gatew"][:, ki * 16:(ki + 1) * 16],
                                   xn[:, ki * T + p * PW: ki * T + (p + 1) * PW],
                                   start=(ki == 0), stop=(ki == KI - 1))
            ins.then_inc(s_mm.h, 1)
            s_mm.inc()
            r["scmm"] = s_mm.val
        @b.scalar
        def _(s):
            S = SelfSync(s, s_s)
            s.wait_ge(s_mm.h, r["scmm"])
            for p in range(NP):
                S.copy(out=scT[0:16, p * PW:(p + 1) * PW], in_=ps[p][0:16, :])
            r["sct"] = s_s.val
        @b.vector
        def _(v):
            V = SelfSync(v, s_v)
            V.memset(ctm[:], 0.0)
            r["ctmz"] = s_v.val
        @b.tensor
        def _(t):
            t.wait_ge(s_s.h, r["sct"])
            ins = None
            for tt in range(8):
                ins = t.matmul(ps[2][:, tt * 16:(tt + 1) * 16],
                               scT[0:16, tt * 128:(tt + 1) * 128],
                               sb["ident"][0:16, 0:16], start=True, stop=True)
            ins.then_inc(s_mm.h, 1)
            s_mm.inc()
            r["tmm"] = s_mm.val
    st = sb["tmpf"]
    sc = sb["tmpf2"]
    with nc.Block() as b:
        @b.vector
        def _(v):
            V = SelfSync(v, s_v)
            v.wait_ge(s_mm.h, r["tmm"])
            V.tensor_copy(st[:, 0:128], ps[2][:, 0:128])
            for tt in range(8):
                V.reduce_max(st[:, 128 + tt:129 + tt], st[:, tt * 16:tt * 16 + 8],
                             axis=mybir.AxisListType.X)
            V.tensor_scalar(out=st[:, 136:144], in0=st[:, 128:136], scalar1=-1.0, scalar2=None,
                            op0=ALU.mult)
            for tt in range(8):
                V.tensor_scalar(out=sc[:, tt * 16:tt * 16 + 8], in0=st[:, tt * 16:tt * 16 + 8],
                                scalar1=st[:, 128 + tt:129 + tt], scalar2=None, op0=ALU.is_equal)
            for tt in range(8):
                V.scalar_tensor_tensor(out=sc[:, tt * 16:tt * 16 + 8],
                                       in0=sc[:, tt * 16:tt * 16 + 8], scalar=-1e30,
                                       in1=st[:, tt * 16:tt * 16 + 8], op0=ALU.mult, op1=ALU.add)
            for tt in range(8):
                V.reduce_max(st[:, 144 + tt:145 + tt], sc[:, tt * 16:tt * 16 + 8],
                             axis=mybir.AxisListType.X)
            r["v1"] = s_v.val
        @b.scalar
        def _(s):
            S = SelfSync(s, s_s)
            s.wait_ge(s_v.h, r["v1"])
            for tt in range(8):
                S.activation(out=sc[:, 128 + tt * 16:128 + tt * 16 + 8],
                             in_=st[:, tt * 16:tt * 16 + 8], func=AF.Exp,
                             bias=st[:, 136 + tt:137 + tt], scale=1.0)
            for tt in range(8):
                S.activation(out=st[:, 152 + tt:153 + tt], in_=st[:, 144 + tt:145 + tt],
                             func=AF.Exp, bias=st[:, 136 + tt:137 + tt], scale=1.0)
            s.wait_ge(s_v.h, r["ctmz"])
            for tt in range(8):
                S.activation(out=ctm[:, tt * 16 + 8:tt * 16 + 9],
                             in_=st[:, tt * 16 + 8:tt * 16 + 9], func=AF.Sigmoid)
            r["s1"] = s_s.val
        @b.vector
        def _(v):
            V = SelfSync(v, s_v)
            v.wait_ge(s_s.h, r["s1"])
            V.tensor_scalar(out=st[:, 160:168], in0=st[:, 152:160], scalar1=1.0, scalar2=None,
                            op0=ALU.add)
            V.reciprocal(out=st[:, 168:176], in_=st[:, 160:168])
            for tt in range(8):
                V.tensor_scalar(out=sc[:, 256 + tt * 16:256 + tt * 16 + 8],
                                in0=st[:, tt * 16:tt * 16 + 8],
                                scalar1=st[:, 144 + tt:145 + tt], scalar2=None, op0=ALU.is_ge)
            for tt in range(8):
                V.scalar_tensor_tensor(out=ctm[:, tt * 16:tt * 16 + 8],
                                       in0=sc[:, 128 + tt * 16:128 + tt * 16 + 8],
                                       scalar=st[:, 168 + tt:169 + tt],
                                       in1=sc[:, 256 + tt * 16:256 + tt * 16 + 8],
                                       op0=ALU.mult, op1=ALU.mult)
            r["v2"] = s_v.val
    with nc.Block() as b:
        @b.tensor
        def _(t):
            t.wait_ge(s_v.h, r["v2"])
            ins = None
            for tt in range(8):
                bank = ps[3] if tt < 4 else ps[2]
                ins = t.matmul(bank[0:16, (tt % 4) * 128:(tt % 4 + 1) * 128],
                               ctm[:, tt * 16:(tt + 1) * 16], sb["ident"][:, :],
                               start=True, stop=True)
            ins.then_inc(s_mm.h, 1)
            s_mm.inc()
            r["cmm"] = s_mm.val
        @b.scalar
        def _(s):
            S = SelfSync(s, s_s)
            s.wait_ge(s_mm.h, r["cmm"])
            S.copy(out=combf[0:16, 0:512], in_=ps[3][0:16, :])
            S.copy(out=combf[0:16, 512:1024], in_=ps[2][0:16, :])
    ctx["combf_ready_s"] = s_s.val


def emit_ffn(ctx):
    nc = ctx["nc"]; sb = ctx["sb"]; ps = ctx["ps"]; dr = ctx["dr"]
    s_w, s_mm, s_v, s_s = ctx["s_w"], ctx["s_mm"], ctx["s_v"], ctx["s_s"]
    xn, hbuf, bcC, x = sb["xn"], sb["hbuf"], sb["bcC"], sb["x"]
    tmpf, tmpf2, combf = sb["tmpf"], sb["tmpf2"], sb["combf"]
    gslot = [sb["wbufA"], sb["wbufB"]]
    uslot = [sb["wbufC"], sb["wbufD"]]

    down_done_mm = s_mm.val
    for e in range(NE + 1):
        shared = (e == NE)
        NH = NHS if shared else NHE
        wgd = dr["shgate"] if shared else dr["wgate"][e]
        wud = dr["shup"] if shared else dr["wup"][e]
        wdd = dr["shdown"] if shared else dr["wdown"][e]
        r = {"ventry": s_v.val}
        with nc.Block() as b:
            @b.tensor
            def _(t):
                t.wait_ge(s_s.h, ctx["combf_ready_s"])
                if e > 0:
                    t.wait_ge(s_v.h, ctx["dnv_last"])
                ins = None
                for p in range(NP):
                    ins = t.matmul(ps[p][:, :],
                                   sb["sel16"][:, min(e, 8) * 128:(min(e, 8) + 1) * 128],
                                   combf[0:16, p * PW:(p + 1) * PW], start=True, stop=True)
                ins.then_inc(s_mm.h, 1)
                s_mm.inc()
                r["bc"] = s_mm.val
            @b.scalar
            def _(s):
                S = SelfSync(s, s_s)
                s.wait_ge(s_mm.h, r["bc"])
                s.wait_ge(s_v.h, r["ventry"])
                for p in range(NP):
                    S.copy(out=bcC[:, p * PW:(p + 1) * PW], in_=ps[p][:, :])
                r["bcc"] = s_s.val

        s_sl = ctx["s_sl"]
        m0 = s_mm.val
        v0 = s_v.val
        s0 = s_s.val
        gmm = lambda si, p: m0 + si * 4 + p + 1
        umm = lambda si, p: m0 + si * 4 + 2 + p + 1
        sil = lambda si, p: s0 + si * 2 + p + 1
        hev = lambda si, p: v0 + (si * 2 + p + 1) * 2
        gtgt = {}
        utgt = {}
        for si in range(NH):
            gtgt[si] = s_sl[si % 2].inc(16)
            utgt[si] = s_sl[2 + si % 2].inc(16)
        with nc.Block() as b:
            @b.tensor
            def _(t):
                for si in range(NH):
                    t.wait_ge(s_sl[si % 2].h, gtgt[si])
                    t.wait_ge(s_sl[2 + si % 2].h, utgt[si])
                    if si >= 2:
                        t.wait_ge(s_v.h, hev(si - 2, NP - 1))
                    for p in range(NP):
                        ins = None
                        for ki in range(KI):
                            ins = t.matmul(ps[(si % 2) * 2 + p][:, :],
                                           gslot[si % 2][:, ki * 128:(ki + 1) * 128],
                                           xn[:, ki * T + p * PW: ki * T + (p + 1) * PW],
                                           start=(ki == 0), stop=(ki == KI - 1))
                        ins.then_inc(s_mm.h, 1)
                        s_mm.inc()
                        assert s_mm.val == gmm(si, p)
                    for p in range(NP):
                        ins = None
                        for ki in range(KI):
                            ins = t.matmul(ps[4 + (si % 2) * 2 + p][:, :],
                                           uslot[si % 2][:, ki * 128:(ki + 1) * 128],
                                           xn[:, ki * T + p * PW: ki * T + (p + 1) * PW],
                                           start=(ki == 0), stop=(ki == KI - 1))
                        ins.then_inc(s_mm.h, 1)
                        s_mm.inc()
                        assert s_mm.val == umm(si, p)
            @b.gpsimd
            def _(g):
                for si in range(NH):
                    if si < 2:
                        g.wait_ge(s_mm.h, down_done_mm)
                    else:
                        g.wait_ge(s_mm.h, umm(si - 2, NP - 1))
                    g.dma_start(out=gslot[si % 2][:],
                                in_=wgd[si].rearrange("p k m -> p (k m)"))\
                        .then_inc(s_sl[si % 2].h, 16)
                    g.dma_start(out=uslot[si % 2][:],
                                in_=wud[si].rearrange("p k m -> p (k m)"))\
                        .then_inc(s_sl[2 + si % 2].h, 16)
            @b.scalar
            def _(s):
                S = SelfSync(s, s_s)
                for si in range(NH):
                    for p in range(NP):
                        s.wait_ge(s_mm.h, gmm(si, p))
                        if si >= 1:
                            s.wait_ge(s_v.h, hev(si - 1, p))
                        S.activation(out=tmpf[:, p * PW:(p + 1) * PW],
                                     in_=ps[(si % 2) * 2 + p][:, :], func=AF.Silu)
                        assert s_s.val == sil(si, p)
            @b.vector
            def _(v):
                V = SelfSync(v, s_v)
                for si in range(NH):
                    for p in range(NP):
                        v.wait_ge(s_s.h, sil(si, p))
                        v.wait_ge(s_mm.h, umm(si, p))
                        if e > 0 and si == 0 and p == 0:
                            v.wait_ge(s_mm.h, down_done_mm)
                        V.tensor_tensor(out=tmpf2[:, p * PW:(p + 1) * PW],
                                        in0=tmpf[:, p * PW:(p + 1) * PW],
                                        in1=ps[4 + (si % 2) * 2 + p][:, :], op=ALU.mult)
                        V.tensor_tensor(out=hbuf[:, si * T + p * PW: si * T + (p + 1) * PW],
                                        in0=tmpf2[:, p * PW:(p + 1) * PW],
                                        in1=bcC[:, p * PW:(p + 1) * PW], op=ALU.mult)
                        assert s_v.val == hev(si, p)
        hev_last = s_v.val
        gu_done = s_mm.val

        m1 = s_mm.val
        v1 = s_v.val
        dmm = lambda c, p: m1 + c * 2 + p + 1
        dvv = lambda c, p: v1 + c * 2 + p + 1
        dtgt = {}
        for c in range(KI):
            dtgt[c] = s_sl[c % 2].inc(16)
        with nc.Block() as b:
            @b.tensor
            def _(t):
                for c in range(KI):
                    t.wait_ge(s_sl[c % 2].h, dtgt[c])
                    if c == 0:
                        t.wait_ge(s_v.h, hev_last)
                    if c >= 2:
                        t.wait_ge(s_v.h, dvv(c - 2, NP - 1))
                    for p in range(NP):
                        ins = None
                        for jj in range(NH):
                            ins = t.matmul(ps[(c % 2) * 2 + p][:, :],
                                           gslot[c % 2][:, jj * 128:(jj + 1) * 128],
                                           hbuf[:, jj * T + p * PW: jj * T + (p + 1) * PW],
                                           start=(jj == 0), stop=(jj == NH - 1))
                        ins.then_inc(s_mm.h, 1)
                        s_mm.inc()
                        assert s_mm.val == dmm(c, p)
            @b.gpsimd
            def _(g):
                for c in range(KI):
                    if c < 2:
                        g.wait_ge(s_mm.h, gu_done)
                    else:
                        g.wait_ge(s_mm.h, dmm(c - 2, NP - 1))
                    g.dma_start(out=gslot[c % 2][:, 0:NH * 128],
                                in_=wdd[c].rearrange("p k m -> p (k m)"))\
                        .then_inc(s_sl[c % 2].h, 16)
            @b.vector
            def _(v):
                V = SelfSync(v, s_v)
                for c in range(KI):
                    for p in range(NP):
                        v.wait_ge(s_mm.h, dmm(c, p))
                        V.scalar_tensor_tensor(
                            out=x[:, c * T + p * PW: c * T + (p + 1) * PW],
                            in0=ps[(c % 2) * 2 + p][:, :], scalar=0.5,
                            in1=x[:, c * T + p * PW: c * T + (p + 1) * PW],
                            op0=ALU.mult, op1=ALU.add)
                        assert s_v.val == dvv(c, p)
        down_done_mm = s_mm.val
        ctx["dnv_last"] = s_v.val






def emit_attention(ctx, fstack_xn, fstack_h):
    nc = ctx["nc"]; sb = ctx["sb"]
    sbuf = ctx["sbuf"]
    s_sl = ctx["s_sl"]

    emit_ln(ctx)

    fstack_h.close()
    astack = ExitStack()
    q = sbuf("qb", [128, H * T], BF16, stack=astack, side="right")

    emit_proj(ctx)

    fstack_xn.close()
    attnv = sbuf("attnv", [128, H * PW], BF16, stack=astack, side="right")
    kst = sbuf("kst", [128, 8 * 128], BF16, stack=astack, side="right")
    vst = sbuf("vst", [128, 8 * 128], BF16, stack=astack, side="right")
    ptb = sbuf("ptb", [128, 4 * PW], BF16, stack=astack, side="right")
    negC = sbuf("negC", [128, 16], F32, stack=astack, side="right")
    dS = sbuf("dS", [128, HKV * DH], BF16, stack=astack, side="right")
    e1a = sbuf("e1a", [128, HKV * 128], BF16, stack=astack, side="right")
    e2a = sbuf("e2a", [128, HKV * 128], BF16, stack=astack, side="right")
    dsc = sbuf("dsc", [128, 24 * 128], BF16, stack=astack, side="right")
    rbc = sbuf("rbc", [128, PW], BF16, stack=astack, side="right")

    emit_barrier_and_bounds(ctx, negC)

    s_mm, s_v, s_s, s_a, s_w = ctx["s_mm"], ctx["s_v"], ctx["s_s"], ctx["s_a"], ctx["s_w"]
    sems_all = [s_mm, s_v, s_s, s_a, s_w] + s_sl
    engs = [nc.tensor, nc.vector, nc.scalar, nc.gpsimd, nc.sync]
    enames = ("tensor", "vector", "scalar", "gpsimd", "sync")
    regs = {}
    with nc.Block() as b:
        for name in enames:
            def mk(nm):
                def f(h):
                    r = h.alloc_register(f"role_{nm}")
                    h.reg_load(r, sb["rolet"][0:1, 0:1])
                    regs[nm] = r
                return f
            getattr(b, name)(mk(name))
    idxs = [eng.compute_val(regs[nm]) for nm, eng in zip(enames, engs)]

    snap = [sm.val for sm in sems_all]
    snap_keys = set(ctx.keys())
    ends = []
    for role in nc.Switch(engines=engs, index=idxs, n=2):
        if role == 1:
            for sm, vv in zip(sems_all, snap):
                sm.val = vv
            for k in list(ctx.keys()):
                if k not in snap_keys:
                    del ctx[k]
        emit_flash_delta_wo(ctx, role, q, attnv, kst, vst, ptb, negC, dS, e1a, e2a, dsc, rbc)
        ends.append([sm.val for sm in sems_all])
    ceil = [max(a, c) for a, c in zip(ends[0], ends[1])]
    own = [nc.tensor, nc.vector, nc.scalar, nc.sync, nc.gpsimd,
           nc.gpsimd, nc.gpsimd, nc.gpsimd, nc.gpsimd,
           nc.sync, nc.sync, nc.sync, nc.sync, nc.sync, nc.sync, nc.sync, nc.sync]
    for role in nc.Switch(engines=engs, index=idxs, n=2):
        for i, sm in enumerate(sems_all):
            d = ceil[i] - ends[role][i]
            if d > 0:
                own[i].wait_ge(sm.h, ends[role][i])
                own[i].sem_inc(sm.h, d)
    for sm, vv in zip(sems_all, ceil):
        sm.val = vv
    astack.close()
    ctx["dnv_last"] = s_v.val


_HALF_N = [0]


def _half_ix(nc, sb, tag):
    rr = {}
    col = _HALF_N[0] % 8
    _HALF_N[0] += 1
    with nc.Block() as b:
        @b.sync
        def _(sy):
            r = sy.alloc_register(f"half_{tag}{_HALF_N[0]}")
            sy.reg_load(r, sb["rolet"][0:1, col:col + 1])
            rr[0] = r
    return nc.sync.compute_val(rr[0])


def _sync_switch(ctx, col, emit_body):
    nc = ctx["nc"]; sb = ctx["sb"]
    rr = {}
    with nc.Block() as b:
        @b.sync
        def _(sy):
            r = sy.alloc_register(f"swr{col}_{_HALF_N[0]}")
            _HALF_N[0] += 1
            sy.reg_load(r, sb["rolet"][0:1, col:col + 1])
            rr[0] = r
    idx = nc.sync.compute_val(rr[0])
    sems = [ctx["s_a"]] + ctx["s_sl"]
    snap = [sm.val for sm in sems]
    ends = []
    for half in nc.Switch(engines=[nc.sync], index=[idx], n=2):
        if half == 1:
            for sm, vv in zip(sems, snap):
                sm.val = vv
        emit_body(half, nc.sync)
        ends.append([sm.val for sm in sems])
    assert ends[0] == ends[1], (ends[0], ends[1])


def emit_proj(ctx):
    nc = ctx["nc"]; sb = ctx["sb"]; ps = ctx["ps"]; dr = ctx["dr"]
    s_a, s_mm, s_v, s_s = ctx["s_a"], ctx["s_mm"], ctx["s_v"], ctx["s_s"]
    s_sl = ctx["s_sl"]
    xn, q, rows = sb["xn"], sb["qb"], sb["rows"]
    wslot = [sb["wbufA"], sb["wbufB"]]
    kstage = [sb["sc1"], sb["sc2"]]
    vstage = sb["bcC"]
    sqs = sb["sc3"]
    sq4 = sb["sc4"]

    with nc.Block() as b:
        @b.vector
        def _(v):
            V = SelfSync(v, s_v)
            V.memset(rows[0:1, T:T + 40], 0.0)

    dt = {}
    for si in range(KI):
        dt[si] = s_sl[si % 2].inc(16)
    m0 = s_mm.val
    pm = lambda si, p: m0 + si * 2 + p + 1
    s0 = s_s.val
    ev = lambda si, p: s0 + si * 2 + p + 1
    with nc.Block() as b:
        @b.gpsimd
        def _(g):
            for si in range(KI):
                if si >= 2:
                    g.wait_ge(s_mm.h, pm(si - 2, NP - 1))
                g.dma_start(out=wslot[si % 2][:],
                            in_=dr["wq"][si].rearrange("p k m -> p (k m)"))\
                    .then_inc(s_sl[si % 2].h, 16)
        @b.tensor
        def _(t):
            t.wait_ge(s_v.h, ctx["xn_ready_v"])
            for si in range(KI):
                t.wait_ge(s_sl[si % 2].h, dt[si])
                if si >= 2:
                    t.wait_ge(s_s.h, ev(si - 2, NP - 1))
                for p in range(NP):
                    ins = None
                    for ki in range(KI):
                        ins = t.matmul(ps[(si % 2) * 2 + p][:, :],
                                       wslot[si % 2][:, ki * 128:(ki + 1) * 128],
                                       xn[:, ki * T + p * PW: ki * T + (p + 1) * PW],
                                       start=(ki == 0), stop=(ki == KI - 1))
                    ins.then_inc(s_mm.h, 1)
                    s_mm.inc()
        @b.scalar
        def _(s):
            S = SelfSync(s, s_s)
            for si in range(KI):
                for p in range(NP):
                    s.wait_ge(s_mm.h, pm(si, p))
                    S.copy(out=q[:, si * T + p * PW: si * T + (p + 1) * PW],
                           in_=ps[(si % 2) * 2 + p][:, :])
                    assert s_s.val == ev(si, p)

    m1 = s_mm.val
    s1 = s_s.val
    v1 = s_v.val
    qm = lambda si, p: m1 + si * 2 + p + 1
    qe = lambda si, p: s1 + si * 4 + p * 2 + 2
    qv = lambda si, p: v1 + si * 4 + p * 2 + 2
    with nc.Block() as b:
        @b.scalar
        def _(s):
            S = SelfSync(s, s_s)
            for si in range(KI):
                for p in range(NP):
                    if (si, p) != (0, 0):
                        s.wait_ge(s_mm.h, qm(si, p - 1) if p > 0 else qm(si - 1, NP - 1))
                    S.activation(out=sqs[:, p * PW:(p + 1) * PW],
                                 in_=q[:, si * T + p * PW: si * T + (p + 1) * PW], func=AF.Exp)
                    S.activation(out=sq4[:, p * PW:(p + 1) * PW],
                                 in_=q[:, si * T + p * PW: si * T + (p + 1) * PW],
                                 func=AF.Square)
                    assert s_s.val == qe(si, p)
        @b.tensor
        def _(t):
            for si in range(KI):
                for p in range(NP):
                    t.wait_ge(s_s.h, qe(si, p))
                    if (si, p) != (0, 0):
                        t.wait_ge(s_v.h, qv(si, p - 1) if p > 0 else qv(si - 1, NP - 1))
                    t.matmul((ps[6] if p == 0 else ps[5])[0:16, 0:PW],
                             sb["colsel"][:, si * 16:(si + 1) * 16],
                             sqs[:, p * PW:(p + 1) * PW], start=True, stop=True)
                    ins = t.matmul(ps[7][0:1, 0:PW], sb["ones"][:, :],
                                   sq4[:, p * PW:(p + 1) * PW], start=True, stop=True)
                    ins.then_inc(s_mm.h, 1)
                    s_mm.inc()
                    assert s_mm.val == qm(si, p)
        @b.vector
        def _(v):
            V = SelfSync(v, s_v)
            for si in range(KI):
                for p in range(NP):
                    v.wait_ge(s_mm.h, qm(si, p))
                    V.reduce_max(rows[0:1, 2 * T:2 * T + 1], ps[7][0:1, 0:PW],
                                 axis=mybir.AxisListType.X)
                    V.tensor_tensor(out=rows[0:1, T + si:T + si + 1],
                                    in0=rows[0:1, T + si:T + si + 1],
                                    in1=rows[0:1, 2 * T:2 * T + 1], op=ALU.max)
                    assert s_v.val == qv(si, p)
            for p in range(NP):
                V.tensor_copy(rows[0:16, p * PW:(p + 1) * PW],
                              (ps[6] if p == 0 else ps[5])[0:16, 0:PW])
    ctx["qcs_done_v"] = s_v.val

    dt = {}
    for si in range(HKV):
        dt[si] = s_sl[si % 2].inc(16)
    m2 = s_mm.val
    s2 = s_s.val
    v2 = s_v.val
    km = lambda si, p: m2 + si * 4 + p + 1
    knm = lambda si, p: m2 + si * 4 + 2 + p + 1
    kev = lambda si, p: s2 + si * 3 + p + 1
    ksq = lambda si: s2 + si * 3 + 3
    knv = lambda si, p: v2 + si * 4 + (p + 1) * 2
    w4base, w5base = s_sl[4].val, s_sl[5].val
    wrt = lambda si, p: (w4base if si % 2 == 0 else w5base) + 32 * (si // 2 + 1)
    with nc.Block() as b:
        @b.gpsimd
        def _(g):
            for si in range(HKV):
                if si >= 2:
                    g.wait_ge(s_mm.h, km(si - 2, NP - 1))
                g.dma_start(out=wslot[si % 2][:],
                            in_=dr["wk"][si].rearrange("p k m -> p (k m)"))\
                    .then_inc(s_sl[si % 2].h, 16)
        @b.tensor
        def _(t):
            for si in range(HKV):
                t.wait_ge(s_sl[si % 2].h, dt[si])
                if si >= 2:
                    t.wait_ge(s_s.h, kev(si - 2, NP - 1))
                for p in range(NP):
                    ins = None
                    for ki in range(KI):
                        ins = t.matmul(ps[(si % 2) * 2 + p][:, :],
                                       wslot[si % 2][:, ki * 128:(ki + 1) * 128],
                                       xn[:, ki * T + p * PW: ki * T + (p + 1) * PW],
                                       start=(ki == 0), stop=(ki == KI - 1))
                    ins.then_inc(s_mm.h, 1)
                    s_mm.inc()
                    assert s_mm.val == km(si, p)
                t.wait_ge(s_s.h, ksq(si))
                for p in range(NP):
                    if si > 0 or p > 0:
                        t.wait_ge(s_v.h, knv(si, p - 1) if p > 0 else knv(si - 1, NP - 1))
                    t.matmul(ps[7][0:1, 0:PW], sb["ones"][:, :],
                             sq4[:, p * PW:(p + 1) * PW],
                             start=True, stop=True).then_inc(s_mm.h, 1)
                    s_mm.inc()
                    assert s_mm.val == knm(si, p)
        @b.scalar
        def _(s):
            S = SelfSync(s, s_s)
            for si in range(HKV):
                for p in range(NP):
                    s.wait_ge(s_mm.h, km(si, p))
                    if si >= 2:
                        s.wait_ge(s_sl[4 + si % 2].h, wrt(si - 2, p))
                    S.copy(out=kstage[si % 2][:, p * PW:(p + 1) * PW],
                           in_=ps[(si % 2) * 2 + p][:, :])
                    assert s_s.val == kev(si, p)
                S.activation(out=sq4[:, 0:T], in_=kstage[si % 2][:], func=AF.Square)
                assert s_s.val == ksq(si)
        @b.vector
        def _(v):
            V = SelfSync(v, s_v)
            for si in range(HKV):
                for p in range(NP):
                    v.wait_ge(s_mm.h, knm(si, p))
                    V.reduce_max(rows[0:1, 2 * T:2 * T + 1], ps[7][0:1, 0:PW],
                                 axis=mybir.AxisListType.X)
                    V.tensor_tensor(out=rows[0:1, T + 16 + si:T + 17 + si],
                                    in0=rows[0:1, T + 16 + si:T + 17 + si],
                                    in1=rows[0:1, 2 * T:2 * T + 1], op=ALU.max)
                    assert s_v.val == knv(si, p)
        @b.sync
        def _(sy):
            for si in range(HKV):
                for p in range(NP):
                    sy.wait_ge(s_s.h, kev(si, p))
                    sy.dma_start(
                        out=dr["ksl"][si, p * 4:(p + 1) * 4, :, :]
                            .rearrange("c d m -> d c m"),
                        in_=kstage[si % 2][:, p * PW:(p + 1) * PW]
                            .rearrange("d (c m) -> d c m", c=4))\
                        .then_inc(s_sl[4 + si % 2].h, 16)
                    s_sl[4 + si % 2].inc(16)
    kmax_done_v = s_v.val

    dt = {}
    for si in range(HKV):
        dt[si] = s_sl[si % 2].inc(16)
    m3 = s_mm.val
    s3 = s_s.val
    vm = lambda si, p: m3 + si * 4 + p * 2 + 1
    vtm = lambda si, p: m3 + si * 4 + p * 2 + 2
    vev = lambda si, p: s3 + si * 4 + p * 2 + 1
    vte = lambda si, p: s3 + si * 4 + p * 2 + 2
    wrt = {}
    for si in range(HKV):
        for p in range(NP):
            wrt[(si, p)] = s_sl[6].inc(64)
    with nc.Block() as b:
        @b.gpsimd
        def _(g):
            for si in range(HKV):
                if si >= 2:
                    g.wait_ge(s_mm.h, vm(si - 2, NP - 1))
                g.dma_start(out=wslot[si % 2][:],
                            in_=dr["wv"][si].rearrange("p k m -> p (k m)"))\
                    .then_inc(s_sl[si % 2].h, 16)
        @b.tensor
        def _(t):
            for si in range(HKV):
                t.wait_ge(s_sl[si % 2].h, dt[si])
                for p in range(NP):
                    if si >= 1:
                        t.wait_ge(s_s.h, vev(si - 1, p))
                    ins = None
                    for ki in range(KI):
                        ins = t.matmul(ps[4 + p][:, :],
                                       wslot[si % 2][:, ki * 128:(ki + 1) * 128],
                                       xn[:, ki * T + p * PW: ki * T + (p + 1) * PW],
                                       start=(ki == 0), stop=(ki == KI - 1))
                    ins.then_inc(s_mm.h, 1)
                    s_mm.inc()
                    assert s_mm.val == vm(si, p)
                    t.wait_ge(s_s.h, vev(si, p))
                    ins = None
                    for c in range(4):
                        ins = t.matmul(ps[6][:, c * 128:(c + 1) * 128],
                                       vstage[:, p * PW + c * 128: p * PW + (c + 1) * 128],
                                       sb["ident"][:, :], start=True, stop=True)
                    ins.then_inc(s_mm.h, 1)
                    s_mm.inc()
                    assert s_mm.val == vtm(si, p)
        @b.scalar
        def _(s):
            S = SelfSync(s, s_s)
            for si in range(HKV):
                for p in range(NP):
                    s.wait_ge(s_mm.h, vm(si, p))
                    S.copy(out=vstage[:, p * PW:(p + 1) * PW], in_=ps[4 + p][:, :])
                    assert s_s.val == vev(si, p)
                    s.wait_ge(s_mm.h, vtm(si, p))
                    if (si, p) != (0, 0):
                        s.wait_ge(s_sl[6].h, wrt[(si, p - 1) if p > 0 else (si - 1, NP - 1)])
                    S.copy(out=sqs[:, 0:PW], in_=ps[6][:, 0:PW])
                    assert s_s.val == vte(si, p)
        @b.sync
        def _(sy):
            for si in range(HKV):
                for p in range(NP):
                    sy.wait_ge(s_s.h, vte(si, p))
                    for cc in range(4):
                        sy.dma_start(
                            out=dr["vsl"][p * 4 + cc, :, si * 128:(si + 1) * 128],
                            in_=sqs[:, cc * 128:(cc + 1) * 128])\
                            .then_inc(s_sl[6].h, 16)
    kslv, vslv = s_sl[4].val, s_sl[6].val

    def _shwrite(half, sy):
        sy.wait_ge(s_sl[4].h, kslv)
        sy.wait_ge(s_sl[5].h, s_sl[5].val)
        sy.wait_ge(s_sl[6].h, vslv)
        sy.wait_ge(s_v.h, kmax_done_v)
        sy.dma_start(out=dr["ksh"][half], in_=dr["ksl"][:]).then_inc(s_a.h, 16)
        sy.dma_start(out=dr["vsh"][half], in_=dr["vsl"][:]).then_inc(s_a.h, 16)
        sy.dma_start(out=dr["rsh"][half, 0:8], in_=rows[0:1, T + 16:T + 24])\
            .then_inc(s_a.h, 16)
        s_a.inc(48)
    _sync_switch(ctx, 3, _shwrite)
    ctx["proj_write_targets"] = (s_sl[4].val, s_sl[5].val, s_sl[6].val, s_a.val)


def emit_barrier_and_bounds(ctx, negC):
    nc = ctx["nc"]; sb = ctx["sb"]; ps = ctx["ps"]; dr = ctx["dr"]
    s_a, s_mm, s_v, s_s, s_w = ctx["s_a"], ctx["s_mm"], ctx["s_v"], ctx["s_s"], ctx["s_w"]
    rows = sb["rows"]
    t4, t5, t6, ta = ctx["proj_write_targets"]
    with nc.Block() as b:
        @b.gpsimd
        def _(g):
            g.wait_ge(ctx["s_sl"][4].h, t4)
            g.wait_ge(ctx["s_sl"][5].h, t5)
            g.wait_ge(ctx["s_sl"][6].h, t6)
            g.wait_ge(s_a.h, ta)
    nc.all_core_barrier()
    s_bar = ctx["s_bar"]
    with nc.Block() as b:
        @b.gpsimd
        def _(g):
            g.sem_inc(s_bar.h, 1)
            s_bar.inc(1)
    bar = s_bar.val
    r = {}
    with nc.Block() as b:
        @b.sync
        def _(sy):
            sy.wait_ge(s_bar.h, bar)
            sy.dma_start(out=rows[0:1, T + 24:T + 32], in_=dr["rsh"][0, 0:8])\
                .then_inc(s_a.h, 16)
            sy.dma_start(out=rows[0:1, T + 32:T + 40], in_=dr["rsh"][1, 0:8])\
                .then_inc(s_a.h, 16)
            s_a.inc(32)
            r["a"] = s_a.val
        @b.vector
        def _(v):
            V = SelfSync(v, s_v)
            v.wait_ge(s_a.h, r["a"])
            V.tensor_tensor(out=rows[0:1, T + 24:T + 32], in0=rows[0:1, T + 24:T + 32],
                            in1=rows[0:1, T + 32:T + 40], op=ALU.max)
            r["v"] = s_v.val
        @b.scalar
        def _(s):
            S = SelfSync(s, s_s)
            s.wait_ge(s_v.h, r["v"])
            S.activation(out=rows[0:1, T + 64:T + 72], in_=rows[0:1, T + 24:T + 32],
                         func=AF.Sqrt)
            S.activation(out=rows[0:1, T + 80:T + 96], in_=rows[0:1, T:T + 16], func=AF.Sqrt)
            r["s"] = s_s.val
        @b.vector
        def _(v):
            V = SelfSync(v, s_v)
            v.wait_ge(s_s.h, r["s"])
            for par in range(2):
                V.tensor_tensor(out=rows[0:1, T + 48 + par:T + 64:2],
                                in0=rows[0:1, T + 80 + par:T + 96:2],
                                in1=rows[0:1, T + 64:T + 72], op=ALU.mult)
            V.tensor_scalar(out=rows[0:1, T + 48:T + 64], in0=rows[0:1, T + 48:T + 64],
                            scalar1=-SCALE, scalar2=None, op0=ALU.mult)
            r["v2"] = s_v.val
    r2 = {}
    with nc.Block() as b:
        @b.tensor
        def _(t):
            t.wait_ge(s_v.h, r["v2"])
            t.matmul(ps[5][:, 0:16], sb["onesrowf"][:, :], rows[0:1, T + 48:T + 64],
                     start=True, stop=True).then_inc(s_mm.h, 1)
            s_mm.inc()
            r2["m"] = s_mm.val
        @b.scalar
        def _(s):
            S = SelfSync(s, s_s)
            s.wait_ge(s_mm.h, r2["m"])
            S.copy(out=negC[:], in_=ps[5][:, 0:16])
    ctx["negc_done_s"] = s_s.val


SKIP_DELTA = [False]
SKIP_FLASH = [False]


def emit_flash_delta_wo(ctx, role, q, attnv, kst, vst, ptb, negC, dS, e1a, e2a, dsc, rbc):
    nc = ctx["nc"]; sb = ctx["sb"]; ps = ctx["ps"]; dr = ctx["dr"]
    s_mm, s_v, s_s = ctx["s_mm"], ctx["s_v"], ctx["s_s"]
    s_sl = ctx["s_sl"]
    x, rows = sb["x"], sb["rows"]
    t = nc.tensor; v = nc.vector; sc = nc.scalar; sy = nc.sync; gp = nc.gpsimd
    V = SelfSync(v, s_v)
    S = SelfSync(sc, s_s)

    t.wait_ge(s_s.h, ctx["negc_done_s"])

    prev_l_evac_s = 0
    prev_av_evac = {}
    wo_done_mm = 0
    wo_done_v = 0
    for p in range(NP if not SKIP_FLASH[0] else 0):
        qlo = role * T + p * PW
        kts = list(range(0, (qlo + PW - 1) // 128 + 1))
        n_kt = len(kts)
        for hp in range(8):
            qh0, qh1 = 2 * hp, 2 * hp + 1
            m_f = s_mm.val
            S_mm = lambda i, qhl: m_f + 6 * i + 3 * qhl + 1
            PV_mm = lambda i, qhl: m_f + 6 * i + 3 * qhl + 2
            L_mm = lambda i, qhl: m_f + 6 * i + 3 * qhl + 3
            KSEM = (4, 5, 8, 9)
            VSEM = (6, 7, 10, 11)
            kd = {}
            vd = {}
            for i, kt in enumerate(kts):
                kd[i] = s_sl[KSEM[i % 4]].inc(16)
                vd[i] = s_sl[VSEM[i % 4]].inc(16)
            for i, kt in enumerate(kts):
                slot = i % 4
                if i >= 4:
                    sy.wait_ge(s_mm.h, S_mm(i - 4, 1))
                    sy.wait_ge(s_mm.h, PV_mm(i - 4, 1))
                else:
                    sy.wait_ge(s_mm.h, m_f)
                sy.dma_start(out=kst[:, slot * 256:slot * 256 + 128],
                             in_=dr["ksh"][kt // NCH, hp, kt % NCH, :, :])\
                    .then_inc(s_sl[KSEM[i % 4]].h, 16)
                sy.dma_start(out=vst[:, slot * 256:slot * 256 + 128],
                             in_=dr["vsh"][kt // NCH, kt % NCH, :, hp * 128:(hp + 1) * 128])\
                    .then_inc(s_sl[VSEM[i % 4]].h, 16)
            exp_done = {}
            for i, kt in enumerate(kts):
                slot = i % 4
                diag = kt * 128 >= qlo
                o = kt - qlo // 128 if diag else 0
                for qhl, qh in ((0, qh0), (1, qh1)):
                    t.wait_ge(s_sl[KSEM[i % 4]].h, kd[i])
                    if i >= 1:
                        t.wait_ge(s_s.h, exp_done.get((i - 1, qhl, "s"), 0))
                        if (i - 1, qhl, "v") in exp_done:
                            t.wait_ge(s_v.h, exp_done[(i - 1, qhl, "v")])
                    if i == 0 and hp == 0 and p == 1:
                        t.wait_ge(s_v.h, wo_done_v)
                    t.matmul(ps[qhl][:, :], kst[:, slot * 256:slot * 256 + 128],
                             q[:, qh * T + p * PW: qh * T + (p + 1) * PW],
                             start=True, stop=True).then_inc(s_mm.h, 1)
                    s_mm.inc()
                    assert s_mm.val == S_mm(i, qhl)
                    ptslot = (i % 2) * 2 + qhl
                    sc.wait_ge(s_mm.h, S_mm(i, qhl))
                    if i >= 2:
                        sc.wait_ge(s_mm.h, L_mm(i - 2, qhl))
                    S.activation(out=ptb[:, ptslot * PW:(ptslot + 1) * PW],
                                 in_=ps[qhl][:, :], func=AF.Exp,
                                 bias=negC[:, qh:qh + 1], scale=SCALE)
                    exp_done[(i, qhl, "s")] = s_s.val
                    if diag:
                        v.wait_ge(s_s.h, s_s.val)
                        V.tensor_tensor(out=ptb[:, ptslot * PW:(ptslot + 1) * PW],
                                        in0=ptb[:, ptslot * PW:(ptslot + 1) * PW],
                                        in1=sb["fmask"][:, o * PW:(o + 1) * PW], op=ALU.mult)
                        exp_done[(i, qhl, "v")] = s_v.val
                        t.wait_ge(s_v.h, s_v.val)
                    else:
                        t.wait_ge(s_s.h, s_s.val)
                    t.wait_ge(s_sl[VSEM[i % 4]].h, vd[i])
                    if i == 0:
                        if (qhl,) in prev_av_evac:
                            t.wait_ge(s_v.h, prev_av_evac[(qhl,)])
                        if qhl == 0 and prev_l_evac_s:
                            t.wait_ge(s_s.h, prev_l_evac_s)
                    t.matmul(ps[2 + qhl][:, :], vst[:, slot * 256:slot * 256 + 128],
                             ptb[:, ptslot * PW:(ptslot + 1) * PW],
                             start=(i == 0), stop=(i == n_kt - 1)).then_inc(s_mm.h, 1)
                    s_mm.inc()
                    assert s_mm.val == PV_mm(i, qhl)
                    t.matmul(ps[6][0:4, 0:PW], sb["lcol"][:, qhl * 4:(qhl + 1) * 4],
                             ptb[:, ptslot * PW:(ptslot + 1) * PW],
                             start=(i == 0 and qhl == 0),
                             stop=(i == n_kt - 1 and qhl == 1)).then_inc(s_mm.h, 1)
                    s_mm.inc()
                    assert s_mm.val == L_mm(i, qhl)
            lmm = s_mm.val
            sc.wait_ge(s_mm.h, lmm)
            S.copy(out=rows[0:2, 2 * T:2 * T + PW], in_=ps[6][0:2, 0:PW])
            prev_l_evac_s = s_s.val
            v.wait_ge(s_s.h, s_s.val)
            V.reciprocal(out=rows[0:2, 2 * T + PW:2 * T + 2 * PW],
                         in_=rows[0:2, 2 * T:2 * T + PW])
            sc.wait_ge(s_v.h, s_v.val)
            S.copy(out=sb["sc4"][0:2, 0:PW], in_=rows[0:2, 2 * T + PW:2 * T + 2 * PW])
            lrec_s = s_s.val
            for qhl, qh in ((0, qh0), (1, qh1)):
                t.wait_ge(s_s.h, lrec_s)
                t.matmul(ps[4 + qhl][:, :], sb["sel16"][0:2, qhl * 128:(qhl + 1) * 128],
                         sb["sc4"][0:2, 0:PW], start=True, stop=True).then_inc(s_mm.h, 1)
                s_mm.inc()
                sc.wait_ge(s_mm.h, s_mm.val)
                sc.wait_ge(s_v.h, s_v.val)
                S.copy(out=rbc[:, :], in_=ps[4 + qhl][:, :])
                v.wait_ge(s_s.h, s_s.val)
                v.wait_ge(s_mm.h, PV_mm(n_kt - 1, qhl))
                if p > 0:
                    v.wait_ge(s_mm.h, wo_done_mm)
                V.tensor_tensor(out=attnv[:, qh * PW:(qh + 1) * PW],
                                in0=ps[2 + qhl][:, :], in1=rbc[:, :], op=ALU.mult)
                prev_av_evac[(qhl,)] = s_v.val
        av_last = s_v.val
        wdt = {}
        for c in range(KI):
            wdt[c] = s_sl[c % 2].inc(16)
        m0 = s_mm.val
        wm = lambda c: m0 + c + 1
        v0w = s_v.val
        for c in range(KI):
            if c < 2:
                gp.wait_ge(s_mm.h, m0)
            else:
                gp.wait_ge(s_mm.h, wm(c - 2))
            gp.dma_start(out=sb["wbufA" if c % 2 == 0 else "wbufB"][:],
                         in_=dr["wo"][c].rearrange("p k m -> p (k m)"))\
                .then_inc(s_sl[c % 2].h, 16)
        for c in range(KI):
            t.wait_ge(s_sl[c % 2].h, wdt[c])
            if c == 0:
                t.wait_ge(s_v.h, av_last)
            if c >= 2:
                t.wait_ge(s_v.h, v0w + c - 1)
            wbuf = sb["wbufA" if c % 2 == 0 else "wbufB"]
            ins = None
            for ki in range(KI):
                ins = t.matmul(ps[(c % 2)][:, :], wbuf[:, ki * 128:(ki + 1) * 128],
                               attnv[:, ki * PW:(ki + 1) * PW],
                               start=(ki == 0), stop=(ki == KI - 1))
            ins.then_inc(s_mm.h, 1)
            s_mm.inc()
            assert s_mm.val == wm(c)
            v.wait_ge(s_mm.h, wm(c))
            V.scalar_tensor_tensor(out=x[:, c * T + p * PW: c * T + (p + 1) * PW],
                                   in0=ps[(c % 2)][:, :], scalar=1.0 - FLA_W,
                                   in1=x[:, c * T + p * PW: c * T + (p + 1) * PW],
                                   op0=ALU.mult, op1=ALU.add)
        wo_done_mm = s_mm.val
        wo_done_v = s_v.val

    if not SKIP_DELTA[0]:
        emit_delta(ctx, role, q, kst, vst, dS, e1a, e2a, dsc)


def emit_delta(ctx, role, q, kst, vst, dS, e1a, e2a, dsc):
    nc = ctx["nc"]; sb = ctx["sb"]; ps = ctx["ps"]; dr = ctx["dr"]
    s_mm, s_v, s_s = ctx["s_mm"], ctx["s_v"], ctx["s_s"]
    s_sl = ctx["s_sl"]
    x, rows = sb["x"], sb["rows"]
    t = nc.tensor; v = nc.vector; sc = nc.scalar; sy = nc.sync
    V = SelfSync(v, s_v)
    S = SelfSync(sc, s_s)
    ident, negident = sb["ident"], sb["negident"]
    msl, msu_neg, mti_neg = sb["msl"], sb["msu_neg"], sb["mti_neg"]
    E = sb["sc3"]
    RD = 2 * T
    SLOT = {n: i * 128 for i, n in enumerate(
        ("L", "LT", "P", "PT", "P2", "PT2", "TT", "Rb", "Ub", "Q1", "Mn", "Ktn"))}
    R_G, R_CA, R_CB, R_BETA, R_IBETA, R_KHAT, R_KBAR, R_NEGB, R_KTIL = \
        [RD + 128 * j for j in range(1, 10)]
    R_RQC, R_QSC = 3 * T + 512, 3 * T + 640

    chunks = list(range(16)) if role == 1 else list(range(8))
    own0 = 8 if role == 1 else 0

    V.memset(dS[:], 0.0)
    for c in chunks:
        own = (c >= own0)
        lc = c - own0
        kt_t = s_sl[4].inc(16)
        vt_t = s_sl[6].inc(16)
        sy.wait_ge(s_mm.h, s_mm.val)
        sy.wait_ge(s_s.h, s_s.val)
        sy.dma_start(out=kst[:].rearrange("d (s m) -> d s m", s=8),
                     in_=dr["ksh"][c // NCH, :, c % NCH, :, :].rearrange("s d m -> d s m"))\
            .then_inc(s_sl[4].h, 16)
        sy.dma_start(out=vst[:], in_=dr["vsh"][c // NCH, c % NCH, :, :])\
            .then_inc(s_sl[6].h, 16)
        t.wait_ge(s_sl[4].h, kt_t)
        t.wait_ge(s_v.h, s_v.val)
        t.wait_ge(s_s.h, s_s.val)
        ins = None
        for s8 in range(8):
            ins = t.matmul(ps[6][0:16, 0:128], sb["wgp"][:, s8 * 16:(s8 + 1) * 16],
                           kst[:, s8 * 128:(s8 + 1) * 128], start=(s8 == 0), stop=(s8 == 7))
        ins.then_inc(s_mm.h, 1)
        s_mm.inc()
        zmm = s_mm.val
        sc.wait_ge(s_mm.h, zmm)
        S.activation(out=rows[0:8, R_G:R_G + 128], in_=ps[6][0:8, 0:128],
                     func=AF.Sigmoid, bias=sb["bg_sb"][0:8, :], scale=1.0)
        S.activation(out=rows[0:8, R_G:R_G + 128], in_=rows[0:8, R_G:R_G + 128], func=AF.Ln)
        g_s = s_s.val
        v.wait_ge(s_s.h, g_s)
        V.tensor_scalar(out=rows[0:8, R_CA:R_CA + 128], in0=rows[0:8, R_G:R_G + 128],
                        scalar1=1.0 / 16.0, scalar2=None, op0=ALU.mult)
        src, dst = R_CA, R_CB
        for sh in (1, 2, 4, 8, 16, 32, 64):
            V.tensor_copy(rows[0:8, dst:dst + sh], rows[0:8, src:src + sh])
            V.tensor_tensor(out=rows[0:8, dst + sh:dst + 128],
                            in0=rows[0:8, src + sh:src + 128],
                            in1=rows[0:8, src:src + 128 - sh], op=ALU.add)
            src, dst = dst, src
        cums_v = s_v.val
        sc.wait_ge(s_v.h, cums_v)
        S.activation(out=rows[0:8, R_BETA:R_BETA + 128], in_=rows[0:8, src:src + 128],
                     func=AF.Exp)
        beta_s = s_s.val
        v.wait_ge(s_s.h, beta_s)
        V.reciprocal(out=rows[0:8, R_IBETA:R_IBETA + 128], in_=rows[0:8, R_BETA:R_BETA + 128])
        sc.wait_ge(s_sl[4].h, kt_t)
        S.activation(out=E[:, 0:1024], in_=kst[:], func=AF.Exp)
        E_s = s_s.val
        t.wait_ge(s_s.h, E_s)
        ins = None
        for s8 in range(8):
            ins = t.matmul(ps[7][0:16, 0:128], sb["colsel"][:, s8 * 16:(s8 + 1) * 16],
                           E[:, s8 * 128:(s8 + 1) * 128], start=(s8 == 0), stop=(s8 == 7))
        ins.then_inc(s_mm.h, 1)
        s_mm.inc()
        cse_mm = s_mm.val
        v.wait_ge(s_mm.h, cse_mm)
        V.reciprocal(out=rows[0:8, R_KHAT:R_KHAT + 128], in_=ps[7][0:8, 0:128])
        V.tensor_tensor(out=rows[0:8, R_KBAR:R_KBAR + 128],
                        in0=rows[0:8, R_BETA:R_BETA + 128],
                        in1=rows[0:8, R_KHAT:R_KHAT + 128], op=ALU.mult)
        V.tensor_tensor(out=rows[0:8, R_KHAT:R_KHAT + 128],
                        in0=rows[0:8, R_IBETA:R_IBETA + 128],
                        in1=rows[0:8, R_KHAT:R_KHAT + 128], op=ALU.mult)
        V.tensor_scalar(out=rows[0:8, R_NEGB:R_NEGB + 128],
                        in0=rows[0:8, R_BETA:R_BETA + 128],
                        scalar1=-1.0, scalar2=None, op0=ALU.mult)
        V.tensor_scalar(out=rows[0:8, R_KTIL:R_KTIL + 128],
                        in0=rows[0:8, R_KHAT:R_KHAT + 128],
                        scalar1=rows[0:8, R_NEGB + 127:R_NEGB + 128], scalar2=None,
                        op0=ALU.mult)
        scales_v = s_v.val
        t.wait_ge(s_v.h, scales_v)
        ins = None
        for s8 in range(8):
            bank = ps[4] if s8 < 4 else ps[5]
            ins = t.matmul(bank[:, (s8 % 4) * 128:(s8 % 4 + 1) * 128],
                           sb["sel16f"][0:8, s8 * 128:(s8 + 1) * 128],
                           rows[0:8, R_KBAR:R_KBAR + 128], start=True, stop=True)
        ins.then_inc(s_mm.h, 1)
        s_mm.inc()
        b1mm = s_mm.val
        v.wait_ge(s_mm.h, b1mm)
        V.tensor_tensor(out=e1a[:, 0:512], in0=E[:, 0:512], in1=ps[4][:, :], op=ALU.mult)
        V.tensor_tensor(out=e1a[:, 512:1024], in0=E[:, 512:1024], in1=ps[5][:, :],
                        op=ALU.mult)
        e1_v = s_v.val
        t.wait_ge(s_v.h, e1_v)
        ins = None
        for s8 in range(8):
            bank = ps[4] if s8 < 4 else ps[5]
            ins = t.matmul(bank[:, (s8 % 4) * 128:(s8 % 4 + 1) * 128],
                           sb["sel16f"][0:8, s8 * 128:(s8 + 1) * 128],
                           rows[0:8, R_KHAT:R_KHAT + 128], start=True, stop=True)
        ins.then_inc(s_mm.h, 1)
        s_mm.inc()
        b2mm = s_mm.val
        v.wait_ge(s_mm.h, b2mm)
        V.tensor_tensor(out=e2a[:, 0:512], in0=E[:, 0:512], in1=ps[4][:, :], op=ALU.mult)
        V.tensor_tensor(out=e2a[:, 512:1024], in0=E[:, 512:1024], in1=ps[5][:, :],
                        op=ALU.mult)
        e2_v = s_v.val

        NSL = 6

        def lane_chain(lane, s8):
            use_sc = (lane % 2 == 1)

            def CP(dst, src):
                if use_sc:
                    sc.wait_ge(s_mm.h, ml["mm"])
                    S.copy(out=dst, in_=src)
                    ml["v"] = s_s.val
                    ml["sem"] = s_s
                else:
                    v.wait_ge(s_mm.h, ml["mm"])
                    V.tensor_copy(dst, src)
                    ml["v"] = s_v.val
                    ml["sem"] = s_v
            E1 = e1a[:, s8 * 128:(s8 + 1) * 128]
            E2 = e2a[:, s8 * 128:(s8 + 1) * 128]
            Esl = E[:, s8 * 128:(s8 + 1) * 128]
            Ssl = dS[:, s8 * 128:(s8 + 1) * 128]
            Vsl = vst[:, s8 * 128:(s8 + 1) * 128]
            SL = lambda n: slice((lane * NSL + {"L": 0, "LT": 1, "P": 2, "PT": 3, "TT": 4,
                                                "Rb": 5, "Ub": 5, "Q1": 0,
                                                "Mn": 1, "Ktn": 2, "BI": 3}[n]) * 128,
                                 (lane * NSL + {"L": 0, "LT": 1, "P": 2, "PT": 3, "TT": 4,
                                                "Rb": 5, "Ub": 5, "Q1": 0,
                                                "Mn": 1, "Ktn": 2, "BI": 3}[n] + 1) * 128)
            psA = ps[4 + lane]
            tub = psA[:, 256:384]
            bcr = psA[:, 384:512]
            csq = psA[0:1, 0:128]
            osp = psA[:, 256:384]
            kcol = psA[:, 256:257]
            ml = {"sem": s_v}

            def t_L():
                t.wait_ge(s_v.h, e2_v)
                t.wait_ge(s_s.h, s_s.val)
                t.matmul(psA[:, 0:128], E1, E2, start=True, stop=True).then_inc(s_mm.h, 1)
                s_mm.inc()
                t.matmul(psA[:, 128:256], E2, E1, start=True, stop=True).then_inc(s_mm.h, 1)
                s_mm.inc()
                ml["mm"] = s_mm.val
            yield ("t", t_L)

            def v_L():
                v.wait_ge(s_mm.h, ml["mm"])
                V.tensor_tensor(out=dsc[:, SL("L")], in0=psA[:, 0:128], in1=msl[:], op=ALU.mult)
                V.tensor_tensor(out=dsc[:, SL("TT")], in0=psA[:, 128:256], in1=msu_neg[:],
                                op=ALU.mult)
                V.tensor_tensor(out=dsc[:, SL("TT")], in0=dsc[:, SL("TT")], in1=ident[:],
                                op=ALU.add)
                V.tensor_tensor(out=dsc[:, SL("LT")], in0=psA[:, 128:256], in1=msu_neg[:],
                                op=ALU.mult)
                V.tensor_scalar(out=dsc[:, SL("LT")], in0=dsc[:, SL("LT")], scalar1=-1.0,
                                scalar2=None, op0=ALU.mult)
                ml["v"] = s_v.val; ml["sem"] = s_v
            yield ("v", v_L)

            names = [("L", "LT")]
            for lev in range(1, NLEV):
                Pn, PTn = names[-1]
                dst_p = "P" if Pn == "L" else "L"
                dst_pt = "PT" if PTn == "LT" else "LT"
                names.append((dst_p, dst_pt))

                def t_sq(Pn=Pn, PTn=PTn, lev=lev):
                    t.wait_ge(ml["sem"].h, ml["v"])
                    t.matmul(psA[:, 0:128], dsc[:, SL(PTn)], dsc[:, SL(Pn)],
                             start=True, stop=True).then_inc(s_mm.h, 1)
                    s_mm.inc()
                    if lev < NLEV - 1:
                        t.matmul(psA[:, 128:256], dsc[:, SL(Pn)], dsc[:, SL(PTn)],
                                 start=True, stop=True).then_inc(s_mm.h, 1)
                        s_mm.inc()
                    ml["mm"] = s_mm.val
                yield ("t", t_sq)

                def v_sq(dst_p=dst_p, dst_pt=dst_pt, lev=lev):
                    CP(dsc[:, SL(dst_p)], psA[:, 0:128])
                    if lev < NLEV - 1:
                        CP(dsc[:, SL(dst_pt)], psA[:, 128:256])
                yield ("v", v_sq)

                def t_tu(dst_p=dst_p):
                    t.wait_ge(ml["sem"].h, ml["v"])
                    t.matmul(tub, dsc[:, SL(dst_p)], dsc[:, SL("TT")], start=True, stop=False)
                    t.matmul(tub, ident[:], dsc[:, SL("TT")], start=False, stop=True)\
                        .then_inc(s_mm.h, 1)
                    s_mm.inc()
                    ml["mm"] = s_mm.val
                yield ("t", t_tu)

                def v_tu():
                    CP(dsc[:, SL("TT")], tub)
                yield ("v", v_tu)

            def t_rhs():
                t.wait_ge(ml["sem"].h, ml["v"])
                t.wait_ge(s_sl[6].h, vt_t)
                t.matmul(psA[:, 0:128], E1, Ssl, start=True, stop=False)
                t.matmul(psA[:, 0:128], negident[:], Vsl, start=False, stop=True)\
                    .then_inc(s_mm.h, 1)
                s_mm.inc()
                ml["mm"] = s_mm.val
            yield ("t", t_rhs)

            def v_rhs():
                CP(dsc[:, SL("Rb")], psA[:, 0:128])
            yield ("v", v_rhs)

            def t_u():
                t.wait_ge(ml["sem"].h, ml["v"])
                t.matmul(psA[:, 0:128], dsc[:, SL("TT")], dsc[:, SL("Rb")],
                         start=True, stop=True).then_inc(s_mm.h, 1)
                s_mm.inc()
                ml["mm"] = s_mm.val
            yield ("t", t_u)

            def v_u():
                CP(dsc[:, SL("Ub")], psA[:, 0:128])
            yield ("v", v_u)

            if own:
                for qh in (2 * s8, 2 * s8 + 1):
                    def s_eq(qh=qh):
                        sc.wait_ge(s_mm.h, s_mm.val)
                        (sc.wait_ge(ml["sem"].h, ml["v"]) if ml["sem"] is not s_s else None)
                        S.activation(out=dsc[:, SL("Q1")],
                                     in_=q[:, qh * T + lc * 128: qh * T + (lc + 1) * 128],
                                     func=AF.Exp)
                        ml["s"] = s_s.val
                    yield ("s", s_eq)

                    def t_q1():
                        t.wait_ge(s_s.h, ml["s"])
                        t.wait_ge(ml["sem"].h, ml["v"])
                        t.matmul(csq, sb["ones"][:, :], dsc[:, SL("Q1")],
                                 start=True, stop=True).then_inc(s_mm.h, 1)
                        s_mm.inc()
                        t.matmul(bcr, sb["sel16f"][0:8, s8 * 128:(s8 + 1) * 128],
                                 rows[0:8, R_BETA:R_BETA + 128], start=True, stop=True)\
                            .then_inc(s_mm.h, 1)
                        s_mm.inc()
                        ml["mm"] = s_mm.val
                    yield ("t", t_q1)

                    def v_q1(lane=lane):
                        v.wait_ge(s_mm.h, ml["mm"])
                        rq = rows[0:1, 3 * T + lane * 128:3 * T + 128 + lane * 128]
                        qs_ = rows[0:1, 3 * T + 512 + lane * 128:3 * T + 640 + lane * 128]
                        V.reciprocal(out=rq, in_=csq)
                        V.tensor_tensor(out=qs_, in0=bcr[0:1, :], in1=rq, op=ALU.mult)
                        ml["v"] = s_v.val; ml["sem"] = s_v
                    yield ("v", v_q1)

                    def t_qb(lane=lane):
                        t.wait_ge(ml["sem"].h, ml["v"])
                        qs_ = rows[0:1, 3 * T + 512 + lane * 128:3 * T + 640 + lane * 128]
                        t.matmul(bcr, sb["onesrowf"][:, :], qs_, start=True, stop=True)\
                            .then_inc(s_mm.h, 1)
                        s_mm.inc()
                        ml["mm"] = s_mm.val
                    yield ("t", t_qb)

                    def v_qb():
                        v.wait_ge(s_mm.h, ml["mm"])
                        V.tensor_tensor(out=dsc[:, SL("Q1")], in0=dsc[:, SL("Q1")],
                                        in1=bcr, op=ALU.mult)
                        ml["v"] = s_v.val; ml["sem"] = s_v
                    yield ("v", v_qb)

                    def t_m():
                        t.wait_ge(ml["sem"].h, ml["v"])
                        t.matmul(psA[:, 0:128], E2, dsc[:, SL("Q1")], start=True, stop=True)\
                            .then_inc(s_mm.h, 1)
                        s_mm.inc()
                        ml["mm"] = s_mm.val
                    yield ("t", t_m)

                    def v_m():
                        v.wait_ge(s_mm.h, ml["mm"])
                        V.tensor_tensor(out=dsc[:, SL("Mn")], in0=psA[:, 0:128],
                                        in1=mti_neg[:], op=ALU.mult)
                        ml["v"] = s_v.val; ml["sem"] = s_v
                    yield ("v", v_m)

                    def t_o():
                        t.wait_ge(ml["sem"].h, ml["v"])
                        t.matmul(osp, Ssl, dsc[:, SL("Q1")], start=True, stop=False)
                        t.matmul(osp, dsc[:, SL("Ub")], dsc[:, SL("Mn")],
                                 start=False, stop=True).then_inc(s_mm.h, 1)
                        s_mm.inc()
                        ml["mm"] = s_mm.val
                    yield ("t", t_o)

                    def v_o(qh=qh):
                        v.wait_ge(s_mm.h, ml["mm"])
                        V.scalar_tensor_tensor(
                            out=x[:, qh * T + lc * 128: qh * T + (lc + 1) * 128],
                            in0=osp, scalar=FLA_W * SCALE,
                            in1=x[:, qh * T + lc * 128: qh * T + (lc + 1) * 128],
                            op0=ALU.mult, op1=ALU.add)
                        ml["v"] = s_v.val; ml["sem"] = s_v
                    yield ("v", v_o)

            def t_kt1():
                t.wait_ge(ml["sem"].h, ml["v"])
                t.matmul(bcr, sb["sel16f"][0:8, s8 * 128:(s8 + 1) * 128],
                         rows[0:8, R_KTIL:R_KTIL + 128], start=True, stop=True)\
                    .then_inc(s_mm.h, 1)
                s_mm.inc()
                ml["mm"] = s_mm.val
            yield ("t", t_kt1)

            def s_kt(lane=lane):
                sc.wait_ge(s_mm.h, ml["mm"])
                S.copy(out=sb["sc4"][0:1, lane * 128:(lane + 1) * 128], in_=bcr[0:1, :])
                ml["s"] = s_s.val
            yield ("s", s_kt)

            def t_kt2(lane=lane):
                t.wait_ge(s_s.h, ml["s"])
                t.matmul(kcol, sb["sc4"][0:1, lane * 128:(lane + 1) * 128],
                         sb["ones"][0:1, :], start=True, stop=True).then_inc(s_mm.h, 1)
                s_mm.inc()
                t.matmul(psA[:, 128:256], Esl, ident[:], start=True, stop=True)\
                    .then_inc(s_mm.h, 1)
                s_mm.inc()
                ml["mm"] = s_mm.val
            yield ("t", t_kt2)

            def v_kt():
                v.wait_ge(s_mm.h, ml["mm"])
                V.tensor_scalar(out=dsc[:, SL("Ktn")], in0=psA[:, 128:256],
                                scalar1=kcol, scalar2=None, op0=ALU.mult)
                ml["v"] = s_v.val; ml["sem"] = s_v
            yield ("v", v_kt)

            def t_bc():
                t.wait_ge(ml["sem"].h, ml["v"])
                t.matmul(bcr, sb["sel16f"][0:8, s8 * 128:(s8 + 1) * 128],
                         rows[0:8, R_BETA:R_BETA + 128], start=True, stop=True)\
                    .then_inc(s_mm.h, 1)
                s_mm.inc()
                ml["mm"] = s_mm.val
            yield ("t", t_bc)

            def v_bi(lane=lane):
                v.wait_ge(s_mm.h, ml["mm"])
                V.tensor_scalar(out=dsc[:, SL("BI")], in0=ident[:],
                                scalar1=psA[:, 511:512],
                                scalar2=None, op0=ALU.mult)
                ml["v"] = s_v.val; ml["sem"] = s_v
            yield ("v", v_bi)

            def t_s1():
                t.wait_ge(ml["sem"].h, ml["v"])
                t.matmul(osp, dsc[:, SL("BI")], Ssl, start=True, stop=False)
                t.matmul(osp, dsc[:, SL("Ktn")], dsc[:, SL("Ub")], start=False, stop=True)\
                    .then_inc(s_mm.h, 1)
                s_mm.inc()
                ml["mm"] = s_mm.val
            yield ("t", t_s1)

            def s_s1():
                sc.wait_ge(s_mm.h, ml["mm"])
                S.copy(out=Ssl, in_=osp)
            yield ("s", s_s1)

        for sa in range(0, HKV, 4):
            chains = [lane_chain(i, sa + i) for i in range(4)]
            alive = [True] * 4
            while any(alive):
                for li in range(4):
                    if alive[li]:
                        try:
                            _, fn = next(chains[li])
                            fn()
                        except StopIteration:
                            alive[li] = False


_NC_CACHE = {}


def kernel(**inputs):
    stage = 4
    g = host_prep(inputs)
    x = np.asarray(inputs["L0_attn_hidden_states"], np.float32)
    if stage not in _NC_CACHE:
        _NC_CACHE[stage] = build_nc(stage)
    nc = _NC_CACHE[stage]
    shared_names = ["wq", "wk", "wv", "wo", "wg", "bg16", "gate", "wgate", "wup", "wdown",
                    "shgate", "shup", "shdown", "c_ident", "c_ones", "c_onesrow", "c_fmask",
                    "c_msl", "c_msu_neg", "c_mti_neg", "c_sel16", "c_lcol", "c_colsel", "wgplace"]
    in_maps = []
    for c in range(8):
        seq, half = c // 2, c % 2
        xc = np.ascontiguousarray(
            x[seq, half * T:(half + 1) * T, :].T.reshape(KI, 128, T))
        m = {"x_in": xc, "rolei": np.full(8, half, np.int32)}
        for n in shared_names:
            m[n] = g[n]
        in_maps.append(m)
    res = run_bass_kernel_spmd(nc, in_maps, core_ids=list(range(8)))
    out = np.empty((B, N, DM), np.float32)
    for c in range(8):
        seq, half = c // 2, c % 2
        yc = res.results[c]["y"]
        out[seq, half * T:(half + 1) * T, :] = yc.reshape(DM, T).T
    return out
